# version 11
# baseline (speedup 1.0000x reference)
import sys
sys.path.insert(0, "/opt/trn_rl_repo")
import numpy as np
import concourse.bass as bass
import concourse.mybir as mybir

F32 = mybir.dt.float32
F32R = mybir.dt.float32r
BF16 = mybir.dt.bfloat16
AF = mybir.ActivationFunctionType
ALU = mybir.AluOpType
AX = mybir.AxisListType

P = 128
D = 2048
NCH = 16
TT = 512
LN_EPS = 1e-5



def tile_lhsT_by_o(WT):
    K, O = WT.shape
    Kc, Oc = K // 128, O // 128
    return np.ascontiguousarray(
        WT.reshape(Kc, 128, Oc, 128).transpose(2, 1, 0, 3).reshape(Oc, 128, Kc * 128))


def tile_lhsT_by_k(WT):
    K = WT.shape[0]
    return np.ascontiguousarray(WT.reshape(K // 128, 128, WT.shape[1]))


def host_consts():
    import ml_dtypes
    i = np.arange(128)
    E = np.zeros((32, 512), np.float32)
    E[np.arange(512) // 16, np.arange(512)] = 1.0
    return {
        "c_ones_row": np.ones((1, 128), np.float32),
        "c_ones_col": np.ones((128, 1), np.float32),
        "c_ident": np.eye(128, dtype=np.float32),
        "c_ident_bf": np.eye(128, dtype=np.float32),
        "c_tri_incl": (i[:, None] <= i[None, :]).astype(np.float32),
        "c_pen_m": (-1e4 * (i[:, None] <= i[None, :])).astype(np.float32),
        "c_pen_mt": (-1e4 * (i[:, None] >= i[None, :])).astype(np.float32),
        "c_pen_at": (-1e4 * (i[:, None] > i[None, :])).astype(np.float32),
        "c_l0exp": E,
        "c_eps": np.tile(np.array([1e-5, 1e-6, 1.0, 0.0], np.float32), (128, 1)),
        "c_onescol_f32": np.ones((128, 1), np.float32),
        "c_onesrow_f32": np.ones((1, 128), np.float32),
        "c_ident_bfr": np.eye(128).astype(ml_dtypes.bfloat16),
        "c_onek_bf": np.ones((128, 128)).astype(ml_dtypes.bfloat16),
        "c_zeros_r": np.zeros((128, 512), np.float32),
    }


CONST_DTYPES = {
    "c_ones_row": F32R, "c_ones_col": F32R, "c_ident": F32,
    "c_ident_bf": F32, "c_tri_incl": F32, "c_pen_m": F32,
    "c_pen_mt": F32, "c_pen_at": F32, "c_l0exp": F32R, "c_eps": F32,
    "c_onescol_f32": F32, "c_onesrow_f32": F32, "c_ident_bfr": BF16,
    "c_onek_bf": BF16, "c_zeros_r": F32R,
}



class B:
    def __init__(self, nc, stack):
        self.nc = nc
        self.stack = stack
        self.sem = {}
        self.cnt = {}
        for name in ("ld", "wld", "wcred", "pe", "dv", "ac", "po", "st"):
            self.sem[name] = stack.enter_context(nc.semaphore(name))
            self.cnt[name] = 0

        self.block = stack.enter_context(nc.Block())
        self.MF32 = stack.enter_context(nc.sbuf_tensor("MF32", [128, 16 * TT], F32))
        self.MR32 = stack.enter_context(nc.sbuf_tensor("MR32", [128, 16 * TT], F32R))
        self.MBF = stack.enter_context(nc.sbuf_tensor("MBF", [128, 5696], BF16))
        self.bank = [stack.enter_context(nc.psum_tensor(f"bank{i}", [128, 512], F32))
                     for i in range(7)]
        self.bank_bf = stack.enter_context(nc.psum_tensor("bankbf", [128, 512], BF16))
        self.bank_guard = [None] * 8
        self.consts = {}
        self.SPR = stack.enter_context(nc.sbuf_tensor("spr", [32, 2 * TT], F32))

    def sb(self, name, shape, dtype=F32):
        return self.stack.enter_context(self.nc.sbuf_tensor(name, shape, dtype))

    def mf(self, s0, w=1):
        return self.MF32[:, s0 * TT:(s0 + w) * TT]

    @property
    def CSC(self):
        return self.MF32[:, 12 * TT:12 * TT + TT + 16]

    def mr(self, s0, w=1):
        return self.MR32[:, s0 * TT:(s0 + w) * TT]

    def new_sem(self, name):
        self.sem[name] = self.stack.enter_context(self.nc.semaphore(name))
        self.cnt[name] = 0
        return name

    def inc(self, name, n=1):
        self.cnt[name] += n
        return self.sem[name], n

    def tok(self, name):
        return (self.sem[name], self.cnt[name])

    def wait(self, eng, token):
        sem, val = token
        if val > 0:
            eng.wait_ge(sem, val)

    def chain(self, eng, name, instr):
        instr.then_inc(*self.inc(name))
        eng.wait_ge(self.sem[name], self.cnt[name])

    def wait_bank(self, eng, bi):
        t = self.bank_guard[bi]
        if t is not None:
            self.wait(eng, t)
            self.bank_guard[bi] = None

    def guard_bank(self, bi, token):
        self.bank_guard[bi] = token


class WStream:

    def __init__(self, B, name="wbuf", width=2048, nbuf=2):
        self.B = B
        self.width = width
        self.nbuf = nbuf
        self.bufs = [B.sb(f"{name}{i}", [128, width], F32R) for i in range(nbuf)]
        self.sems = [B.new_sem(f"{name}_s{i}") for i in range(nbuf)]
        self.gsems = [B.new_sem(f"{name}_g{i}") for i in range(nbuf)]
        self.free_tok = [None] * nbuf
        self.n_loaded = 0
        self.n_used = 0

    def load(self, src_ap):
        bb = self.B
        i = self.n_loaded
        self.n_loaded += 1
        w = src_ap.shape[-1]
        buf = self.bufs[i % self.nbuf]
        sname = self.sems[i % self.nbuf]
        free_tok = self.free_tok[i % self.nbuf]

        def body(q):
            if free_tok is not None:
                bb.wait(q, free_tok)
            q.dma_start(out=buf[:, 0:w], in_=src_ap).then_inc(*bb.inc(sname, 16))

        if i % 2 == 0:
            @bb.block.sync
            def _(sync):
                body(sync)
        else:
            @bb.block.gpsimd
            def _(g):
                body(g)
        return None

    def load_view(self, src_ap, w):
        bb = self.B
        i = self.n_loaded
        self.n_loaded += 1
        buf = self.bufs[i % self.nbuf]
        sname = self.sems[i % self.nbuf]
        gname = self.gsems[i % self.nbuf]
        free_tok = self.free_tok[i % self.nbuf]
        a = src_ap.shape[1]
        dst = buf[:, 0:w].rearrange("p (a q) -> p a q", a=a)

        def body(q):
            if free_tok is not None:
                bb.wait(q, free_tok)
            q.dma_start(out=dst, in_=src_ap).then_inc(*bb.inc(sname, 16))

        if i % 2 == 0:
            @bb.block.sync
            def _(sync):
                body(sync)
        else:
            @bb.block.gpsimd
            def _(g):
                body(g)

    def use(self, eng):
        i = self.n_used
        eng.wait_ge(self.B.sem[self.sems[i % self.nbuf]],
                    16 * (i // self.nbuf + 1))
        return self.bufs[i % self.nbuf]

    def done(self, token):
        self.free_tok[self.n_used % self.nbuf] = token
        self.n_used += 1



class LNBufs:

    def __init__(self, B):
        self.sq = [B.mr(12 + i) for i in range(4)]
        self.rows = B.sb("ln_rows", [1, 2 * TT], F32R)
        self.m_row = B.MF32[0:1, 2 * TT + 384:2 * TT + 896]
        self.q_row = B.MF32[0:1, 2 * TT + 896:2 * TT + 1408]
        self.rsb = B.mf(13).bitcast(F32)
        self.mrb = B.mf(14).bitcast(F32)
        self.t1 = B.mf(15).bitcast(F32)


def emit_ln_core(B, L, x_ch, tag, pre_toks=None):
    nc = B.nc
    ones_c = B.consts["c_ones_col"]
    bsum, bsq = B.bank[4], B.bank[5]
    pe0 = B.cnt["pe"]

    for k in range(NCH):
        slab = L.sq[k % 4]

        @B.block.vector
        def _(v, k=k, slab=slab):
            if pre_toks is not None:
                B.wait(v, pre_toks[k])
            if k >= 4:
                v.wait_ge(B.sem["pe"], pe0 + 2 * (k - 4) + 2)
            v.tensor_tensor(slab[:], x_ch[k].bitcast(F32), x_ch[k].bitcast(F32),
                            ALU.mult).then_inc(*B.inc("dv"))
        sq_tok = B.tok("dv")

        @B.block.tensor
        def _(t, k=k, slab=slab, sq_tok=sq_tok):
            if k == 0:
                B.wait_bank(t, 4)
                B.wait_bank(t, 5)
            B.wait(t, sq_tok)
            t.matmul(bsum[0:1, 0:TT], ones_c[:], x_ch[k],
                     start=(k == 0), stop=(k == NCH - 1),
                     skip_group_check=True).then_inc(*B.inc("pe"))
            t.matmul(bsq[0:1, 0:TT], ones_c[:], slab[:],
                     start=(k == 0), stop=(k == NCH - 1),
                     skip_group_check=True).then_inc(*B.inc("pe"))
    stats_tok = B.tok("pe")

    m_row = L.m_row
    q_row = L.q_row
    rs_row = L.rows[0:1, 0:TT]
    mr_row = L.rows[0:1, TT:2 * TT]

    @B.block.vector
    def _(v):
        B.wait(v, stats_tok)
        i1 = v.tensor_scalar(m_row, bsum[0:1, 0:TT], 1.0 / D, None, op0=ALU.mult)
        i2 = v.tensor_scalar(q_row, bsq[0:1, 0:TT], 1.0 / D, None, op0=ALU.mult)
        B.chain(v, "dv", i2)
        i3 = v.tensor_tensor(rs_row, m_row.bitcast(F32), m_row.bitcast(F32), ALU.mult)
        B.chain(v, "dv", i3)
        v.tensor_tensor(q_row, q_row.bitcast(F32), rs_row.bitcast(F32),
                        ALU.subtract).then_inc(*B.inc("dv"))
    dv_tok = B.tok("dv")

    @B.block.scalar
    def _(s):
        B.wait(s, dv_tok)
        i1 = s.activation(q_row, q_row.bitcast(F32), AF.Ln,
                          bias=B.consts["c_eps"][0:1, 0:1])
        B.chain(s, "ac", i1)
        s.activation(rs_row, q_row.bitcast(F32), AF.Exp, scale=-0.5).then_inc(*B.inc("ac"))
    ac2 = B.tok("ac")

    @B.block.vector
    def _(v):
        B.wait(v, ac2)
        v.tensor_tensor(mr_row, m_row.bitcast(F32), rs_row.bitcast(F32),
                        ALU.mult).then_inc(*B.inc("dv"))
    dv2 = B.tok("dv")

    ones_r = B.consts["c_ones_row"]

    @B.block.tensor
    def _(t):
        B.wait(t, dv2)
        t.matmul(B.bank[4][:, 0:TT], ones_r[:], rs_row, start=True, stop=True)
        t.matmul(B.bank[5][:, 0:TT], ones_r[:], mr_row, start=True, stop=True) \
            .then_inc(*B.inc("pe"))
    pe2 = B.tok("pe")

    @B.block.vector
    def _(v):
        B.wait(v, pe2)
        v.tensor_copy(L.rsb[:], B.bank[4][:, 0:TT])
        v.tensor_copy(L.mrb[:], B.bank[5][:, 0:TT]).then_inc(*B.inc("dv"))
    B.guard_bank(4, B.tok("dv"))
    B.guard_bank(5, B.tok("dv"))
    return B.tok("dv")


def emit_ln_xn(B, L, x_ch, xn_ch, tag="", pre_toks=None):
    ready = emit_ln_core(B, L, x_ch, tag, pre_toks=pre_toks)
    toks = []
    for k in range(NCH):
        @B.block.vector
        def _(v, k=k):
            if k == 0:
                B.wait(v, ready)
            i1 = v.tensor_tensor(xn_ch[k], x_ch[k].bitcast(F32), L.rsb[:],
                                 ALU.mult)
            B.chain(v, "dv", i1)
            v.tensor_tensor(xn_ch[k], xn_ch[k].bitcast(F32), L.mrb[:],
                            ALU.subtract).then_inc(*B.inc("dv"))
        toks.append(B.tok("dv"))
    return toks


def emit_ln_residual(B, L, y_ch, x_ch, tag="", pre_toks=None):
    ready = emit_ln_core(B, L, y_ch, tag, pre_toks=pre_toks)
    toks = []
    for k in range(NCH):
        @B.block.vector
        def _(v, k=k):
            if k == 0:
                B.wait(v, ready)
            i1 = v.tensor_tensor(y_ch[k], y_ch[k].bitcast(F32), L.rsb[:],
                                 ALU.mult)
            B.chain(v, "dv", i1)
            i2 = v.tensor_tensor(y_ch[k], y_ch[k].bitcast(F32), L.mrb[:],
                                 ALU.subtract)
            B.chain(v, "dv", i2)
            v.tensor_tensor(x_ch[k], x_ch[k].bitcast(F32), y_ch[k].bitcast(F32),
                            ALU.add).then_inc(*B.inc("dv"))
        toks.append(B.tok("dv"))
    return toks



def emit_swiglu_mlp(B, ws, xn_ch, xn_toks, y_ch, Wg_dram, Wu_dram, Wd_dram,
                    n_inter_ch, sgbufs, asup, accumulate, tag="",
                    scale_tile=None, scale_tok=None):
    SUP = 16
    n_sup = (n_inter_ch + SUP - 1) // SUP
    OGS = [(0, 6), (6, 12), (12, 16)]

    for sup in range(n_sup):
        i0 = sup * SUP
        i1 = min(i0 + SUP, n_inter_ch)
        ni = i1 - i0
        a_toks = []
        for ii, ic in enumerate(range(i0, i1)):
            ws.load(Wg_dram[ic])
            ws.load(Wu_dram[ic])
            bg = B.bank[0 if ii % 2 == 0 else 2]
            bu = B.bank[1 if ii % 2 == 0 else 3]

            @B.block.tensor
            def _(t, bg=bg, bu=bu, ii=ii):
                slab = ws.use(t)
                B.wait_bank(t, 0 if ii % 2 == 0 else 2)
                for d in range(NCH):
                    if sup == 0 and ii == 0:
                        B.wait(t, xn_toks[d])
                    mm = t.matmul(bg[:, 0:TT], slab[:, d * 128:(d + 1) * 128],
                                  xn_ch[d], start=(d == 0), stop=(d == NCH - 1),
                                  skip_group_check=True)
                mm.then_inc(*B.inc("pe"))
                ws.done(B.tok("pe"))
                slab = ws.use(t)
                B.wait_bank(t, 1 if ii % 2 == 0 else 3)
                for d in range(NCH):
                    mm = t.matmul(bu[:, 0:TT], slab[:, d * 128:(d + 1) * 128],
                                  xn_ch[d], start=(d == 0), stop=(d == NCH - 1),
                                  skip_group_check=True)
                mm.then_inc(*B.inc("pe"))
                ws.done(B.tok("pe"))
            mm_tok = B.tok("pe")
            sg = sgbufs[ii % 2]

            @B.block.scalar
            def _(s, sg=sg, mm_tok=mm_tok, bg=bg):
                B.wait(s, mm_tok)
                s.activation(sg[:], bg[:, 0:TT], AF.Silu).then_inc(*B.inc("ac"))
            ac_tok = B.tok("ac")

            @B.block.vector
            def _(v, sg=sg, ac_tok=ac_tok, bu=bu, ii=ii):
                B.wait(v, ac_tok)
                if scale_tile is not None:
                    B.wait(v, scale_tok)
                    i = v.tensor_tensor(asup[ii], sg[:], bu[:, 0:TT], ALU.mult)
                    B.chain(v, "dv", i)
                    v.tensor_tensor(asup[ii], asup[ii].bitcast(F32), scale_tile,
                                    ALU.mult).then_inc(*B.inc("dv"))
                else:
                    v.tensor_tensor(asup[ii], sg[:], bu[:, 0:TT],
                                    ALU.mult).then_inc(*B.inc("dv"))
            a_toks.append(B.tok("dv"))
            B.guard_bank(0 if ii % 2 == 0 else 2, B.tok("ac"))
            B.guard_bank(1 if ii % 2 == 0 else 3, B.tok("dv"))

        for og_lo, og_hi in OGS:
            ogw = og_hi - og_lo
            ow = ogw * 128
            ii = 0
            while ii < ni:
                npair = 2 if ii + 1 < ni else 1
                ic = i0 + ii
                src = Wd_dram[ic:ic + npair, :, og_lo * 128:og_hi * 128] \
                    .transpose([1, 0, 2])
                ws.load_view(src, npair * ow)

                @B.block.tensor
                def _(t, ii=ii, ogw=ogw, ow=ow, npair=npair):
                    slab = ws.use(t)
                    if ii == 0:
                        for bi in range(ogw):
                            B.wait_bank(t, bi)
                    for a in range(npair):
                        B.wait(t, a_toks[ii + a])
                        for oo in range(ogw):
                            mm = t.matmul(B.bank[oo][:, 0:TT],
                                          slab[:, a * ow + oo * 128:
                                               a * ow + (oo + 1) * 128],
                                          asup[ii + a],
                                          start=(ii + a == 0),
                                          stop=(ii + a == ni - 1),
                                          skip_group_check=True)
                    mm.then_inc(*B.inc("pe"))
                    ws.done(B.tok("pe"))
                ii += npair
            down_tok = B.tok("pe")
            for oo in range(ogw):
                o = og_lo + oo

                @B.block.vector
                def _(v, oo=oo, o=o, down_tok=down_tok):
                    B.wait(v, down_tok)
                    if accumulate or sup > 0:
                        v.tensor_tensor(y_ch[o], y_ch[o].bitcast(F32),
                                        B.bank[oo][:, 0:TT], ALU.add) \
                            .then_inc(*B.inc("dv"))
                    else:
                        v.tensor_copy(y_ch[o], B.bank[oo][:, 0:TT]) \
                            .then_inc(*B.inc("dv"))
                B.guard_bank(oo, B.tok("dv"))
    return B.tok("dv")



class L0Bufs:
    def __init__(self, B, L):
        self.XP = [B.mr(4 + k) for k in range(4)]
        self.SZ = [B.mr(8 + k) for k in range(4)]
        self.XA = [B.mr(k) for k in range(4)]
        self.YF = [B.mr(4 + k) for k in range(4)]
        self.PF = B.mr(12)
        self.PB = [B.mf(k) for k in range(4)]
        self.PC = [B.mf(4 + k) for k in range(4)]
        self.DBS = [B.mf(8 + k) for k in range(2)]
        self.ABS = [B.mf(10 + k) for k in range(2)]
        self.c1 = B.mf(12)
        self.c2 = B.mf(13)
        self.TAIL0 = B.sb("l0_tail", [P, 12], F32)
        self.dsc = B.SPR
        self.DELTA = B.mr(13)
        self.SH = B.sb("l0_sh", [P, 4], F32)
        self.wdt = B.sb("l0s_wdt", [P, 32], F32R)
        self.cw = B.sb("l0s_cw", [P, 4 * 4], F32)
        self.cb = B.sb("l0s_cb", [P, 4], F32)
        self.acol = B.sb("l0s_acol", [P, 4], F32)
        self.dcol = B.sb("l0s_dcol", [P, 4], F32)
        self.bdt = B.sb("l0s_bdt", [32, 1], F32)


def emit_l0attn_consts(B, L0, d):
    @B.block.gpsimd
    def _(g):
        for src, dst in [(d["l0_wdt"], L0.wdt[:]), (d["l0_cw"], L0.cw[:]),
                         (d["l0_cb"], L0.cb[:]), (d["l0_alog"], L0.dcol[:]),
                         (d["l0_dcol"], None), (d["l0_bdt"], L0.bdt[:])]:
            if dst is None:
                continue
            g.dma_start(out=dst, in_=src).then_inc(*B.inc("ld", 16))
        g.dma_start(out=L0.c1[:, 0:4], in_=d["l0_dcol"]).then_inc(*B.inc("ld", 16))
    ld_tok = B.tok("ld")

    @B.block.scalar
    def _(s):
        B.wait(s, ld_tok)
        s.activation(L0.acol[:], L0.dcol[:], AF.Exp).then_inc(*B.inc("ac"))
    ac_tok = B.tok("ac")

    @B.block.vector
    def _(v):
        B.wait(v, ac_tok)
        i = v.tensor_scalar(L0.acol[:], L0.acol[:], -1.0, None, op0=ALU.mult)
        B.chain(v, "dv", i)
        i = v.tensor_copy(L0.dcol[:], L0.c1[:, 0:4])
        B.chain(v, "dv", i)
        i = v.memset(L0.SH[:], 0.0)
        B.chain(v, "dv", i)
        v.memset(L0.TAIL0[:], 0.0).then_inc(*B.inc("dv"))
    return B.tok("ld"), B.tok("dv")


def emit_l0attn(B, L0, ws, xn_ch, xn_toks, y_ch, W, tile, tag=""):
    tail_tok = B.tok("dv")

    for o in range(4):
        ws.load(W["WxT"][o])

        @B.block.tensor
        def _(t, o=o):
            slab = ws.use(t)
            bk = o % 2
            B.wait_bank(t, bk)
            if o == 0:
                B.wait(t, tail_tok)
            for d in range(NCH):
                if o == 0:
                    B.wait(t, xn_toks[d])
                mm = t.matmul(B.bank[bk][:, 0:TT], slab[:, d * 128:(d + 1) * 128],
                              xn_ch[d], start=(d == 0), stop=(d == NCH - 1),
                              skip_group_check=True)
            mm.then_inc(*B.inc("pe"))
            ws.done(B.tok("pe"))
        mm_tok = B.tok("pe")

        @B.block.scalar
        def _(s, o=o, mm_tok=mm_tok):
            B.wait(s, mm_tok)
            s.copy(L0.XP[o], B.bank[o % 2][:, 0:TT]).then_inc(*B.inc("ac"))
        B.guard_bank(o % 2, B.tok("ac"))
    for o in range(4):
        ws.load(W["WzT"][o])

        @B.block.tensor
        def _(t, o=o):
            slab = ws.use(t)
            bk = o % 2
            B.wait_bank(t, bk)
            for d in range(NCH):
                mm = t.matmul(B.bank[bk][:, 0:TT], slab[:, d * 128:(d + 1) * 128],
                              xn_ch[d], start=(d == 0), stop=(d == NCH - 1),
                              skip_group_check=True)
            mm.then_inc(*B.inc("pe"))
            ws.done(B.tok("pe"))
        mm_tok = B.tok("pe")

        @B.block.scalar
        def _(s, o=o, mm_tok=mm_tok):
            B.wait(s, mm_tok)
            s.activation(L0.SZ[o][:], B.bank[o % 2][:, 0:TT],
                         AF.Silu).then_inc(*B.inc("ac"))
        B.guard_bank(o % 2, B.tok("ac"))
    xproj_tok = B.tok("ac")

    CSC = B.CSC
    for k in range(4):
        @B.block.vector
        def _(v, k=k):
            if k == 0:
                B.wait(v, xproj_tok)
            B.wait(v, B.tok("dv"))
            B.wait(v, B.tok("ac"))
            pre = L0.XP[k].bitcast(F32)
            cw = L0.cw
            i = v.tensor_copy(CSC[:, 0:3], L0.TAIL0[:, 3 * k:3 * k + 3])
            B.chain(v, "dv", i)
            i = v.tensor_copy(CSC[:, 3:6], pre[:, 0:3])
            B.chain(v, "dv", i)
            i = v.tensor_copy(L0.TAIL0[:, 3 * k:3 * k + 3], pre[:, TT - 3:TT])
            B.chain(v, "dv", i)
            i = v.tensor_scalar(CSC[:, 8:11], CSC[:, 0:3],
                                cw[:, 4 * k:4 * k + 1], None, op0=ALU.mult)
            B.chain(v, "dv", i)
            for kk in range(1, 4):
                i = v.scalar_tensor_tensor(CSC[:, 8:11], CSC[:, kk:kk + 3],
                                           cw[:, 4 * k + kk:4 * k + kk + 1],
                                           CSC[:, 8:11], op0=ALU.mult, op1=ALU.add)
                B.chain(v, "dv", i)
            i = v.tensor_scalar(CSC[:, 16:16 + TT - 3], pre[:, 0:TT - 3],
                                cw[:, 4 * k:4 * k + 1], None, op0=ALU.mult)
            B.chain(v, "dv", i)
            for kk in range(1, 3):
                i = v.scalar_tensor_tensor(CSC[:, 16:16 + TT - 3],
                                           pre[:, kk:TT - 3 + kk],
                                           cw[:, 4 * k + kk:4 * k + kk + 1],
                                           CSC[:, 16:16 + TT - 3],
                                           op0=ALU.mult, op1=ALU.add)
                B.chain(v, "dv", i)
            v.scalar_tensor_tensor(CSC[:, 16:16 + TT - 3], pre[:, 3:TT],
                                   cw[:, 4 * k + 3:4 * k + 4],
                                   CSC[:, 16:16 + TT - 3],
                                   op0=ALU.mult, op1=ALU.add).then_inc(*B.inc("dv"))
        dv_tok = B.tok("dv")

        @B.block.scalar
        def _(s, k=k, dv_tok=dv_tok):
            B.wait(s, dv_tok)
            i = s.activation(L0.XA[k][:, 0:3], CSC[:, 8:11], AF.Silu,
                             bias=L0.cb[:, k:k + 1])
            B.chain(s, "ac", i)
            s.activation(L0.XA[k][:, 3:TT], CSC[:, 16:16 + TT - 3], AF.Silu,
                         bias=L0.cb[:, k:k + 1]).then_inc(*B.inc("ac"))
    xa_tok = B.tok("ac")
    xa_toks = B.tok("ac")

    for o in range(9):
        ws.load(W["WpT"][o])

        @B.block.tensor
        def _(t, o=o):
            slab = ws.use(t)
            bk = o % 2
            B.wait_bank(t, bk)
            if o == 0:
                B.wait(t, xa_tok)
            for d in range(4):
                mm = t.matmul(B.bank[bk][:, 0:TT], slab[:, d * 128:(d + 1) * 128],
                              L0.XA[d][:], start=(d == 0), stop=(d == 3),
                              skip_group_check=True)
            mm.then_inc(*B.inc("pe"))
            ws.done(B.tok("pe"))
        mm_tok = B.tok("pe")

        @B.block.scalar
        def _(s, o=o, mm_tok=mm_tok):
            B.wait(s, mm_tok)
            if o == 0:
                s.copy(L0.PF[:], B.bank[o % 2][:, 0:TT]).then_inc(*B.inc("ac"))
            elif o <= 4:
                s.copy(L0.PB[o - 1][:], B.bank[o % 2][:, 0:TT]).then_inc(*B.inc("ac"))
            else:
                s.copy(L0.PC[o - 5][:], B.bank[o % 2][:, 0:TT]).then_inc(*B.inc("ac"))
        B.guard_bank(o % 2, B.tok("ac"))
    params_tok = B.tok("ac")

    @B.block.tensor
    def _(t):
        B.wait(t, params_tok)
        B.wait_bank(t, 2)
        t.matmul(B.bank[2][0:32, 0:TT], L0.wdt[:], L0.PF[:],
                 start=True, stop=True).then_inc(*B.inc("pe"))
    dmm_tok = B.tok("pe")
    ds = L0.dsc
    v_r = ds[0:32, 0:TT]
    a_r = ds[0:32, TT:2 * TT]
    dl = L0.DELTA[0:32, :]

    @B.block.vector
    def _(v):
        B.wait(v, dmm_tok)
        v.tensor_scalar(v_r, B.bank[2][0:32, 0:TT], L0.bdt[0:32, 0:1], None,
                        op0=ALU.add).then_inc(*B.inc("dv"))
    B.guard_bank(2, B.tok("dv"))
    dv_tok = B.tok("dv")

    @B.block.scalar
    def _(s):
        B.wait(s, dv_tok)
        i = s.activation(a_r, v_r, AF.Abs)
        B.chain(s, "ac", i)
        i = s.activation(a_r, a_r, AF.Exp, scale=-1.0)
        B.chain(s, "ac", i)
        i = s.activation(a_r, a_r, AF.Ln, bias=B.consts["c_eps"][0:32, 2:3])
        B.chain(s, "ac", i)
        s.activation(dl, v_r, AF.Relu).then_inc(*B.inc("ac"))
    ac_tok = B.tok("ac")

    @B.block.vector
    def _(v):
        B.wait(v, ac_tok)
        v.tensor_tensor(dl, dl.bitcast(F32), a_r, ALU.add).then_inc(*B.inc("dv"))
    delta_tok = B.tok("dv")

    l0exp = B.mr(14)[0:32, :]

    @B.block.gpsimd
    def _(g):
        B.wait(g, B.tok("pe"))
        g.dma_start(out=l0exp, in_=W["c_l0exp_d"]).then_inc(*B.inc("ld", 16))
    lexp_tok = B.tok("ld")
    ab_free = [None, None]
    for k in range(4):
        bk = 2 + (k % 2)

        @B.block.tensor
        def _(t, k=k, bk=bk):
            B.wait_bank(t, bk)
            B.wait(t, delta_tok)
            B.wait(t, lexp_tok)
            t.matmul(B.bank[bk][:, 0:TT],
                     l0exp[:, k * 128:(k + 1) * 128],
                     L0.DELTA[0:32, :],
                     start=True, stop=True).then_inc(*B.inc("pe"))
        db_tok = B.tok("pe")

        @B.block.scalar
        def _(s, k=k, bk=bk, db_tok=db_tok, ft=ab_free[k % 2]):
            B.wait(s, db_tok)
            if ft is not None:
                B.wait(s, ft)
            s.activation(L0.ABS[k % 2][:], B.bank[bk][:, 0:TT], AF.Exp,
                         scale=L0.acol[:, k:k + 1]).then_inc(*B.inc("ac"))
        ab_tok = B.tok("ac")

        @B.block.vector
        def _(v, k=k, bk=bk, ab_tok=ab_tok):
            B.wait(v, ab_tok)
            B.wait(v, B.tok("dv"))
            i = v.tensor_copy(L0.c1[:], B.bank[bk][:, 0:TT])
            B.chain(v, "dv", i)
            i = v.tensor_tensor(L0.c2[:], L0.c1[:], L0.PB[k][:], ALU.mult)
            B.chain(v, "dv", i)
            i = v.tensor_tensor(L0.DBS[k % 2][:], L0.c2[:],
                                L0.XA[k][:].bitcast(F32), ALU.mult)
            B.chain(v, "dv", i)
            i = v.tensor_tensor_scan(L0.PB[k][:], L0.ABS[k % 2][:], L0.DBS[k % 2][:],
                                     L0.SH[:, k:k + 1], op0=ALU.mult, op1=ALU.add)
            B.chain(v, "dv", i)
            i = v.tensor_copy(L0.SH[:, k:k + 1], L0.PB[k][:, TT - 1:TT])
            B.chain(v, "dv", i)
            i = v.tensor_tensor(L0.c2[:], L0.PB[k][:], L0.PC[k][:], ALU.mult)
            B.chain(v, "dv", i)
            i = v.scalar_tensor_tensor(L0.c1[:], L0.XA[k][:].bitcast(F32),
                                       L0.dcol[:, k:k + 1], L0.c2[:],
                                       op0=ALU.mult, op1=ALU.add)
            B.chain(v, "dv", i)
            v.tensor_tensor(L0.YF[k][:], L0.c1[:], L0.SZ[k][:].bitcast(F32),
                            ALU.mult).then_inc(*B.inc("dv"))
        B.guard_bank(bk, B.tok("dv"))
        ab_free[k % 2] = B.tok("dv")
    yf_tok = B.tok("dv")

    toks = []
    for o in range(NCH):
        ws.load(W["WoT"][o])

        @B.block.tensor
        def _(t, o=o):
            slab = ws.use(t)
            bk = o % 2
            B.wait_bank(t, bk)
            if o == 0:
                B.wait(t, yf_tok)
            for d in range(4):
                mm = t.matmul(B.bank[bk][:, 0:TT], slab[:, d * 128:(d + 1) * 128],
                              L0.YF[d][:], start=(d == 0), stop=(d == 3),
                              skip_group_check=True)
            mm.then_inc(*B.inc("pe"))
            ws.done(B.tok("pe"))
        mm_tok = B.tok("pe")

        @B.block.scalar
        def _(s, o=o, mm_tok=mm_tok):
            B.wait(s, mm_tok)
            s.copy(y_ch[o], B.bank[o % 2][:, 0:TT]).then_inc(*B.inc("ac"))
        B.guard_bank(o % 2, B.tok("ac"))
        toks.append(B.tok("ac"))
    return toks



class L0FRBufs:
    def __init__(self, B, L):
        mf = B.MF32
        self.rsc = mf[:, 9 * TT + 32:9 * TT + 48]
        self.t8 = mf[:, 9 * TT + 48:9 * TT + 56]
        self.cw4 = mf[:, 9 * TT:9 * TT + 4]
        self.CB0 = B.mf(4)
        self.crow = L.rows[0:1, 0:TT]
        self.wrt = B.sb("l0fr_wrt", [P, 128], F32R)


def emit_l0fr_consts(B, R, d):
    @B.block.gpsimd
    def _(g):
        g.dma_start(out=R.wrt[:], in_=d["l0f_wrt"]).then_inc(*B.inc("ld", 16))
    return B.tok("ld")


def emit_l0ffn_router(B, R, xn_ch, xn_toks):
    @B.block.tensor
    def _(t):
        B.wait_bank(t, 6)
        for dd in range(NCH):
            B.wait(t, xn_toks[dd])
        for tc in range(4):
            for dd in range(NCH):
                mm = t.matmul(B.bank[6][:, 8 * tc:8 * (tc + 1)],
                              xn_ch[dd][:, tc * 128:(tc + 1) * 128],
                              R.wrt[:, 8 * dd:8 * (dd + 1)],
                              start=(dd == 0), stop=(dd == NCH - 1),
                              skip_group_check=True)
        mm.then_inc(*B.inc("pe"))
    lg_tok = B.tok("pe")

    cols = []
    for tc in range(4):
        lg = B.bank[6][:, 8 * tc:8 * (tc + 1)]
        m1 = R.rsc[:, 4 * tc + 0:4 * tc + 1]
        m2 = R.rsc[:, 4 * tc + 1:4 * tc + 2]
        rs = R.rsc[:, 4 * tc + 2:4 * tc + 3]
        cw = R.cw4[:, tc:tc + 1]

        @B.block.vector
        def _(v, lg=lg, m1=m1, m2=m2, tc=tc):
            if tc == 0:
                B.wait(v, lg_tok)
            B.wait(v, B.tok("dv"))
            i = v.tensor_reduce(m1, lg, axis=AX.X, op=ALU.max)
            B.chain(v, "dv", i)
            i = v.tensor_scalar(R.t8[:], lg, m1, None, op0=ALU.is_ge)
            B.chain(v, "dv", i)
            i = v.scalar_tensor_tensor(R.t8[:], R.t8[:], -1e9, lg,
                                       op0=ALU.mult, op1=ALU.add)
            B.chain(v, "dv", i)
            i = v.tensor_reduce(m2, R.t8[:], axis=AX.X, op=ALU.max)
            B.chain(v, "dv", i)
            v.tensor_tensor(m2, m2, m1, ALU.subtract).then_inc(*B.inc("dv"))
        dv1 = B.tok("dv")

        @B.block.scalar
        def _(s, m2=m2, rs=rs, dv1=dv1):
            B.wait(s, dv1)
            s.activation(rs, m2, AF.Exp).then_inc(*B.inc("ac"))
        ac1 = B.tok("ac")

        @B.block.vector
        def _(v, rs=rs, m1=m1, cw=cw, ac1=ac1):
            B.wait(v, ac1)
            B.wait(v, B.tok("dv"))
            i = v.tensor_scalar(cw, rs, 1.0, None, op0=ALU.add)
            B.chain(v, "dv", i)
            i = v.reciprocal(cw, cw)
            B.chain(v, "dv", i)
            i = v.tensor_tensor(m1, rs, cw, ALU.mult)
            B.chain(v, "dv", i)
            v.tensor_tensor(cw, cw, m1, ALU.add).then_inc(*B.inc("dv"))
        cols.append(cw)
    B.guard_bank(6, B.tok("dv"))
    return emit_cbcast(B, R, cols, R.CB0[:])



class L1FBufs:
    def __init__(self, B, L):
        mf = B.MF32
        self.COMB = mf[:, 9 * TT:9 * TT + 32]
        self.rsc = mf[:, 9 * TT + 32:9 * TT + 48]
        self.t8 = mf[:, 9 * TT + 48:9 * TT + 56]
        self.CB = B.mf(4)
        self.SGB = B.mf(5)
        self.crow = L.rows[0:1, 0:TT]
        self.sgrow = L.rows[0:1, TT:2 * TT]
        self.wrt = B.sb("l1fs_wrt", [P, 16 * 8], F32R)
        self.wsgt = B.sb("l1fs_wsgt", [P, 16], F32R)


def emit_l1ffn_consts(B, F, d):
    @B.block.gpsimd
    def _(g):
        g.dma_start(out=F.wrt[:], in_=d["l1f_wrt"]).then_inc(*B.inc("ld", 16))
        g.dma_start(out=F.wsgt[:], in_=d["l1f_wsgt"]).then_inc(*B.inc("ld", 16))
    return B.tok("ld")


def emit_l1ffn_router(B, F, xn_ch, xn_toks):
    @B.block.tensor
    def _(t):
        B.wait_bank(t, 6)
        for d in range(NCH):
            B.wait(t, xn_toks[d])
        for tc in range(4):
            for d in range(NCH):
                mm = t.matmul(B.bank[6][:, 8 * tc:8 * (tc + 1)],
                              xn_ch[d][:, tc * 128:(tc + 1) * 128],
                              F.wrt[:, 8 * d:8 * (d + 1)],
                              start=(d == 0), stop=(d == NCH - 1),
                              skip_group_check=True)
        mm.then_inc(*B.inc("pe"))
    lg_tok = B.tok("pe")

    for tc in range(4):
        lg = B.bank[6][:, 8 * tc:8 * (tc + 1)]
        pp = F.COMB[:, 8 * tc:8 * (tc + 1)]
        m1 = F.rsc[:, 4 * tc + 0:4 * tc + 1]
        m2 = F.rsc[:, 4 * tc + 1:4 * tc + 2]
        rs = F.rsc[:, 4 * tc + 2:4 * tc + 3]

        @B.block.vector
        def _(v, lg=lg, pp=pp, m1=m1, tc=tc):
            if tc == 0:
                B.wait(v, lg_tok)
            else:
                B.wait(v, B.tok("ac"))
            B.wait(v, B.tok("dv"))
            i = v.tensor_reduce(m1, lg, axis=AX.X, op=ALU.max)
            B.chain(v, "dv", i)
            v.tensor_scalar(pp, lg, m1, None, op0=ALU.subtract).then_inc(*B.inc("dv"))
        dv1 = B.tok("dv")

        @B.block.scalar
        def _(s, pp=pp, dv1=dv1):
            B.wait(s, dv1)
            s.activation(pp, pp, AF.Exp).then_inc(*B.inc("ac"))
        ac1 = B.tok("ac")

        @B.block.vector
        def _(v, pp=pp, m1=m1, m2=m2, rs=rs, ac1=ac1):
            B.wait(v, ac1)
            i = v.tensor_reduce(rs, pp, axis=AX.X, op=ALU.add)
            B.chain(v, "dv", i)
            i = v.reciprocal(rs, rs)
            B.chain(v, "dv", i)
            i = v.tensor_scalar(pp, pp, rs, None, op0=ALU.mult)
            B.chain(v, "dv", i)
            i = v.tensor_reduce(m1, pp, axis=AX.X, op=ALU.max)
            B.chain(v, "dv", i)
            i = v.tensor_scalar(F.t8[:], pp, m1, None, op0=ALU.is_ge)
            B.chain(v, "dv", i)
            i = v.scalar_tensor_tensor(F.t8[:], F.t8[:], -1e9, pp,
                                       op0=ALU.mult, op1=ALU.add)
            B.chain(v, "dv", i)
            i = v.tensor_reduce(m2, F.t8[:], axis=AX.X, op=ALU.max)
            B.chain(v, "dv", i)
            i = v.tensor_scalar(F.t8[:], pp, m2, None, op0=ALU.is_ge)
            B.chain(v, "dv", i)
            v.tensor_tensor(pp, pp, F.t8[:], ALU.mult).then_inc(*B.inc("dv"))
    B.guard_bank(6, B.tok("dv"))
    return B.tok("dv")


def emit_cbcast(B, F, col_src_ap_list, dst_tile):
    @B.block.tensor
    def _(t):
        B.wait_bank(t, 6)
        for tc in range(4):
            mm = t.matmul(B.bank[6][0:1, tc * 128:(tc + 1) * 128],
                          col_src_ap_list[tc], B.consts["c_ident"][:],
                          start=True, stop=True, skip_group_check=True)
        mm.then_inc(*B.inc("pe"))
    r_tok = B.tok("pe")

    @B.block.vector
    def _(v):
        B.wait(v, r_tok)
        B.wait(v, B.tok("dv"))
        v.tensor_copy(F.crow[:], B.bank[6][0:1, 0:TT]).then_inc(*B.inc("dv"))
    B.guard_bank(6, B.tok("dv"))
    c_tok = B.tok("dv")

    @B.block.tensor
    def _(t):
        B.wait_bank(t, 3)
        B.wait(t, c_tok)
        t.matmul(B.bank[3][:, 0:TT], B.consts["c_ones_row"][:], F.crow[:],
                 start=True, stop=True).then_inc(*B.inc("pe"))
    b_tok = B.tok("pe")

    @B.block.vector
    def _(v):
        B.wait(v, b_tok)
        B.wait(v, B.tok("dv"))
        v.tensor_copy(dst_tile, B.bank[3][:, 0:TT]).then_inc(*B.inc("dv"))
    B.guard_bank(3, B.tok("dv"))
    return B.tok("dv")


def emit_gu_mlp(B, ws, xn_ch, xn_toks, y_ch, GU_dram, D_dram, n_inter_ch,
                gu_off, sgbufs, asup, scale_tile, scale_tok, first_writer,
                tag=""):
    SUP = 16
    n_sup = (n_inter_ch + SUP - 1) // SUP
    OGS = [(0, 6), (6, 12), (12, 16)]
    for sup in range(n_sup):
        i0 = sup * SUP
        i1 = min(i0 + SUP, n_inter_ch)
        ni = i1 - i0
        a_toks = []
        for ii, ic in enumerate(range(i0, i1)):
            ws.load(GU_dram[ic])
            ws.load(GU_dram[ic + gu_off])
            bg = B.bank[0 if ii % 2 == 0 else 2]
            bu = B.bank[1 if ii % 2 == 0 else 3]

            @B.block.tensor
            def _(t, bg=bg, bu=bu, ii=ii, sup=sup):
                slab = ws.use(t)
                B.wait_bank(t, 0 if ii % 2 == 0 else 2)
                for d in range(NCH):
                    if sup == 0 and ii == 0:
                        B.wait(t, xn_toks[d])
                    mm = t.matmul(bg[:, 0:TT], slab[:, d * 128:(d + 1) * 128],
                                  xn_ch[d], start=(d == 0), stop=(d == NCH - 1),
                                  skip_group_check=True)
                mm.then_inc(*B.inc("pe"))
                ws.done(B.tok("pe"))
                slab = ws.use(t)
                B.wait_bank(t, 1 if ii % 2 == 0 else 3)
                for d in range(NCH):
                    mm = t.matmul(bu[:, 0:TT], slab[:, d * 128:(d + 1) * 128],
                                  xn_ch[d], start=(d == 0), stop=(d == NCH - 1),
                                  skip_group_check=True)
                mm.then_inc(*B.inc("pe"))
                ws.done(B.tok("pe"))
            mm_tok = B.tok("pe")
            sg = sgbufs[ii % 2]

            @B.block.scalar
            def _(s, sg=sg, mm_tok=mm_tok, bg=bg):
                B.wait(s, mm_tok)
                s.activation(sg[:], bg[:, 0:TT], AF.Silu).then_inc(*B.inc("ac"))
            ac_tok = B.tok("ac")

            @B.block.vector
            def _(v, sg=sg, ac_tok=ac_tok, bu=bu, ii=ii):
                B.wait(v, ac_tok)
                if scale_tile is not None:
                    B.wait(v, scale_tok)
                    i = v.tensor_tensor(asup[ii], sg[:], bu[:, 0:TT], ALU.mult)
                    B.chain(v, "dv", i)
                    v.tensor_tensor(asup[ii], asup[ii].bitcast(F32), scale_tile,
                                    ALU.mult).then_inc(*B.inc("dv"))
                else:
                    v.tensor_tensor(asup[ii], sg[:], bu[:, 0:TT],
                                    ALU.mult).then_inc(*B.inc("dv"))
            a_toks.append(B.tok("dv"))
            B.guard_bank(0 if ii % 2 == 0 else 2, B.tok("ac"))
            B.guard_bank(1 if ii % 2 == 0 else 3, B.tok("dv"))

        for og_lo, og_hi in OGS:
            ogw = og_hi - og_lo
            ow = ogw * 128
            ii = 0
            while ii < ni:
                npair = 2 if ii + 1 < ni else 1
                ic = i0 + ii
                src = D_dram[ic:ic + npair, :, og_lo * 128:og_hi * 128] \
                    .transpose([1, 0, 2])
                ws.load_view(src, npair * ow)

                @B.block.tensor
                def _(t, ii=ii, ogw=ogw, ow=ow, npair=npair):
                    slab = ws.use(t)
                    if ii == 0:
                        for bi in range(ogw):
                            B.wait_bank(t, bi)
                    for a in range(npair):
                        B.wait(t, a_toks[ii + a])
                        for oo in range(ogw):
                            mm = t.matmul(B.bank[oo][:, 0:TT],
                                          slab[:, a * ow + oo * 128:
                                               a * ow + (oo + 1) * 128],
                                          asup[ii + a],
                                          start=(ii + a == 0),
                                          stop=(ii + a == ni - 1),
                                          skip_group_check=True)
                    mm.then_inc(*B.inc("pe"))
                    ws.done(B.tok("pe"))
                ii += npair
            down_tok = B.tok("pe")
            for oo in range(ogw):
                o = og_lo + oo

                @B.block.vector
                def _(v, oo=oo, o=o, down_tok=down_tok, sup=sup):
                    B.wait(v, down_tok)
                    if first_writer and sup == 0:
                        v.tensor_copy(y_ch[o], B.bank[oo][:, 0:TT]) \
                            .then_inc(*B.inc("dv"))
                    else:
                        v.tensor_tensor(y_ch[o], y_ch[o].bitcast(F32),
                                        B.bank[oo][:, 0:TT], ALU.add) \
                            .then_inc(*B.inc("dv"))
                B.guard_bank(oo, B.tok("dv"))
    return B.tok("dv")


def emit_l1ffn(B, F, ws, xn_ch, xn_toks, y_ch, d, sgbufs, asup, tile):
    r_tok = emit_l1ffn_router(B, F, xn_ch, xn_toks)

    @B.block.tensor
    def _(t):
        B.wait_bank(t, 3)
        for dd in range(NCH):
            mm = t.matmul(B.bank[3][0:1, 0:TT], F.wsgt[:, dd:dd + 1],
                          xn_ch[dd], start=(dd == 0), stop=(dd == NCH - 1),
                          skip_group_check=True)
        mm.then_inc(*B.inc("pe"))
    sg_tok = B.tok("pe")

    @B.block.scalar
    def _(s):
        B.wait(s, sg_tok)
        s.activation(F.sgrow[:], B.bank[3][0:1, 0:TT],
                     AF.Sigmoid).then_inc(*B.inc("ac"))
    B.guard_bank(3, B.tok("ac"))
    sgr_tok = B.tok("ac")

    @B.block.tensor
    def _(t):
        B.wait_bank(t, 3)
        B.wait(t, sgr_tok)
        t.matmul(B.bank[3][:, 0:TT], B.consts["c_ones_row"][:], F.sgrow[:],
                 start=True, stop=True).then_inc(*B.inc("pe"))
    sgb_tok = B.tok("pe")

    @B.block.vector
    def _(v):
        B.wait(v, sgb_tok)
        v.tensor_copy(F.SGB[:], B.bank[3][:, 0:TT]).then_inc(*B.inc("dv"))
    B.guard_bank(3, B.tok("dv"))
    sgbt = B.tok("dv")

    emit_gu_mlp(B, ws, xn_ch, xn_toks, y_ch, d["WsGU"], d["WsD"], 44, 44,
                sgbufs, asup, F.SGB[:], sgbt, first_writer=True)

    for e in range(8):
        cols = [F.COMB[:, 8 * tc + e:8 * tc + e + 1] for tc in range(4)]
        cb_tok = emit_cbcast(B, F, cols, F.CB[:])
        emit_gu_mlp(B, ws, xn_ch, xn_toks, y_ch, d[f"WeGU{e}"], d[f"WeD{e}"],
                    11, 11, sgbufs, asup, F.CB[:], cb_tok, first_writer=False)
    return B.tok("dv")



class L1ABufs:
    def __init__(self, B, yt_slabs, xnt_slabs):
        self.PRE = yt_slabs
        self.QKV = xnt_slabs
        self.SZ1 = [B.mr(8 + k) for k in range(8)]
        self.YNF = [B.mr(k) for k in range(8)]
        self.TAIL = B.sb("a1_tail", [P, 48], F32)
        self.BET = B.MF32[0:16, 10 * TT:11 * TT]
        self.GG = B.MF32[0:16, 11 * TT:12 * TT]
        self.gsc = B.SPR
        mb = B.MBF
        self.QBF = [mb[:, 128 * j:128 * (j + 1)] for j in range(4)]
        self.KBF = [mb[:, 512 + 128 * j:512 + 128 * (j + 1)] for j in range(4)]
        self.VBF = [mb[:, 1024 + 128 * j:1024 + 128 * (j + 1)] for j in range(8)]
        self.kT = mb[:, 2048:2560]
        self.qT = mb[:, 2560:3072]
        self.vT = mb[:, 3072:4096]
        self.Mbf = mb[:, 4096:4224]
        self.Mtbf = mb[:, 4224:4352]
        self.Atbf = mb[:, 4352:4480]
        self.Zs = [mb[:, 4480 + 128 * j:4480 + 128 * (j + 1)] for j in range(6)]
        self.Pb = [mb[:, 5248 + 128 * j:5248 + 128 * (j + 1)] for j in range(2)]
        self.Xap = [mb[:, 5504 + 64 * j:5504 + 64 * (j + 1)] for j in range(2)]
        self.kp = mb[:, 5632:5696]
        mf = B.MF32
        self.OT = B.mf(0, 2)
        self.EMS = [mf[:, 2 * TT:2 * TT + 128], mf[:, 2 * TT + 128:2 * TT + 256]]
        self.TM = mf[:, 2 * TT + 256:2 * TT + 384]
        self.G16 = mf[:, 3 * TT:3 * TT + 192]
        self.EX = mf[:, 3 * TT + 192:3 * TT + 240]
        self.LR = mf[:, 3 * TT + 240:3 * TT + 256]
        self.RSO = mf[:, 3 * TT + 256:3 * TT + 272]
        self.Rf = mf[:, 8 * TT:8 * TT + 64]
        self.t64 = mf[:, 8 * TT + 64:8 * TT + 128]
        self.GTOT = mf[0:1, 2 * TT + 1408:2 * TT + 1424]
        self.EGR = mf[0:1, 2 * TT + 1424:2 * TT + 1440]
        self.EGRp = mf[0:1, 2 * TT + 1440:2 * TT + 1456]
        self.EGB = mf[0:64, 2 * TT + 1472:2 * TT + 1488]
        self.S = B.sb("a1_s", [64, 1024], F32R)
        self.Shi = B.sb("a1_shi", [P, 512], F32R)
        self.cw1 = B.sb("a1_cw1", [P, 64], F32)
        self.nea = B.sb("a1_nea", [16, 1], F32)
        self.dtb = B.sb("a1_dtb", [16, 1], F32)
        self.nwc = B.sb("a1_nwc", [P, 8], F32)


def emit_l1attn_consts(B, A, d):
    @B.block.gpsimd
    def _(g):
        for src, dst in [(d["a1d_cw1"], A.cw1[:]), (d["a1d_alog"], A.nea[:]),
                         (d["a1d_dtb"], A.dtb[:]), (d["a1d_nwc"], A.nwc[:])]:
            g.dma_start(out=dst, in_=src).then_inc(*B.inc("ld", 16))
        zr = d["c_zeros_r"]
        g.dma_start(out=A.S[0:64, 0:512], in_=zr[0:64, :]).then_inc(*B.inc("ld", 16))
        g.dma_start(out=A.S[0:64, 512:1024], in_=zr[0:64, :]).then_inc(*B.inc("ld", 16))
        g.dma_start(out=A.Shi[:], in_=zr[:]).then_inc(*B.inc("ld", 16))
    ld_tok = B.tok("ld")

    @B.block.scalar
    def _(s):
        B.wait(s, ld_tok)
        s.activation(A.nea[:], A.nea[:], AF.Exp).then_inc(*B.inc("ac"))
    ac_tok = B.tok("ac")

    @B.block.vector
    def _(v):
        B.wait(v, ac_tok)
        i = v.tensor_scalar(A.nea[:], A.nea[:], -1.0, None, op0=ALU.mult)
        B.chain(v, "dv", i)
        v.memset(A.TAIL[:], 0.0).then_inc(*B.inc("dv"))
    return B.tok("ld"), B.tok("dv")


def _proj_evac(B, ws, Wdram, n_o, xn_ch, xn_toks, evac_fn, first_wait=None):
    for o in range(n_o):
        ws.load(Wdram[o])

        @B.block.tensor
        def _(t, o=o):
            slab = ws.use(t)
            bk = o % 2
            B.wait_bank(t, bk)
            if o == 0 and first_wait is not None:
                B.wait(t, first_wait)
            for dd in range(NCH):
                if o == 0 and xn_toks is not None:
                    B.wait(t, xn_toks[dd])
                mm = t.matmul(B.bank[bk][:, 0:TT], slab[:, dd * 128:(dd + 1) * 128],
                              xn_ch[dd], start=(dd == 0), stop=(dd == NCH - 1),
                              skip_group_check=True)
            mm.then_inc(*B.inc("pe"))
            ws.done(B.tok("pe"))
        mm_tok = B.tok("pe")
        evac_fn(o, B.bank[o % 2], mm_tok)


def emit_l1attn(B, A, ws, xn_ch, xn_toks, y_ch_out, d, tile):
    nc = B.nc
    def ev_qkv(o, bank, mm_tok):
        @B.block.scalar
        def _(s, o=o, bank=bank, mm_tok=mm_tok):
            B.wait(s, mm_tok)
            s.copy(A.PRE[o][:], bank[:, 0:TT]).then_inc(*B.inc("ac"))
        B.guard_bank(o % 2, B.tok("ac"))
    _proj_evac(B, ws, d["W1qkvT"], 16, xn_ch, xn_toks, ev_qkv)

    def ev_z(o, bank, mm_tok):
        @B.block.scalar
        def _(s, o=o, bank=bank, mm_tok=mm_tok):
            B.wait(s, mm_tok)
            s.activation(A.SZ1[o][:], bank[:, 0:TT], AF.Silu).then_inc(*B.inc("ac"))
        B.guard_bank(o % 2, B.tok("ac"))
    _proj_evac(B, ws, d["W1zT"], 8, xn_ch, None, ev_z)

    ws.load(d["a1d_wbt"])
    ws.load(d["a1d_wat"])

    @B.block.tensor
    def _(t):
        B.wait_bank(t, 2)
        B.wait_bank(t, 3)
        slab = ws.use(t)
        for dd in range(NCH):
            mb = t.matmul(B.bank[2][0:16, 0:TT], slab[:, 16 * dd:16 * (dd + 1)],
                          xn_ch[dd], start=(dd == 0), stop=(dd == NCH - 1),
                          skip_group_check=True)
        mb.then_inc(*B.inc("pe"))
        ws.done(B.tok("pe"))
        slab = ws.use(t)
        for dd in range(NCH):
            ma = t.matmul(B.bank[3][0:16, 0:TT], slab[:, 16 * dd:16 * (dd + 1)],
                          xn_ch[dd], start=(dd == 0), stop=(dd == NCH - 1),
                          skip_group_check=True)
        ma.then_inc(*B.inc("pe"))
        ws.done(B.tok("pe"))
    ba_tok = B.tok("pe")

    @B.block.scalar
    def _(s):
        B.wait(s, ba_tok)
        s.activation(A.BET, B.bank[2][0:16, 0:TT], AF.Sigmoid).then_inc(*B.inc("ac"))
    B.guard_bank(2, B.tok("ac"))
    bet_tok = B.tok("ac")

    v_r = A.gsc[0:16, 0:TT]
    a_r = A.gsc[0:16, TT:2 * TT]

    @B.block.vector
    def _(v):
        B.wait(v, ba_tok)
        v.tensor_scalar(v_r, B.bank[3][0:16, 0:TT], A.dtb[0:16, 0:1], None,
                        op0=ALU.add).then_inc(*B.inc("dv"))
    B.guard_bank(3, B.tok("dv"))
    g1 = B.tok("dv")

    @B.block.scalar
    def _(s):
        B.wait(s, g1)
        i = s.activation(a_r, v_r, AF.Abs)
        B.chain(s, "ac", i)
        i = s.activation(a_r, a_r, AF.Exp, scale=-1.0)
        B.chain(s, "ac", i)
        i = s.activation(a_r, a_r, AF.Ln, bias=B.consts["c_eps"][0:16, 2:3])
        B.chain(s, "ac", i)
        s.activation(A.GG, v_r, AF.Relu).then_inc(*B.inc("ac"))
    g2 = B.tok("ac")

    @B.block.vector
    def _(v):
        B.wait(v, g2)
        i = v.tensor_tensor(A.GG, A.GG, a_r, ALU.add)
        B.chain(v, "dv", i)
        v.tensor_scalar(A.GG, A.GG, A.nea[0:16, 0:1], None,
                        op0=ALU.mult).then_inc(*B.inc("dv"))
    gg_tok = B.tok("dv")
    A._gg_tok = gg_tok
    A._bet_tok = bet_tok

    qkv_toks = []
    for k in range(NCH):
        @B.block.vector
        def _(v, k=k):
            B.wait(v, B.tok("ac"))
            B.wait(v, B.tok("dv"))
            pre = A.PRE[k][:].bitcast(F32)
            cw = A.cw1
            i = v.tensor_copy(B.CSC[:, 0:3], A.TAIL[:, 3 * k:3 * k + 3])
            B.chain(v, "dv", i)
            i = v.tensor_copy(B.CSC[:, 3:6], A.PRE[k][:, 0:3].bitcast(F32))
            B.chain(v, "dv", i)
            i = v.tensor_copy(A.TAIL[:, 3 * k:3 * k + 3],
                              A.PRE[k][:, TT - 3:TT].bitcast(F32))
            B.chain(v, "dv", i)
            i = v.tensor_scalar(B.CSC[:, 8:11], B.CSC[:, 0:3],
                                cw[:, 4 * k:4 * k + 1], None, op0=ALU.mult)
            B.chain(v, "dv", i)
            for kk in range(1, 4):
                i = v.scalar_tensor_tensor(B.CSC[:, 8:11], B.CSC[:, kk:kk + 3],
                                           cw[:, 4 * k + kk:4 * k + kk + 1],
                                           B.CSC[:, 8:11], op0=ALU.mult, op1=ALU.add)
                B.chain(v, "dv", i)
            i = v.tensor_scalar(B.CSC[:, 16:16 + TT - 3], pre[:, 0:TT - 3],
                                cw[:, 4 * k:4 * k + 1], None, op0=ALU.mult)
            B.chain(v, "dv", i)
            for kk in range(1, 3):
                i = v.scalar_tensor_tensor(B.CSC[:, 16:16 + TT - 3],
                                           pre[:, kk:TT - 3 + kk],
                                           cw[:, 4 * k + kk:4 * k + kk + 1],
                                           B.CSC[:, 16:16 + TT - 3],
                                           op0=ALU.mult, op1=ALU.add)
                B.chain(v, "dv", i)
            v.scalar_tensor_tensor(B.CSC[:, 16:16 + TT - 3],
                                   pre[:, 3:TT], cw[:, 4 * k + 3:4 * k + 4],
                                   B.CSC[:, 16:16 + TT - 3],
                                   op0=ALU.mult, op1=ALU.add).then_inc(*B.inc("dv"))
        dvt = B.tok("dv")

        @B.block.scalar
        def _(s, k=k, dvt=dvt):
            B.wait(s, dvt)
            i = s.activation(A.QKV[k][:, 0:3], B.CSC[:, 8:11], AF.Silu)
            B.chain(s, "ac", i)
            s.activation(A.QKV[k][:, 3:TT], B.CSC[:, 16:16 + TT - 3],
                         AF.Silu).then_inc(*B.inc("ac"))
        qkv_toks.append(B.tok("ac"))
    return qkv_toks


def emit_l1attn_gdn(B, A, qkv_toks, gg_tok, bet_tok, s_tok_in):
    Ibf = B.consts["c_ident_bfr"]
    If32 = B.consts["c_ident"]
    onesr = B.consts["c_onesrow_f32"]
    onesc = B.consts["c_onescol_f32"]
    seg = lambda i: A.G16[:, 16 * i:16 * (i + 1)]
    GT, BT, G, LB, LRK16, LRQ16, GKP, GMB, GKM, NGKM, GA = [seg(i) for i in range(11)]
    EDEC = A.EX[:, 0:16]
    EO = A.EX[:, 16:32]
    EK = A.EX[:, 32:48]
    eps2 = B.consts["c_eps"][:, 1:2]

    s_tok = s_tok_in
    ynf_toks = []
    for cc in range(4):
        csl = slice(cc * 128, (cc + 1) * 128)

        @B.block.vector
        def _(v, cc=cc, csl=csl):
            B.wait(v, B.tok("dv"))
            B.wait(v, B.tok("pe"))
            for j in range(4):
                if cc == 0:
                    B.wait(v, qkv_toks[j])
                i = v.tensor_copy(A.QBF[j][:], A.QKV[j][:, csl].bitcast(F32))
                B.chain(v, "dv", i)
            for j in range(4):
                if cc == 0:
                    B.wait(v, qkv_toks[4 + j])
                i = v.tensor_copy(A.KBF[j][:], A.QKV[4 + j][:, csl].bitcast(F32))
                B.chain(v, "dv", i)
            for j in range(7):
                if cc == 0:
                    B.wait(v, qkv_toks[8 + j])
                i = v.tensor_copy(A.VBF[j][:], A.QKV[8 + j][:, csl].bitcast(F32))
                B.chain(v, "dv", i)
            if cc == 0:
                B.wait(v, qkv_toks[15])
            v.tensor_copy(A.VBF[7][:],
                          A.QKV[15][:, csl].bitcast(F32)).then_inc(*B.inc("dv"))
        bf_tok = B.tok("dv")

        for (srcs, dst, lbl) in ((A.QBF, A.qT, "q"), (A.KBF, A.kT, "k")):
            @B.block.tensor
            def _(t, srcs=srcs, cc=cc):
                B.wait_bank(t, 7)
                B.wait(t, bf_tok)
                for j in range(4):
                    mm = t.transpose(B.bank_bf[:, 128 * j:128 * (j + 1)],
                                     srcs[j][:], Ibf[:])
                mm.then_inc(*B.inc("pe"))
            tr_tok = B.tok("pe")

            @B.block.vector
            def _(v, dst=dst, tr_tok=tr_tok):
                B.wait(v, tr_tok)
                B.wait(v, B.tok("dv"))
                v.tensor_copy(dst, B.bank_bf[:, 0:512]).then_inc(*B.inc("dv"))
            B.guard_bank(7, B.tok("dv"))
        for half in range(2):
            @B.block.tensor
            def _(t, half=half):
                B.wait_bank(t, 7)
                for j in range(4):
                    mm = t.transpose(B.bank_bf[:, 128 * j:128 * (j + 1)],
                                     A.VBF[4 * half + j][:], Ibf[:])
                mm.then_inc(*B.inc("pe"))
            tr_tok = B.tok("pe")

            @B.block.vector
            def _(v, half=half, tr_tok=tr_tok):
                B.wait(v, tr_tok)
                B.wait(v, B.tok("dv"))
                v.tensor_copy(A.vT[:, 512 * half:512 * (half + 1)],
                              B.bank_bf[:, 0:512]).then_inc(*B.inc("dv"))
            B.guard_bank(7, B.tok("dv"))
        tr_done = B.tok("dv")

        @B.block.vector
        def _(v):
            B.wait(v, tr_done)
            csc = B.CSC[:, 0:512]
            i = v.tensor_tensor(csc, A.kT, A.kT, ALU.mult)
            B.chain(v, "dv", i)
            i = v.tensor_reduce(A.LR[:, 0:8],
                                csc.rearrange("p (h k) -> p h k", k=64),
                                axis=AX.X, op=ALU.add)
            B.chain(v, "dv", i)
            i = v.tensor_tensor(csc, A.qT, A.qT, ALU.mult)
            B.chain(v, "dv", i)
            v.tensor_reduce(A.LR[:, 8:16],
                            csc.rearrange("p (h k) -> p h k", k=64),
                            axis=AX.X, op=ALU.add).then_inc(*B.inc("dv"))
        ss_tok = B.tok("dv")

        @B.block.scalar
        def _(s, ss_tok=ss_tok):
            B.wait(s, ss_tok)
            i = s.activation(A.LR[:, 0:8], A.LR[:, 0:8], AF.Ln, bias=eps2)
            B.chain(s, "ac", i)
            s.activation(A.LR[:, 8:16], A.LR[:, 8:16], AF.Ln,
                         bias=eps2).then_inc(*B.inc("ac"))
        ln_tok = B.tok("ac")

        @B.block.tensor
        def _(t, cc=cc, csl=csl):
            B.wait_bank(t, 2)
            B.wait(t, gg_tok)
            B.wait(t, bet_tok)
            t.transpose(B.bank[2][:, 0:16], A.GG[:, csl],
                        If32[0:16, 0:16]).then_inc(*B.inc("pe"))
            t.transpose(B.bank[2][:, 16:32], A.BET[:, csl],
                        If32[0:16, 0:16]).then_inc(*B.inc("pe"))
        gtr_tok = B.tok("pe")

        @B.block.vector
        def _(v, gtr_tok=gtr_tok):
            B.wait(v, gtr_tok)
            B.wait(v, B.tok("dv"))
            i = v.tensor_copy(GT, B.bank[2][:, 0:16])
            B.chain(v, "dv", i)
            v.tensor_copy(BT, B.bank[2][:, 16:32]).then_inc(*B.inc("dv"))
        B.guard_bank(2, B.tok("dv"))
        gb_tok = B.tok("dv")

        @B.block.tensor
        def _(t, gb_tok=gb_tok):
            B.wait_bank(t, 2)
            B.wait(t, gb_tok)
            t.matmul(B.bank[2][:, 0:16], B.consts["c_tri_incl"][:], GT,
                     start=True, stop=True).then_inc(*B.inc("pe"))
            t.matmul(B.bank[2][0:1, 16:32], onesc[:], GT,
                     start=True, stop=True).then_inc(*B.inc("pe"))
        cum_tok = B.tok("pe")

        @B.block.vector
        def _(v, cum_tok=cum_tok):
            B.wait(v, cum_tok)
            B.wait(v, B.tok("dv"))
            i = v.tensor_copy(G, B.bank[2][:, 0:16])
            B.chain(v, "dv", i)
            v.tensor_copy(A.GTOT[:], B.bank[2][0:1, 16:32]).then_inc(*B.inc("dv"))
        B.guard_bank(2, B.tok("dv"))
        gc_tok = B.tok("dv")

        @B.block.scalar
        def _(s, gc_tok=gc_tok):
            B.wait(s, gc_tok)
            i = s.activation(LB, BT, AF.Ln)
            B.chain(s, "ac", i)
            s.activation(A.EGR[:], A.GTOT[:], AF.Exp).then_inc(*B.inc("ac"))
        lb_tok = B.tok("ac")

        @B.block.vector
        def _(v, lb_tok=lb_tok):
            B.wait(v, lb_tok)
            B.wait(v, B.tok("dv"))
            i = v.tensor_copy(A.EGRp[0:1, 0:8:2], A.EGR[0:1, 0:16:4])
            B.chain(v, "dv", i)
            i = v.tensor_copy(A.EGRp[0:1, 1:8:2], A.EGR[0:1, 1:16:4])
            B.chain(v, "dv", i)
            i = v.tensor_copy(A.EGRp[0:1, 8:16:2], A.EGR[0:1, 2:16:4])
            B.chain(v, "dv", i)
            v.tensor_copy(A.EGRp[0:1, 9:16:2],
                          A.EGR[0:1, 3:16:4]).then_inc(*B.inc("dv"))
        egp_tok = B.tok("dv")

        @B.block.vector
        def _(v, lb_tok=lb_tok, ln_tok=ln_tok):
            B.wait(v, lb_tok)
            B.wait(v, ln_tok)
            B.wait(v, B.tok("dv"))
            i = v.tensor_scalar(A.LR[:, 0:8], A.LR[:, 0:8], -0.5, None, op0=ALU.mult)
            B.chain(v, "dv", i)
            i = v.tensor_scalar(A.LR[:, 8:16], A.LR[:, 8:16], -0.5,
                                -0.5 * float(np.log(64.0)), op0=ALU.mult, op1=ALU.add)
            B.chain(v, "dv", i)
            i = v.tensor_copy(LRK16[:, 0:16:2], A.LR[:, 0:8])
            B.chain(v, "dv", i)
            i = v.tensor_copy(LRK16[:, 1:16:2], A.LR[:, 0:8])
            B.chain(v, "dv", i)
            i = v.tensor_copy(LRQ16[:, 0:16:2], A.LR[:, 8:16])
            B.chain(v, "dv", i)
            i = v.tensor_copy(LRQ16[:, 1:16:2], A.LR[:, 8:16])
            B.chain(v, "dv", i)
            i = v.tensor_tensor(GKP, G, LRK16, ALU.add)
            B.chain(v, "dv", i)
            i = v.tensor_tensor(GMB, GKP, LB, ALU.add)
            B.chain(v, "dv", i)
            i = v.tensor_tensor(GKM, G, LRK16, ALU.subtract)
            B.chain(v, "dv", i)
            i = v.tensor_scalar(NGKM, GKM, -1.0, None, op0=ALU.mult)
            B.chain(v, "dv", i)
            v.tensor_tensor(GA, G, LRQ16, ALU.add).then_inc(*B.inc("dv"))
        cols_tok = B.tok("dv")

        @B.block.tensor
        def _(t, cols_tok=cols_tok, egp_tok=egp_tok):
            B.wait_bank(t, 2)
            B.wait(t, cols_tok)
            B.wait(t, egp_tok)
            t.matmul(B.bank[2][0:64, 0:16], onesr[0:1, 0:64], A.EGRp[:],
                     start=True, stop=True).then_inc(*B.inc("pe"))
            t.matmul(B.bank[2][:, 16:32], onesr[:], A.GTOT[:],
                     start=True, stop=True).then_inc(*B.inc("pe"))
        eg_tok = B.tok("pe")

        @B.block.vector
        def _(v, eg_tok=eg_tok):
            B.wait(v, eg_tok)
            B.wait(v, B.tok("dv"))
            i = v.tensor_copy(A.EGB[:], B.bank[2][0:64, 0:16])
            B.chain(v, "dv", i)
            v.tensor_tensor(EDEC, B.bank[2][:, 16:32], GKM,
                            ALU.subtract).then_inc(*B.inc("dv"))
        B.guard_bank(2, B.tok("dv"))
        ed_tok = B.tok("dv")

        @B.block.scalar
        def _(s, ed_tok=ed_tok):
            B.wait(s, ed_tok)
            i = s.activation(EDEC, EDEC, AF.Exp)
            B.chain(s, "ac", i)
            i = s.activation(EO, GA, AF.Exp)
            B.chain(s, "ac", i)
            s.activation(EK, GKP, AF.Exp).then_inc(*B.inc("ac"))
        ex_tok = B.tok("ac")

        CBv = B.MF32[:, 4 * TT:5 * TT].bitcast(BF16)
        S5 = B.MF32
        fsets = [
            dict(Mbf=A.Mbf, Mtbf=A.Mtbf, Atbf=A.Atbf, TM=A.TM, TMA=A.TMA2 if False else B.MF32[:, 2 * TT + 384:2 * TT + 512],
                 EMS0=A.EMS[0], EMS1=A.EMS[1]),
            dict(Mbf=CBv[:, 0:128], Mtbf=CBv[:, 128:256], Atbf=CBv[:, 256:384],
                 TM=S5[:, 5 * TT:5 * TT + 128], TMA=S5[:, 5 * TT + 128:5 * TT + 256],
                 EMS0=S5[:, 5 * TT + 256:5 * TT + 384],
                 EMS1=S5[:, 5 * TT + 384:5 * TT + 512]),
        ]
        sigma_free = [None, None]
        tm_free = [None, None]
        et_toks = [None] * 16
        et2_toks = [None] * 16
        gram_toks = [None] * 16

        def head_params(h):
            hp = h // 2
            j = hp // 2
            base = (hp % 2) * 64
            grp = hp % 2
            slot = (hp // 2) * 2 + (h % 2)
            scol = 512 * grp + 64 * slot
            return hp, j, base, grp, slot, scol

        def emit_front(h):
            sg = h % 2
            fs = fsets[sg]
            hp, j, base, grp, slot, scol = head_params(h)
            ksl = A.KBF[j][base:base + 64, :]
            qsl = A.QBF[j][base:base + 64, :]

            @B.block.tensor
            def _(t, ksl=ksl, qsl=qsl, h=h):
                B.wait_bank(t, 0)
                B.wait_bank(t, 1)
                if h == 0:
                    B.wait(t, ex_tok)
                t.matmul(B.bank[0][:, 0:128], ksl, ksl, start=True,
                         stop=True).then_inc(*B.inc("pe"))
                t.matmul(B.bank[1][:, 0:128], ksl, qsl, start=True,
                         stop=True).then_inc(*B.inc("pe"))
            gram_toks[h] = B.tok("pe")

            @B.block.vector
            def _(v, h=h, fs=fs, ft=tm_free[sg]):
                B.wait(v, B.tok("dv"))
                if ft is not None:
                    B.wait(v, ft)
                i = v.tensor_scalar(fs["TM"], B.consts["c_onek_bf"][:],
                                    NGKM[:, h:h + 1], None, op0=ALU.mult)
                B.chain(v, "dv", i)
                v.tensor_scalar(fs["TMA"], B.consts["c_onek_bf"][:],
                                GA[:, h:h + 1], None,
                                op0=ALU.mult).then_inc(*B.inc("dv"))
            p_tok = B.tok("dv")

            @B.block.tensor
            def _(t, p_tok=p_tok, fs=fs):
                B.wait_bank(t, 2)
                B.wait(t, p_tok)
                t.transpose(B.bank[2][:, 0:128], fs["TM"], If32[:]) \
                    .then_inc(*B.inc("pe"))
                t.transpose(B.bank[2][:, 128:256], fs["TMA"], If32[:]) \
                    .then_inc(*B.inc("pe"))
            ptr_tok = B.tok("pe")

            @B.block.vector
            def _(v, ptr_tok=ptr_tok, fs=fs):
                B.wait(v, ptr_tok)
                B.wait(v, B.tok("dv"))
                i = v.tensor_tensor(fs["EMS0"], B.bank[2][:, 0:128],
                                    B.consts["c_pen_m"][:], ALU.add)
                B.chain(v, "dv", i)
                v.tensor_tensor(fs["EMS1"], B.bank[2][:, 128:256],
                                B.consts["c_pen_at"][:],
                                ALU.add).then_inc(*B.inc("dv"))
            B.guard_bank(2, B.tok("dv"))
            pm_tok = B.tok("dv")

            @B.block.scalar
            def _(s, pm_tok=pm_tok, h=h, fs=fs):
                B.wait(s, pm_tok)
                i = s.activation(fs["TM"], fs["EMS0"], AF.Exp,
                                 scale=1.0, bias=GMB[:, h:h + 1])
                B.chain(s, "ac", i)
                s.activation(fs["TMA"], fs["EMS1"], AF.Exp, scale=1.0,
                             bias=NGKM[:, h:h + 1]).then_inc(*B.inc("ac"))
            e_tok = B.tok("ac")

            @B.block.vector
            def _(v, e_tok=e_tok, fs=fs, h=h, sf=sigma_free[sg]):
                B.wait(v, e_tok)
                B.wait(v, gram_toks[h])
                B.wait(v, B.tok("dv"))
                if sf is not None:
                    B.wait(v, sf)
                i = v.tensor_tensor(fs["Mbf"], fs["TM"], B.bank[0][:, 0:128],
                                    ALU.mult)
                B.chain(v, "dv", i)
                v.tensor_tensor(fs["Atbf"], fs["TMA"], B.bank[1][:, 0:128],
                                ALU.mult).then_inc(*B.inc("dv"))
            B.guard_bank(0, B.tok("dv"))
            B.guard_bank(1, B.tok("dv"))
            tm_free[sg] = B.tok("dv")
            et_toks[h] = B.tok("dv")

        def emit_front_b(h):
            sg = h % 2
            fs = fsets[sg]

            @B.block.tensor
            def _(t, et_tok=et_toks[h], fs=fs):
                B.wait_bank(t, 7)
                B.wait(t, et_tok)
                t.transpose(B.bank_bf[:, 0:128], fs["Mbf"], Ibf[:]) \
                    .then_inc(*B.inc("pe"))
            mt_tok = B.tok("pe")

            @B.block.vector
            def _(v, mt_tok=mt_tok, fs=fs):
                B.wait(v, mt_tok)
                B.wait(v, B.tok("dv"))
                v.tensor_copy(fs["Mtbf"], B.bank_bf[:, 0:128]) \
                    .then_inc(*B.inc("dv"))
            B.guard_bank(7, B.tok("dv"))
            et2_toks[h] = B.tok("dv")

        def emit_back(h):
            nonlocal_state = {}
            sg = h % 2
            fs = fsets[sg]
            hp, j, base, grp, slot, scol = head_params(h)
            kfr = A.QKV[4 + j][base:base + 64, csl]
            qfr = A.QKV[j][base:base + 64, csl]
            if grp == 0:
                Ssl = A.S[0:64, scol:scol + 64]
            else:
                Ssl = A.Shi[64:128, 64 * slot:64 * slot + 64]

            zlist = [fs["Mtbf"]] + A.Zs[0:6]
            plist = [fs["Mbf"], A.Pb[0], A.Pb[1], A.Pb[0], A.Pb[1], A.Pb[0]]
            lev_tok = et2_toks[h]
            for jj in range(6):
                @B.block.tensor
                def _(t, jj=jj, lev_tok=lev_tok):
                    B.wait_bank(t, 3)
                    if jj < 5:
                        B.wait_bank(t, 4)
                    B.wait(t, lev_tok)
                    t.matmul(B.bank[3][:, 0:128], plist[jj][:], zlist[jj][:],
                             start=True, stop=True).then_inc(*B.inc("pe"))
                    if jj < 5:
                        t.matmul(B.bank[4][:, 0:128], zlist[jj][:], plist[jj][:],
                                 start=True, stop=True).then_inc(*B.inc("pe"))
                zt = B.tok("pe")

                @B.block.vector
                def _(v, jj=jj, zt=zt):
                    B.wait(v, zt)
                    B.wait(v, B.tok("dv"))
                    i = v.tensor_copy(zlist[jj + 1][:], B.bank[3][:, 0:128])
                    if jj < 5:
                        B.chain(v, "dv", i)
                        i = v.tensor_copy(plist[jj + 1][:], B.bank[4][:, 0:128])
                    i.then_inc(*B.inc("dv"))
                B.guard_bank(3, B.tok("dv"))
                if jj < 5:
                    B.guard_bank(4, B.tok("dv"))
                lev_tok = B.tok("dv")
            z_tok = B.tok("dv")

            @B.block.tensor
            def _(t, kfr=kfr, qfr=qfr, Ssl=Ssl, s_tok=s_tok):
                B.wait_bank(t, 3)
                B.wait_bank(t, 4)
                if s_tok is not None:
                    for st_ in s_tok:
                        B.wait(t, st_)
                t.matmul(B.bank[3][:, 0:64], kfr, Ssl, start=True,
                         stop=True).then_inc(*B.inc("pe"))
                t.matmul(B.bank[4][:, 0:64], qfr, Ssl, start=True,
                         stop=True).then_inc(*B.inc("pe"))
            ks_tok = B.tok("pe")

            @B.block.vector
            def _(v, h=h, ks_tok=ks_tok):
                B.wait(v, ks_tok)
                B.wait(v, B.tok("dv"))
                i = v.tensor_scalar(A.t64[:], B.bank[3][:, 0:64], EK[:, h:h + 1],
                                    None, op0=ALU.mult)
                B.chain(v, "dv", i)
                i = v.tensor_tensor(A.Rf[:], A.vT[:, 64 * h:64 * (h + 1)],
                                    A.t64[:], ALU.subtract)
                B.chain(v, "dv", i)
                v.tensor_scalar(A.Xap[0][:], A.Rf[:], BT[:, h:h + 1], None,
                                op0=ALU.mult).then_inc(*B.inc("dv"))
            B.guard_bank(3, B.tok("dv"))
            r_tok = B.tok("dv")

            xidx = 0
            for jj in range(6, -1, -1):
                @B.block.tensor
                def _(t, jj=jj, xidx=xidx, r_tok=r_tok):
                    B.wait_bank(t, 3)
                    if jj == 6:
                        B.wait(t, r_tok)
                        B.wait(t, z_tok)
                    t.matmul(B.bank[3][:, 0:64], zlist[jj][:], A.Xap[xidx][:],
                             start=True, stop=True).then_inc(*B.inc("pe"))
                ap_tok = B.tok("pe")

                @B.block.vector
                def _(v, jj=jj, xidx=xidx, ap_tok=ap_tok):
                    B.wait(v, ap_tok)
                    B.wait(v, B.tok("dv"))
                    op = ALU.subtract if jj == 0 else ALU.add
                    v.tensor_tensor(A.Xap[1 - xidx][:], A.Xap[xidx][:],
                                    B.bank[3][:, 0:64], op).then_inc(*B.inc("dv"))
                B.guard_bank(3, B.tok("dv"))
                xidx = 1 - xidx
            delta_tok = B.tok("dv")
            Xfin = A.Xap[xidx]

            @B.block.tensor
            def _(t, delta_tok=delta_tok, fs=fs):
                B.wait_bank(t, 3)
                B.wait(t, delta_tok)
                t.matmul(B.bank[3][:, 0:64], fs["Atbf"], Xfin[:],
                         start=True, stop=True).then_inc(*B.inc("pe"))
            o2_tok = B.tok("pe")
            sigma_free[sg] = o2_tok

            @B.block.vector
            def _(v, h=h, o2_tok=o2_tok, hp=hp):
                B.wait(v, o2_tok)
                B.wait(v, B.tok("dv"))
                i = v.tensor_scalar(A.t64[:], B.bank[4][:, 0:64], EO[:, h:h + 1],
                                    None, op0=ALU.mult)
                B.chain(v, "dv", i)
                i = v.tensor_tensor(A.OT[:, 64 * h:64 * (h + 1)], A.t64[:],
                                    B.bank[3][:, 0:64], ALU.add)
                B.chain(v, "dv", i)
                v.tensor_scalar(A.kp[:], A.kT[:, 64 * hp:64 * (hp + 1)],
                                EDEC[:, h:h + 1], None,
                                op0=ALU.mult).then_inc(*B.inc("dv"))
            B.guard_bank(3, B.tok("dv"))
            B.guard_bank(4, B.tok("dv"))
            kp_tok = B.tok("dv")

            sbank = 5 if scol < 512 else 6
            soff = scol % 512
            first_sb = sbank not in seen_sbank
            seen_sbank.add(sbank)

            @B.block.tensor
            def _(t, sbank=sbank, soff=soff, kp_tok=kp_tok, first_sb=first_sb):
                if first_sb:
                    B.wait_bank(t, sbank)
                B.wait(t, kp_tok)
                t.matmul(B.bank[sbank][0:64, soff:soff + 64],
                         A.kp[:], Xfin[:], start=True,
                         stop=True).then_inc(*B.inc("pe"))

        seen_sbank = set()
        emit_front(0)
        emit_front_b(0)
        for h in range(16):
            if h + 1 < 16:
                emit_front(h + 1)
            emit_back(h)
            if h + 1 < 16:
                emit_front_b(h + 1)
        head_done = B.tok("pe")

        @B.block.vector
        def _(v, head_done=head_done):
            B.wait(v, head_done)
            B.wait(v, B.tok("dv"))
            S3 = A.S[:].rearrange("p (h k) -> p h k", k=64)
            i = v.tensor_tensor(S3, A.S[:].bitcast(F32)
                                .rearrange("p (h k) -> p h k", k=64),
                                A.EGB[:].unsqueeze(2).broadcast_to([64, 16, 64]),
                                ALU.mult)
            B.chain(v, "dv", i)
            i = v.tensor_tensor(A.S[0:64, 0:512], A.S[0:64, 0:512].bitcast(F32),
                                B.bank[5][0:64, 0:512], ALU.add)
            B.chain(v, "dv", i)
            v.tensor_tensor(A.S[0:64, 512:1024], A.S[0:64, 512:1024].bitcast(F32),
                            B.bank[6][0:64, 0:512], ALU.add).then_inc(*B.inc("dv"))
        B.guard_bank(5, B.tok("dv"))
        B.guard_bank(6, B.tok("dv"))
        supd_tok = B.tok("dv")

        @B.block.gpsimd
        def _(g, supd_tok=supd_tok):
            B.wait(g, supd_tok)
            g.dma_start(out=A.Shi[64:128, 0:512],
                        in_=A.S[0:64, 512:1024]).then_inc(*B.inc("ld", 16))
        s_tok = [supd_tok, B.tok("ld")]

        @B.block.vector
        def _(v):
            B.wait(v, B.tok("dv"))
            csc = B.CSC[:, 0:512]
            i = v.tensor_tensor(csc, A.OT[:, 0:512], A.OT[:, 0:512], ALU.mult)
            B.chain(v, "dv", i)
            i = v.tensor_reduce(A.RSO[:, 0:8],
                                csc.rearrange("p (h k) -> p h k", k=64),
                                axis=AX.X, op=ALU.add)
            B.chain(v, "dv", i)
            i = v.tensor_tensor(csc, A.OT[:, 512:1024], A.OT[:, 512:1024], ALU.mult)
            B.chain(v, "dv", i)
            v.tensor_reduce(A.RSO[:, 8:16],
                            csc.rearrange("p (h k) -> p h k", k=64),
                            axis=AX.X, op=ALU.add).then_inc(*B.inc("dv"))
        rs_tok = B.tok("dv")

        @B.block.scalar
        def _(s, rs_tok=rs_tok):
            B.wait(s, rs_tok)
            i = s.activation(A.RSO[:], A.RSO[:], AF.Ln, scale=1.0 / 64.0, bias=eps2)
            B.chain(s, "ac", i)
            s.activation(A.RSO[:], A.RSO[:], AF.Exp, scale=-0.5).then_inc(*B.inc("ac"))
        rso_tok = B.tok("ac")

        @B.block.vector
        def _(v, rso_tok=rso_tok):
            B.wait(v, rso_tok)
            B.wait(v, B.tok("dv"))
            O3 = A.OT[:].rearrange("p (h k) -> p h k", k=64)
            v.tensor_tensor(O3, O3, A.RSO[:].unsqueeze(2).broadcast_to([128, 16, 64]),
                            ALU.mult).then_inc(*B.inc("dv"))
        yn_tok = B.tok("dv")

        for ch in range(8):
            @B.block.tensor
            def _(t, ch=ch, yn_tok=yn_tok):
                B.wait_bank(t, 2)
                B.wait(t, yn_tok)
                t.transpose(B.bank[2][:, 0:128], A.OT[:, 128 * ch:128 * (ch + 1)],
                            If32[:]).then_inc(*B.inc("pe"))
            ytr_tok = B.tok("pe")

            @B.block.vector
            def _(v, ch=ch, csl=csl, ytr_tok=ytr_tok):
                B.wait(v, ytr_tok)
                B.wait(v, B.tok("dv"))
                v.scalar_tensor_tensor(A.YNF[ch][:, csl], B.bank[2][:, 0:128],
                                       A.nwc[:, ch:ch + 1],
                                       A.SZ1[ch][:, csl], op0=ALU.mult,
                                       op1=ALU.mult).then_inc(*B.inc("dv"))
            B.guard_bank(2, B.tok("dv"))
        ynf_toks.append(B.tok("dv"))
    return ynf_toks[-1], s_tok


def emit_l1attn_full(B, A, ws, xn_ch, xn_toks, d, tile, s_tok_in):
    qkv_toks_pre = emit_l1attn(B, A, ws, xn_ch, xn_toks, None, d, tile)
    gg_tok = A._gg_tok
    bet_tok = A._bet_tok
    ynf_tok, s_tok = emit_l1attn_gdn(B, A, qkv_toks_pre, gg_tok, bet_tok, s_tok_in)

    y_toks = []
    for o in range(NCH):
        ws.load(d["W1oT"][o])

        @B.block.tensor
        def _(t, o=o, ynf_tok=ynf_tok):
            slab = ws.use(t)
            bk = o % 2
            B.wait_bank(t, bk)
            if o == 0:
                B.wait(t, ynf_tok)
            for ch in range(8):
                mm = t.matmul(B.bank[bk][:, 0:TT], slab[:, ch * 128:(ch + 1) * 128],
                              A.YNF[ch][:], start=(ch == 0), stop=(ch == 7),
                              skip_group_check=True)
            mm.then_inc(*B.inc("pe"))
            ws.done(B.tok("pe"))
        mm_tok = B.tok("pe")

        @B.block.scalar
        def _(s, o=o, mm_tok=mm_tok):
            B.wait(s, mm_tok)
            s.copy(A.PRE[o][:], B.bank[o % 2][:, 0:TT]).then_inc(*B.inc("ac"))
        B.guard_bank(o % 2, B.tok("ac"))
        y_toks.append(B.tok("ac"))
    return [p[:] for p in A.PRE], y_toks, s_tok
from contextlib import ExitStack
from concourse.bass_utils import run_bass_kernel_spmd

CN_ALL = ("c_ones_row", "c_ones_col", "c_eps", "c_ident", "c_ident_bfr",
          "c_tri_incl", "c_pen_m", "c_pen_mt", "c_pen_at",
          "c_onescol_f32", "c_onesrow_f32", "c_onek_bf", "c_zeros_r",
          "c_l0exp")

NTILES = 4


def _weights_l0a(z):
    return {
        "WxT": tile_lhsT_by_o(np.ascontiguousarray(z["L0_attn_W_in_x"].T)),
        "WzT": tile_lhsT_by_o(np.ascontiguousarray(z["L0_attn_W_in_z"].T)),
        "WpT": tile_lhsT_by_o(np.ascontiguousarray(z["L0_attn_W_param"].T)),
        "WoT": tile_lhsT_by_o(np.ascontiguousarray(z["L0_attn_W_out"].T)),
        "l0_wdt": np.ascontiguousarray(z["L0_attn_W_dt"].T),
        "l0_cw": np.ascontiguousarray(
            z["L0_attn_conv_w"][:, 0, :].reshape(4, 128, 4).transpose(1, 0, 2)
            .reshape(128, 16)),
        "l0_cb": np.ascontiguousarray(z["L0_attn_conv_b"].reshape(4, 128).T),
        "l0_alog": np.ascontiguousarray(z["L0_attn_A_log"].reshape(512)
                                        .reshape(4, 128).T),
        "l0_dcol": np.ascontiguousarray(z["L0_attn_D"].reshape(4, 128).T),
        "l0_bdt": np.ascontiguousarray(z["L0_attn_b_dt"].reshape(32, 1)),
    }


W_L0A_DT = {"WxT": F32R, "WzT": F32R, "WpT": F32R, "WoT": F32R,
            "l0_wdt": F32R, "l0_cw": F32, "l0_cb": F32, "l0_alog": F32,
            "l0_dcol": F32, "l0_bdt": F32}
W_L0A_SHP = {"WxT": [4, 128, 2048], "WzT": [4, 128, 2048], "WpT": [9, 128, 512],
             "WoT": [16, 128, 512], "l0_wdt": [128, 32], "l0_cw": [128, 16],
             "l0_cb": [128, 4], "l0_alog": [128, 4], "l0_dcol": [128, 4],
             "l0_bdt": [32, 1]}


def _weights_l0f(z):
    return {
        "Wg": tile_lhsT_by_o(np.ascontiguousarray(z["L0_ffn_sh_gate"].T)),
        "Wu": tile_lhsT_by_o(np.ascontiguousarray(z["L0_ffn_sh_up"].T)),
        "Wd": tile_lhsT_by_k(np.ascontiguousarray(z["L0_ffn_sh_down"].T)),
        "Rg": tile_lhsT_by_o(np.ascontiguousarray(z["L0_ffn_r_gate"].T)),
        "Ru": tile_lhsT_by_o(np.ascontiguousarray(z["L0_ffn_r_up"].T)),
        "Rd": tile_lhsT_by_k(np.ascontiguousarray(z["L0_ffn_r_down"].T)),
        "l0f_wrt": np.ascontiguousarray(
            z["L0_ffn_router_weight"].T.reshape(16, 128, 8)
            .transpose(1, 0, 2).reshape(128, 128)),
    }


W_L0F_SHP = {"Wg": [64, 128, 2048], "Wu": [64, 128, 2048], "Wd": [64, 128, 2048],
             "Rg": [8, 128, 2048], "Ru": [8, 128, 2048], "Rd": [8, 128, 2048],
             "l0f_wrt": [128, 128]}


def _weights_l1a(z):
    return {
        "W1qkvT": tile_lhsT_by_o(np.ascontiguousarray(z["L1_attn_W_qkv"].T)),
        "W1zT": tile_lhsT_by_o(np.ascontiguousarray(z["L1_attn_W_z"].T)),
        "W1oT": tile_lhsT_by_o(np.ascontiguousarray(z["L1_attn_W_out"].T)),
        "a1d_wbt": np.ascontiguousarray(
            z["L1_attn_W_b"].T.reshape(16, 128, 16).transpose(1, 0, 2).reshape(128, 256)),
        "a1d_wat": np.ascontiguousarray(
            z["L1_attn_W_a"].T.reshape(16, 128, 16).transpose(1, 0, 2).reshape(128, 256)),
        "a1d_cw1": np.ascontiguousarray(
            z["L1_attn_conv_w"][:, 0, :].reshape(16, 128, 4).transpose(1, 0, 2)
            .reshape(128, 64)),
        "a1d_alog": np.ascontiguousarray(z["L1_attn_A_log"].reshape(16, 1)),
        "a1d_dtb": np.ascontiguousarray(z["L1_attn_dt_bias"].reshape(16, 1)),
        "a1d_nwc": np.ascontiguousarray(
            np.tile(z["L1_attn_norm_weight"], 16).reshape(8, 128).T),
    }


W_L1A_DT = {"W1qkvT": F32R, "W1zT": F32R, "W1oT": F32R, "a1d_wbt": F32R,
            "a1d_wat": F32R, "a1d_cw1": F32, "a1d_alog": F32, "a1d_dtb": F32,
            "a1d_nwc": F32}
W_L1A_SHP = {"W1qkvT": [16, 128, 2048], "W1zT": [8, 128, 2048],
             "W1oT": [16, 128, 1024], "a1d_wbt": [128, 256],
             "a1d_wat": [128, 256], "a1d_cw1": [128, 64], "a1d_alog": [16, 1],
             "a1d_dtb": [16, 1], "a1d_nwc": [128, 8]}


def _weights_l1f(z):
    W = {"WsGU": tile_lhsT_by_o(np.ascontiguousarray(z["L1_ffn_Ws_gate_up"])),
         "WsD": tile_lhsT_by_k(np.ascontiguousarray(z["L1_ffn_Ws_down"]))}
    for e in range(8):
        W[f"WeGU{e}"] = tile_lhsT_by_o(np.ascontiguousarray(z["L1_ffn_We_gate_up"][e]))
        W[f"WeD{e}"] = tile_lhsT_by_k(np.ascontiguousarray(z["L1_ffn_We_down"][e]))
    W["l1f_wrt"] = np.ascontiguousarray(
        z["L1_ffn_W_router"].T.reshape(16, 128, 8).transpose(1, 0, 2).reshape(128, 128))
    W["l1f_wsgt"] = np.ascontiguousarray(
        z["L1_ffn_W_sgate"].T.reshape(16, 128).T)
    return W


W_L1F_SHP = {"WsGU": [88, 128, 2048], "WsD": [44, 128, 2048],
             "l1f_wrt": [128, 128], "l1f_wsgt": [128, 16]}
for _e in range(8):
    W_L1F_SHP[f"WeGU{_e}"] = [22, 128, 2048]
    W_L1F_SHP[f"WeD{_e}"] = [11, 128, 2048]


def _std_frame(nc, wshapes, wdt, consts_np, cn):
    x_in = nc.dram_tensor("x_in", [NTILES, 128, NCH * TT], F32R,
                          kind="ExternalInput").ap()
    x_out = nc.dram_tensor("x_out", [NTILES, 128, NCH * TT], F32R,
                           kind="ExternalOutput").ap()
    d = {n: nc.dram_tensor(n, wshapes[n], wdt.get(n, F32R),
                           kind="ExternalInput").ap() for n in wshapes}
    dconsts = {n: nc.dram_tensor(n, list(consts_np[n].shape), CONST_DTYPES[n],
                                 kind="ExternalInput").ap() for n in cn}
    return x_in, x_out, d, dconsts


def _frame_body(stack, nc, cn, dconsts, consts_np):
    bb = B(nc, stack)
    staged = [n for n in cn if n not in ("c_zeros_r", "c_l0exp")]
    for name in staged:
        h = bb.sb("s_" + name, list(consts_np[name].shape), CONST_DTYPES[name])
        bb.consts[name] = h

    @bb.block.gpsimd
    def _(g):
        for name in staged:
            g.dma_start(out=bb.consts[name][:], in_=dconsts[name]) \
                .then_inc(*bb.inc("ld", 16))

    xt = bb.sb("xt", [P, NCH * TT], F32R)
    xnt = bb.sb("xnt", [P, NCH * TT], F32R)
    yt = bb.sb("yt", [P, NCH * TT], F32R)
    x_ch = [xt[:, k * TT:(k + 1) * TT] for k in range(NCH)]
    xn_ch = [xnt[:, k * TT:(k + 1) * TT] for k in range(NCH)]
    y_ch = [yt[:, k * TT:(k + 1) * TT] for k in range(NCH)]
    L = LNBufs(bb)
    return bb, xt, x_ch, xn_ch, y_ch, L


_CN_BASE = ("c_ones_row", "c_ones_col", "c_eps")


def _per_tile_io(bb, xt, tile, x_in, extra_tok=None):
    @bb.block.gpsimd
    def _(g, tile=tile):
        if tile > 0:
            bb.wait(g, bb.tok("st"))
        elif extra_tok is not None:
            bb.wait(g, extra_tok)
        g.dma_start(out=xt[:, 0:NCH * TT], in_=x_in[tile]) \
            .then_inc(*bb.inc("ld", 16))
    return bb.tok("ld")


def _store_tile(bb, xt, tile, x_out, res_toks):
    @bb.block.gpsimd
    def _(g, tile=tile):
        bb.wait(g, res_toks[-1])
        g.dma_start(out=x_out[tile], in_=xt[:, 0:NCH * TT]) \
            .then_inc(*bb.inc("st", 16))


def _finish(bb):
    @bb.block.gpsimd
    def _(g):
        g.wait_ge(bb.sem["st"], bb.cnt["st"])


def build_l0a(consts_np):
    cn = _CN_BASE + ("c_l0exp",)
    nc = bass.Bass("TRN2", target_bir_lowering=False)
    x_in, x_out, d, dconsts = _std_frame(nc, W_L0A_SHP, W_L0A_DT, consts_np, cn)
    with ExitStack() as stack:
        bb, xt, x_ch, xn_ch, y_ch, L = _frame_body(stack, nc, cn, dconsts, consts_np)
        L0 = L0Bufs(bb, L)
        ws = WStream(bb)
        d = dict(d)
        d["c_l0exp_d"] = dconsts["c_l0exp"]
        ldt, dvt = emit_l0attn_consts(bb, L0, d)
        for engd in (bb.block.vector, bb.block.scalar, bb.block.tensor):
            @engd
            def _(e):
                bb.wait(e, ldt)
                bb.wait(e, dvt)
        for tile in range(NTILES):
            ld_tok = _per_tile_io(bb, xt, tile, x_in, ldt)
            xn_toks = emit_ln_xn(bb, L, x_ch, xn_ch, pre_toks=[ld_tok] * NCH)
            a_toks = emit_l0attn(bb, L0, ws, xn_ch, xn_toks, y_ch, d, tile)
            res_toks = emit_ln_residual(bb, L, y_ch, x_ch, pre_toks=a_toks)
            _store_tile(bb, xt, tile, x_out, res_toks)
        _finish(bb)
    return nc


def build_l0f(consts_np):
    cn = _CN_BASE + ("c_ident",)
    nc = bass.Bass("TRN2", target_bir_lowering=False)
    x_in, x_out, d, dconsts = _std_frame(nc, W_L0F_SHP, {}, consts_np, cn)
    with ExitStack() as stack:
        bb, xt, x_ch, xn_ch, y_ch, L = _frame_body(stack, nc, cn, dconsts, consts_np)
        ws = WStream(bb)
        sgbufs = [bb.mf(6), bb.mf(7)]
        asup = [bb.mr(i) for i in range(16)]
        R = L0FRBufs(bb, L)
        emit_l0fr_consts(bb, R, d)
        const_tok = bb.tok("ld")
        for tile in range(NTILES):
            ld_tok = _per_tile_io(bb, xt, tile, x_in, const_tok)
            xn_toks = emit_ln_xn(bb, L, x_ch, xn_ch, pre_toks=[ld_tok] * NCH)
            cb_tok = emit_l0ffn_router(bb, R, xn_ch, xn_toks)
            emit_swiglu_mlp(bb, ws, xn_ch, xn_toks, y_ch, d["Wg"], d["Wu"],
                            d["Wd"], 64, sgbufs, asup, accumulate=False)
            f_tok = emit_swiglu_mlp(bb, ws, xn_ch, xn_toks, y_ch, d["Rg"],
                                    d["Ru"], d["Rd"], 8, sgbufs, asup,
                                    accumulate=True, scale_tile=R.CB0[:],
                                    scale_tok=cb_tok)
            res_toks = emit_ln_residual(bb, L, y_ch, x_ch, pre_toks=[f_tok] * NCH)
            _store_tile(bb, xt, tile, x_out, res_toks)
        _finish(bb)
    return nc


def build_l1a(consts_np):
    cn = _CN_BASE + ("c_ident", "c_ident_bfr", "c_tri_incl", "c_pen_m",
                     "c_pen_mt", "c_pen_at", "c_onescol_f32", "c_onesrow_f32",
                     "c_onek_bf", "c_zeros_r")
    nc = bass.Bass("TRN2", target_bir_lowering=False)
    x_in, x_out, d, dconsts = _std_frame(nc, W_L1A_SHP, W_L1A_DT, consts_np, cn)
    with ExitStack() as stack:
        bb, xt, x_ch, xn_ch, y_ch, L = _frame_body(stack, nc, cn, dconsts, consts_np)
        A = L1ABufs(bb, y_ch, xn_ch)
        ws = WStream(bb)
        d2 = dict(d)
        d2["c_zeros_r"] = dconsts["c_zeros_r"]
        ldt, dvt = emit_l1attn_consts(bb, A, d2)
        for engd in (bb.block.vector, bb.block.scalar, bb.block.tensor):
            @engd
            def _(e):
                bb.wait(e, ldt)
                bb.wait(e, dvt)
        s_tok = [dvt]
        for tile in range(NTILES):
            ld_tok = _per_tile_io(bb, xt, tile, x_in, ldt)
            xn_toks = emit_ln_xn(bb, L, x_ch, xn_ch, pre_toks=[ld_tok] * NCH)
            y_list, y_toks, s_tok = emit_l1attn_full(bb, A, ws, xn_ch, xn_toks,
                                                     d2, tile, s_tok)
            res_toks = emit_ln_residual(bb, L, y_list, x_ch, pre_toks=y_toks)
            _store_tile(bb, xt, tile, x_out, res_toks)
        _finish(bb)
    return nc


def build_l1f(consts_np):
    cn = _CN_BASE + ("c_ident",)
    nc = bass.Bass("TRN2", target_bir_lowering=False)
    x_in, x_out, d, dconsts = _std_frame(nc, W_L1F_SHP, {}, consts_np, cn)
    with ExitStack() as stack:
        bb, xt, x_ch, xn_ch, y_ch, L = _frame_body(stack, nc, cn, dconsts, consts_np)
        F = L1FBufs(bb, L)
        ws = WStream(bb)
        sgbufs = [bb.mf(6), bb.mf(7)]
        asup = [bb.mr(i) for i in range(16)]
        ldt = emit_l1ffn_consts(bb, F, d)
        for engd in (bb.block.vector, bb.block.scalar, bb.block.tensor):
            @engd
            def _(e):
                bb.wait(e, ldt)
        for tile in range(NTILES):
            ld_tok = _per_tile_io(bb, xt, tile, x_in, ldt)
            xn_toks = emit_ln_xn(bb, L, x_ch, xn_ch, pre_toks=[ld_tok] * NCH)
            f_tok = emit_l1ffn(bb, F, ws, xn_ch, xn_toks, y_ch, d, sgbufs,
                               asup, tile)
            res_toks = emit_ln_residual(bb, L, y_ch, x_ch, pre_toks=[f_tok] * NCH)
            _store_tile(bb, xt, tile, x_out, res_toks)
        _finish(bb)
    return nc


def _to_tiles(xb):
    xf = np.ascontiguousarray(xb.T)
    return np.stack([
        xf[:, t * TT:(t + 1) * TT].reshape(NCH, 128, TT).transpose(1, 0, 2)
        .reshape(128, NCH * TT) for t in range(NTILES)])


def _from_tiles(out):
    return np.concatenate(
        [out[t].reshape(128, NCH, TT).transpose(1, 0, 2).reshape(2048, TT)
         for t in range(NTILES)], axis=1).T


def kernel_unfused(**inputs):
    z = inputs
    consts_np = host_consts()
    x = np.asarray(z["L0_attn_hidden_states"], dtype=np.float32)
    cores = list(range(8))
    xs = [_to_tiles(x[b]) for b in range(8)]

    launches = [
        (build_l0a, _weights_l0a, _CN_BASE + ("c_l0exp",)),
        (build_l0f, _weights_l0f, _CN_BASE),
        (build_l1a, _weights_l1a, _CN_BASE + ("c_ident", "c_ident_bfr",
                                              "c_tri_incl", "c_pen_m",
                                              "c_pen_mt", "c_pen_at",
                                              "c_onescol_f32", "c_onesrow_f32",
                                              "c_onek_bf", "c_zeros_r")),
        (build_l1f, _weights_l1f, _CN_BASE + ("c_ident",)),
    ]
    for build_fn, wfn, cn in launches:
        W = wfn(z)
        nc = build_fn(consts_np)
        base = dict(W)
        for n in cn:
            base[n] = consts_np[n]
        in_maps = [dict(base, x_in=xs[b]) for b in range(8)]
        res = run_bass_kernel_spmd(nc, in_maps, core_ids=cores)
        xs = [np.asarray(res.results[b]["x_out"]) for b in range(8)]

    out = np.stack([_from_tiles(xs[b]) for b in range(8)]).astype(np.float32)
    return out



def build_fused(consts_np):
    cn = ("c_ones_row", "c_ones_col", "c_eps", "c_ident", "c_ident_bfr",
          "c_tri_incl", "c_pen_m", "c_pen_mt", "c_pen_at", "c_onescol_f32",
          "c_onesrow_f32", "c_onek_bf", "c_zeros_r", "c_l0exp")
    nc = bass.Bass("TRN2", target_bir_lowering=False)
    wshapes = {}
    wshapes.update(W_L0A_SHP)
    wshapes.update(W_L0F_SHP)
    wshapes.update(W_L1A_SHP)
    wshapes.update(W_L1F_SHP)
    wdt = {}
    wdt.update(W_L0A_DT)
    wdt.update(W_L1A_DT)
    x_in, x_out, d, dconsts = _std_frame(nc, wshapes, wdt, consts_np, cn)
    with ExitStack() as stack:
        bb, xt, x_ch, xn_ch, y_ch, L = _frame_body(stack, nc, cn, dconsts, consts_np)
        L0 = L0Bufs(bb, L)
        A = L1ABufs(bb, y_ch, xn_ch)
        F = L1FBufs(bb, L)
        R = L0FRBufs(bb, L)
        ws = WStream(bb)
        sgbufs = [bb.mf(6), bb.mf(7)]
        asup = [bb.mr(i) for i in range(16)]
        d2 = dict(d)
        d2["c_zeros_r"] = dconsts["c_zeros_r"]
        d2["c_l0exp_d"] = dconsts["c_l0exp"]
        emit_l0fr_consts(bb, R, d2)
        ldt0, dvt0 = emit_l0attn_consts(bb, L0, d2)
        ldt1, dvt1 = emit_l1attn_consts(bb, A, d2)
        ldt2 = emit_l1ffn_consts(bb, F, d2)
        for engd in (bb.block.vector, bb.block.scalar, bb.block.tensor):
            @engd
            def _(e):
                bb.wait(e, ldt2)
                bb.wait(e, dvt0)
                bb.wait(e, dvt1)
        s_tok = [dvt1]
        for tile in range(NTILES):
            ld_tok = _per_tile_io(bb, xt, tile, x_in, ldt2)
            xn_toks = emit_ln_xn(bb, L, x_ch, xn_ch, pre_toks=[ld_tok] * NCH)
            a_toks = emit_l0attn(bb, L0, ws, xn_ch, xn_toks, y_ch, d2, tile)
            res_toks = emit_ln_residual(bb, L, y_ch, x_ch, pre_toks=a_toks)
            xn_toks = emit_ln_xn(bb, L, x_ch, xn_ch, pre_toks=res_toks)
            cb_tok = emit_l0ffn_router(bb, R, xn_ch, xn_toks)
            emit_swiglu_mlp(bb, ws, xn_ch, xn_toks, y_ch, d2["Wg"], d2["Wu"],
                            d2["Wd"], 64, sgbufs, asup, accumulate=False)
            f_tok = emit_swiglu_mlp(bb, ws, xn_ch, xn_toks, y_ch, d2["Rg"],
                                    d2["Ru"], d2["Rd"], 8, sgbufs, asup,
                                    accumulate=True, scale_tile=R.CB0[:],
                                    scale_tok=cb_tok)
            res_toks = emit_ln_residual(bb, L, y_ch, x_ch, pre_toks=[f_tok] * NCH)
            xn_toks = emit_ln_xn(bb, L, x_ch, xn_ch, pre_toks=res_toks)
            y_list, y_toks, s_tok = emit_l1attn_full(bb, A, ws, xn_ch, xn_toks,
                                                     d2, tile, s_tok)
            res_toks = emit_ln_residual(bb, L, y_list, x_ch, pre_toks=y_toks)
            xn_toks = emit_ln_xn(bb, L, x_ch, xn_ch, pre_toks=res_toks)
            f_tok = emit_l1ffn(bb, F, ws, xn_ch, xn_toks, y_ch, d2, sgbufs,
                               asup, tile)
            res_toks = emit_ln_residual(bb, L, y_ch, x_ch, pre_toks=[f_tok] * NCH)
            _store_tile(bb, xt, tile, x_out, res_toks)
        _finish(bb)
    return nc


def kernel(**inputs):
    z = inputs
    consts_np = host_consts()
    x = np.asarray(z["L0_attn_hidden_states"], dtype=np.float32)
    xs = [_to_tiles(x[b]) for b in range(8)]
    base = {}
    for wfn in (_weights_l0a, _weights_l0f, _weights_l1a, _weights_l1f):
        base.update(wfn(z))
    base.update({n: consts_np[n] for n in
                 ("c_ones_row", "c_ones_col", "c_eps", "c_ident", "c_ident_bfr",
                  "c_tri_incl", "c_pen_m", "c_pen_mt", "c_pen_at",
                  "c_onescol_f32", "c_onesrow_f32", "c_onek_bf", "c_zeros_r",
                  "c_l0exp")})
    nc = build_fused(consts_np)
    in_maps = [dict(base, x_in=xs[b]) for b in range(8)]
    res = run_bass_kernel_spmd(nc, in_maps, core_ids=list(range(8)))
    out = np.stack([_from_tiles(np.asarray(res.results[b]["x_out"]))
                    for b in range(8)]).astype(np.float32)
    return out


# version 13
# speedup vs baseline: 1.1417x; 1.1417x over previous
import sys
sys.path.insert(0, "/opt/trn_rl_repo")
import numpy as np
import concourse.bass as bass
import concourse.mybir as mybir

F32 = mybir.dt.float32
F32R = mybir.dt.float32r
BF16 = mybir.dt.bfloat16
AF = mybir.ActivationFunctionType
ALU = mybir.AluOpType
AX = mybir.AxisListType

P = 128
D = 2048
NCH = 16
TT = 512
LN_EPS = 1e-5



def tile_lhsT_by_o(WT):
    K, O = WT.shape
    Kc, Oc = K // 128, O // 128
    return np.ascontiguousarray(
        WT.reshape(Kc, 128, Oc, 128).transpose(2, 1, 0, 3).reshape(Oc, 128, Kc * 128))


def tile_lhsT_by_k(WT):
    K = WT.shape[0]
    return np.ascontiguousarray(WT.reshape(K // 128, 128, WT.shape[1]))


def host_consts():
    import ml_dtypes
    i = np.arange(128)
    E = np.zeros((32, 512), np.float32)
    E[np.arange(512) // 16, np.arange(512)] = 1.0
    return {
        "c_ones_row": np.ones((1, 128), np.float32),
        "c_ones_col": np.ones((128, 1), np.float32),
        "c_ident": np.eye(128, dtype=np.float32),
        "c_ident_bf": np.eye(128, dtype=np.float32),
        "c_tri_incl": (i[:, None] <= i[None, :]).astype(np.float32),
        "c_pen_m": (-1e4 * (i[:, None] <= i[None, :])).astype(ml_dtypes.bfloat16),
        "c_pen_at": (-1e4 * (i[:, None] > i[None, :])).astype(ml_dtypes.bfloat16),
        "c_l0exp": E,
        "c_eps": np.tile(np.array([1e-5, 1e-6, 1.0, 0.0], np.float32), (128, 1)),
        "c_onescol_f32": np.ones((128, 1), np.float32),
        "c_onesrow_f32": np.ones((1, 128), np.float32),
        "c_ident_bfr": np.eye(128).astype(ml_dtypes.bfloat16),
        "c_onek_bf": np.ones((128, 128)).astype(ml_dtypes.bfloat16),
        "c_zeros_r": np.zeros((128, 512), np.float32),
    }


CONST_DTYPES = {
    "c_ones_row": F32R, "c_ones_col": F32R, "c_ident": F32,
    "c_ident_bf": F32, "c_tri_incl": F32, "c_pen_m": BF16,
    "c_pen_at": BF16, "c_l0exp": F32R, "c_eps": F32,
    "c_onescol_f32": F32, "c_onesrow_f32": F32, "c_ident_bfr": BF16,
    "c_onek_bf": BF16, "c_zeros_r": F32R,
}



class B:
    def __init__(self, nc, stack):
        self.nc = nc
        self.stack = stack
        self.sem = {}
        self.cnt = {}
        for name in ("ld", "wld", "wcred", "pe", "dv", "ac", "po", "st"):
            self.sem[name] = stack.enter_context(nc.semaphore(name))
            self.cnt[name] = 0

        self.block = stack.enter_context(nc.Block())
        self.MF32 = stack.enter_context(nc.sbuf_tensor("MF32", [128, 16 * TT], F32))
        self.MR32 = stack.enter_context(nc.sbuf_tensor("MR32", [128, 16 * TT], F32R))
        self.MBF = stack.enter_context(nc.sbuf_tensor("MBF", [128, 5696], BF16))
        self.bank = [stack.enter_context(nc.psum_tensor(f"bank{i}", [128, 512], F32))
                     for i in range(7)]
        self.bank_bf = stack.enter_context(nc.psum_tensor("bankbf", [128, 512], BF16))
        self.bank_guard = [None] * 8
        self.consts = {}
        self.SPR = self.MF32[0:32, 6 * TT:8 * TT]

    def sb(self, name, shape, dtype=F32):
        return self.stack.enter_context(self.nc.sbuf_tensor(name, shape, dtype))

    def mf(self, s0, w=1):
        return self.MF32[:, s0 * TT:(s0 + w) * TT]

    @property
    def CSC(self):
        return self.MF32[:, 12 * TT:12 * TT + TT + 16]

    def mr(self, s0, w=1):
        return self.MR32[:, s0 * TT:(s0 + w) * TT]

    def new_sem(self, name):
        self.sem[name] = self.stack.enter_context(self.nc.semaphore(name))
        self.cnt[name] = 0
        return name

    def inc(self, name, n=1):
        self.cnt[name] += n
        return self.sem[name], n

    def tok(self, name):
        return (self.sem[name], self.cnt[name])

    def wait(self, eng, token):
        sem, val = token
        if val > 0:
            eng.wait_ge(sem, val)

    def chain(self, eng, name, instr):
        instr.then_inc(*self.inc(name))
        eng.wait_ge(self.sem[name], self.cnt[name])

    def wait_bank(self, eng, bi):
        t = self.bank_guard[bi]
        if t is not None:
            self.wait(eng, t)
            self.bank_guard[bi] = None

    def guard_bank(self, bi, token):
        self.bank_guard[bi] = token


class WStream:

    def __init__(self, B, name="wbuf", width=2048, nbuf=2):
        self.B = B
        self.width = width
        self.nbuf = nbuf
        self.bufs = [B.sb(f"{name}{i}", [128, width], BF16) for i in range(nbuf)]
        self.sems = [B.new_sem(f"{name}_s{i}") for i in range(nbuf)]
        self.gsems = [B.new_sem(f"{name}_g{i}") for i in range(nbuf)]
        self.free_tok = [None] * nbuf
        self.n_loaded = 0
        self.n_used = 0

    def load(self, src_ap):
        bb = self.B
        i = self.n_loaded
        self.n_loaded += 1
        w = src_ap.shape[-1]
        buf = self.bufs[i % self.nbuf]
        sname = self.sems[i % self.nbuf]
        free_tok = self.free_tok[i % self.nbuf]

        def body(q):
            if free_tok is not None:
                bb.wait(q, free_tok)
            q.dma_start(out=buf[:, 0:w], in_=src_ap).then_inc(*bb.inc(sname, 16))

        if i % 2 == 0:
            @bb.block.sync
            def _(sync):
                body(sync)
        else:
            @bb.block.gpsimd
            def _(g):
                body(g)
        return None

    def load_view(self, src_ap, w):
        bb = self.B
        i = self.n_loaded
        self.n_loaded += 1
        buf = self.bufs[i % self.nbuf]
        sname = self.sems[i % self.nbuf]
        gname = self.gsems[i % self.nbuf]
        free_tok = self.free_tok[i % self.nbuf]
        a = src_ap.shape[1]
        dst = buf[:, 0:w].rearrange("p (a q) -> p a q", a=a)

        def body(q):
            if free_tok is not None:
                bb.wait(q, free_tok)
            q.dma_start(out=dst, in_=src_ap).then_inc(*bb.inc(sname, 16))

        if i % 2 == 0:
            @bb.block.sync
            def _(sync):
                body(sync)
        else:
            @bb.block.gpsimd
            def _(g):
                body(g)

    def use(self, eng):
        i = self.n_used
        eng.wait_ge(self.B.sem[self.sems[i % self.nbuf]],
                    16 * (i // self.nbuf + 1))
        return self.bufs[i % self.nbuf]

    def done(self, token):
        self.free_tok[self.n_used % self.nbuf] = token
        self.n_used += 1



class LNBufs:

    def __init__(self, B):
        self.sq = [B.mr(12 + i) for i in range(4)]
        self.rows = B.MR32[0:1, 12 * TT:14 * TT]
        self.m_row = B.MF32[0:1, 2 * TT + 384:2 * TT + 896]
        self.q_row = B.MF32[0:1, 2 * TT + 896:2 * TT + 1408]
        self.rsb = B.mf(13).bitcast(F32)
        self.mrb = B.mf(14).bitcast(F32)
        self.t1 = B.mf(15).bitcast(F32)


def emit_ln_core(B, L, x_ch, tag, pre_toks=None):
    nc = B.nc
    ones_c = B.consts["c_ones_col"]
    bsum, bsq = B.bank[4], B.bank[5]
    pe0 = B.cnt["pe"]

    for k in range(NCH):
        slab = L.sq[k % 4]

        @B.block.vector
        def _(v, k=k, slab=slab):
            if pre_toks is not None:
                B.wait(v, pre_toks[k])
            if k >= 4:
                v.wait_ge(B.sem["pe"], pe0 + 2 * (k - 4) + 2)
            v.tensor_tensor(slab[:], x_ch[k].bitcast(F32), x_ch[k].bitcast(F32),
                            ALU.mult).then_inc(*B.inc("dv"))
        sq_tok = B.tok("dv")

        @B.block.tensor
        def _(t, k=k, slab=slab, sq_tok=sq_tok):
            if k == 0:
                B.wait_bank(t, 4)
                B.wait_bank(t, 5)
            B.wait(t, sq_tok)
            t.matmul(bsum[0:1, 0:TT], ones_c[:], x_ch[k],
                     start=(k == 0), stop=(k == NCH - 1),
                     skip_group_check=True).then_inc(*B.inc("pe"))
            t.matmul(bsq[0:1, 0:TT], ones_c[:], slab[:],
                     start=(k == 0), stop=(k == NCH - 1),
                     skip_group_check=True).then_inc(*B.inc("pe"))
    stats_tok = B.tok("pe")

    m_row = L.m_row
    q_row = L.q_row
    rs_row = L.rows[0:1, 0:TT]
    mr_row = L.rows[0:1, TT:2 * TT]

    @B.block.vector
    def _(v):
        B.wait(v, stats_tok)
        i1 = v.tensor_scalar(m_row, bsum[0:1, 0:TT], 1.0 / D, None, op0=ALU.mult)
        i2 = v.tensor_scalar(q_row, bsq[0:1, 0:TT], 1.0 / D, None, op0=ALU.mult)
        B.chain(v, "dv", i2)
        i3 = v.tensor_tensor(rs_row, m_row.bitcast(F32), m_row.bitcast(F32), ALU.mult)
        B.chain(v, "dv", i3)
        v.tensor_tensor(q_row, q_row.bitcast(F32), rs_row.bitcast(F32),
                        ALU.subtract).then_inc(*B.inc("dv"))
    dv_tok = B.tok("dv")

    @B.block.scalar
    def _(s):
        B.wait(s, dv_tok)
        i1 = s.activation(q_row, q_row.bitcast(F32), AF.Ln,
                          bias=B.consts["c_eps"][0:1, 0:1])
        B.chain(s, "ac", i1)
        s.activation(rs_row, q_row.bitcast(F32), AF.Exp, scale=-0.5).then_inc(*B.inc("ac"))
    ac2 = B.tok("ac")

    @B.block.vector
    def _(v):
        B.wait(v, ac2)
        v.tensor_tensor(mr_row, m_row.bitcast(F32), rs_row.bitcast(F32),
                        ALU.mult).then_inc(*B.inc("dv"))
    dv2 = B.tok("dv")

    ones_r = B.consts["c_ones_row"]

    @B.block.tensor
    def _(t):
        B.wait(t, dv2)
        t.matmul(B.bank[4][:, 0:TT], ones_r[:], rs_row, start=True, stop=True)
        t.matmul(B.bank[5][:, 0:TT], ones_r[:], mr_row, start=True, stop=True) \
            .then_inc(*B.inc("pe"))
    pe2 = B.tok("pe")

    @B.block.vector
    def _(v):
        B.wait(v, pe2)
        v.tensor_copy(L.rsb[:], B.bank[4][:, 0:TT])
        v.tensor_copy(L.mrb[:], B.bank[5][:, 0:TT]).then_inc(*B.inc("dv"))
    B.guard_bank(4, B.tok("dv"))
    B.guard_bank(5, B.tok("dv"))
    return B.tok("dv")


def emit_ln_xn(B, L, x_ch, xn_ch, scr_ch=None, tag="", pre_toks=None):
    ready = emit_ln_core(B, L, x_ch, tag, pre_toks=pre_toks)
    toks = []
    for k in range(NCH):
        @B.block.vector
        def _(v, k=k):
            if k == 0:
                B.wait(v, ready)
            mid = xn_ch[k] if scr_ch is None else scr_ch[k]
            i1 = v.tensor_tensor(mid, x_ch[k].bitcast(F32), L.rsb[:],
                                 ALU.mult)
            B.chain(v, "dv", i1)
            v.tensor_tensor(xn_ch[k], mid.bitcast(F32), L.mrb[:],
                            ALU.subtract).then_inc(*B.inc("dv"))
        toks.append(B.tok("dv"))
    return toks


def emit_ln_residual(B, L, y_ch, x_ch, tag="", pre_toks=None):
    ready = emit_ln_core(B, L, y_ch, tag, pre_toks=pre_toks)
    toks = []
    for k in range(NCH):
        @B.block.vector
        def _(v, k=k):
            if k == 0:
                B.wait(v, ready)
            i1 = v.tensor_tensor(y_ch[k], y_ch[k].bitcast(F32), L.rsb[:],
                                 ALU.mult)
            B.chain(v, "dv", i1)
            i2 = v.tensor_tensor(y_ch[k], y_ch[k].bitcast(F32), L.mrb[:],
                                 ALU.subtract)
            B.chain(v, "dv", i2)
            v.tensor_tensor(x_ch[k], x_ch[k].bitcast(F32), y_ch[k].bitcast(F32),
                            ALU.add).then_inc(*B.inc("dv"))
        toks.append(B.tok("dv"))
    return toks



def emit_swiglu_mlp(B, ws, xn_ch, xn_toks, y_ch, Wg_dram, Wu_dram, Wd_dram,
                    n_inter_ch, sgbufs, asup, accumulate, tag="",
                    scale_tile=None, scale_tok=None):
    SUP = 16
    n_sup = (n_inter_ch + SUP - 1) // SUP
    OGS = [(0, 6), (6, 12), (12, 16)]

    for sup in range(n_sup):
        i0 = sup * SUP
        i1 = min(i0 + SUP, n_inter_ch)
        ni = i1 - i0
        a_toks = []
        for ii, ic in enumerate(range(i0, i1)):
            ws.load(Wg_dram[ic])
            ws.load(Wu_dram[ic])
            bg = B.bank[0 if ii % 2 == 0 else 2]
            bu = B.bank[1 if ii % 2 == 0 else 3]

            @B.block.tensor
            def _(t, bg=bg, bu=bu, ii=ii):
                slab = ws.use(t)
                B.wait_bank(t, 0 if ii % 2 == 0 else 2)
                for d in range(NCH):
                    if sup == 0 and ii == 0:
                        B.wait(t, xn_toks[d])
                    mm = t.matmul(bg[:, 0:TT], slab[:, d * 128:(d + 1) * 128],
                                  xn_ch[d], start=(d == 0), stop=(d == NCH - 1),
                                  skip_group_check=True)
                mm.then_inc(*B.inc("pe"))
                ws.done(B.tok("pe"))
                slab = ws.use(t)
                B.wait_bank(t, 1 if ii % 2 == 0 else 3)
                for d in range(NCH):
                    mm = t.matmul(bu[:, 0:TT], slab[:, d * 128:(d + 1) * 128],
                                  xn_ch[d], start=(d == 0), stop=(d == NCH - 1),
                                  skip_group_check=True)
                mm.then_inc(*B.inc("pe"))
                ws.done(B.tok("pe"))
            mm_tok = B.tok("pe")
            sg = sgbufs[ii % 2]

            @B.block.scalar
            def _(s, sg=sg, mm_tok=mm_tok, bg=bg):
                B.wait(s, mm_tok)
                s.activation(sg[:], bg[:, 0:TT], AF.Silu).then_inc(*B.inc("ac"))
            ac_tok = B.tok("ac")

            @B.block.vector
            def _(v, sg=sg, ac_tok=ac_tok, bu=bu, ii=ii):
                B.wait(v, ac_tok)
                if scale_tile is not None:
                    B.wait(v, scale_tok)
                    i = v.tensor_tensor(asup[ii], sg[:], bu[:, 0:TT], ALU.mult)
                    B.chain(v, "dv", i)
                    v.tensor_tensor(asup[ii], asup[ii], scale_tile,
                                    ALU.mult).then_inc(*B.inc("dv"))
                else:
                    v.tensor_tensor(asup[ii], sg[:], bu[:, 0:TT],
                                    ALU.mult).then_inc(*B.inc("dv"))
            a_toks.append(B.tok("dv"))
            B.guard_bank(0 if ii % 2 == 0 else 2, B.tok("ac"))
            B.guard_bank(1 if ii % 2 == 0 else 3, B.tok("dv"))

        for og_lo, og_hi in OGS:
            ogw = og_hi - og_lo
            ow = ogw * 128
            ii = 0
            while ii < ni:
                npair = 2 if ii + 1 < ni else 1
                ic = i0 + ii
                src = Wd_dram[ic:ic + npair, :, og_lo * 128:og_hi * 128] \
                    .transpose([1, 0, 2])
                ws.load_view(src, npair * ow)

                @B.block.tensor
                def _(t, ii=ii, ogw=ogw, ow=ow, npair=npair):
                    slab = ws.use(t)
                    if ii == 0:
                        for bi in range(ogw):
                            B.wait_bank(t, bi)
                    for a in range(npair):
                        B.wait(t, a_toks[ii + a])
                        for oo in range(ogw):
                            mm = t.matmul(B.bank[oo][:, 0:TT],
                                          slab[:, a * ow + oo * 128:
                                               a * ow + (oo + 1) * 128],
                                          asup[ii + a],
                                          start=(ii + a == 0),
                                          stop=(ii + a == ni - 1),
                                          skip_group_check=True)
                    mm.then_inc(*B.inc("pe"))
                    ws.done(B.tok("pe"))
                ii += npair
            down_tok = B.tok("pe")
            for oo in range(ogw):
                o = og_lo + oo

                @B.block.vector
                def _(v, oo=oo, o=o, down_tok=down_tok):
                    B.wait(v, down_tok)
                    if accumulate or sup > 0:
                        v.tensor_tensor(y_ch[o], y_ch[o].bitcast(F32),
                                        B.bank[oo][:, 0:TT], ALU.add) \
                            .then_inc(*B.inc("dv"))
                    else:
                        v.tensor_copy(y_ch[o], B.bank[oo][:, 0:TT]) \
                            .then_inc(*B.inc("dv"))
                B.guard_bank(oo, B.tok("dv"))
    return B.tok("dv")



class L0Bufs:
    def __init__(self, B, L):
        self.XP = [B.mr(4 + k) for k in range(4)]
        self.SZ = [B.mr(8 + k) for k in range(4)]
        self.XA = [B.mr(k) for k in range(4)]
        self.YF = [B.MBF[:, 512 * k:512 * (k + 1)] for k in range(4)]
        self.PF = B.mr(12)
        self.PB = [B.mf(k) for k in range(4)]
        self.PC = [B.mf(4 + k) for k in range(4)]
        self.DBS = [B.mf(8 + k) for k in range(2)]
        self.ABS = [B.mf(10 + k) for k in range(2)]
        self.c1 = B.mf(12)
        self.c2 = B.mf(13)
        self.TAIL0 = B.sb("l0_tail", [P, 12], F32)
        self.dsc = B.SPR
        self.DELTA = B.mr(13)
        self.SH = B.sb("l0_sh", [P, 4], F32)
        self.wdt = B.sb("l0s_wdt", [P, 32], F32R)
        self.cw = B.sb("l0s_cw", [P, 4 * 4], F32)
        self.cb = B.sb("l0s_cb", [P, 4], F32)
        self.acol = B.sb("l0s_acol", [P, 4], F32)
        self.dcol = B.sb("l0s_dcol", [P, 4], F32)
        self.bdt = B.sb("l0s_bdt", [32, 1], F32)


def emit_l0attn_consts(B, L0, d):
    @B.block.gpsimd
    def _(g):
        for src, dst in [(d["l0_wdt"], L0.wdt[:]), (d["l0_cw"], L0.cw[:]),
                         (d["l0_cb"], L0.cb[:]), (d["l0_alog"], L0.dcol[:]),
                         (d["l0_dcol"], None), (d["l0_bdt"], L0.bdt[:])]:
            if dst is None:
                continue
            g.dma_start(out=dst, in_=src).then_inc(*B.inc("ld", 16))
        g.dma_start(out=L0.c1[:, 0:4], in_=d["l0_dcol"]).then_inc(*B.inc("ld", 16))
    ld_tok = B.tok("ld")

    @B.block.scalar
    def _(s):
        B.wait(s, ld_tok)
        s.activation(L0.acol[:], L0.dcol[:], AF.Exp).then_inc(*B.inc("ac"))
    ac_tok = B.tok("ac")

    @B.block.vector
    def _(v):
        B.wait(v, ac_tok)
        i = v.tensor_scalar(L0.acol[:], L0.acol[:], -1.0, None, op0=ALU.mult)
        B.chain(v, "dv", i)
        i = v.tensor_copy(L0.dcol[:], L0.c1[:, 0:4])
        B.chain(v, "dv", i)
        i = v.memset(L0.SH[:], 0.0)
        B.chain(v, "dv", i)
        v.memset(L0.TAIL0[:], 0.0).then_inc(*B.inc("dv"))
    return B.tok("ld"), B.tok("dv")


def emit_l0attn(B, L0, ws, xn_ch, xn_toks, y_ch, W, tile, tag=""):
    tail_tok = B.tok("dv")

    for o in range(4):
        ws.load(W["WxT"][o])

        @B.block.tensor
        def _(t, o=o):
            slab = ws.use(t)
            bk = o % 2
            B.wait_bank(t, bk)
            if o == 0:
                B.wait(t, tail_tok)
            for d in range(NCH):
                if o == 0:
                    B.wait(t, xn_toks[d])
                mm = t.matmul(B.bank[bk][:, 0:TT], slab[:, d * 128:(d + 1) * 128],
                              xn_ch[d], start=(d == 0), stop=(d == NCH - 1),
                              skip_group_check=True)
            mm.then_inc(*B.inc("pe"))
            ws.done(B.tok("pe"))
        mm_tok = B.tok("pe")

        @B.block.scalar
        def _(s, o=o, mm_tok=mm_tok):
            B.wait(s, mm_tok)
            s.copy(L0.XP[o], B.bank[o % 2][:, 0:TT]).then_inc(*B.inc("ac"))
        B.guard_bank(o % 2, B.tok("ac"))
    for o in range(4):
        ws.load(W["WzT"][o])

        @B.block.tensor
        def _(t, o=o):
            slab = ws.use(t)
            bk = o % 2
            B.wait_bank(t, bk)
            for d in range(NCH):
                mm = t.matmul(B.bank[bk][:, 0:TT], slab[:, d * 128:(d + 1) * 128],
                              xn_ch[d], start=(d == 0), stop=(d == NCH - 1),
                              skip_group_check=True)
            mm.then_inc(*B.inc("pe"))
            ws.done(B.tok("pe"))
        mm_tok = B.tok("pe")

        @B.block.scalar
        def _(s, o=o, mm_tok=mm_tok):
            B.wait(s, mm_tok)
            s.activation(L0.SZ[o][:], B.bank[o % 2][:, 0:TT],
                         AF.Silu).then_inc(*B.inc("ac"))
        B.guard_bank(o % 2, B.tok("ac"))
    xproj_tok = B.tok("ac")

    CSC = B.CSC
    for k in range(4):
        @B.block.vector
        def _(v, k=k):
            if k == 0:
                B.wait(v, xproj_tok)
            B.wait(v, B.tok("dv"))
            B.wait(v, B.tok("ac"))
            pre = L0.XP[k].bitcast(F32)
            cw = L0.cw
            i = v.tensor_copy(CSC[:, 0:3], L0.TAIL0[:, 3 * k:3 * k + 3])
            B.chain(v, "dv", i)
            i = v.tensor_copy(CSC[:, 3:6], pre[:, 0:3])
            B.chain(v, "dv", i)
            i = v.tensor_copy(L0.TAIL0[:, 3 * k:3 * k + 3], pre[:, TT - 3:TT])
            B.chain(v, "dv", i)
            i = v.tensor_scalar(CSC[:, 8:11], CSC[:, 0:3],
                                cw[:, 4 * k:4 * k + 1], None, op0=ALU.mult)
            B.chain(v, "dv", i)
            for kk in range(1, 4):
                i = v.scalar_tensor_tensor(CSC[:, 8:11], CSC[:, kk:kk + 3],
                                           cw[:, 4 * k + kk:4 * k + kk + 1],
                                           CSC[:, 8:11], op0=ALU.mult, op1=ALU.add)
                B.chain(v, "dv", i)
            i = v.tensor_scalar(CSC[:, 16:16 + TT - 3], pre[:, 0:TT - 3],
                                cw[:, 4 * k:4 * k + 1], None, op0=ALU.mult)
            B.chain(v, "dv", i)
            for kk in range(1, 3):
                i = v.scalar_tensor_tensor(CSC[:, 16:16 + TT - 3],
                                           pre[:, kk:TT - 3 + kk],
                                           cw[:, 4 * k + kk:4 * k + kk + 1],
                                           CSC[:, 16:16 + TT - 3],
                                           op0=ALU.mult, op1=ALU.add)
                B.chain(v, "dv", i)
            v.scalar_tensor_tensor(CSC[:, 16:16 + TT - 3], pre[:, 3:TT],
                                   cw[:, 4 * k + 3:4 * k + 4],
                                   CSC[:, 16:16 + TT - 3],
                                   op0=ALU.mult, op1=ALU.add).then_inc(*B.inc("dv"))
        dv_tok = B.tok("dv")

        @B.block.scalar
        def _(s, k=k, dv_tok=dv_tok):
            B.wait(s, dv_tok)
            i = s.activation(L0.XA[k][:, 0:3], CSC[:, 8:11], AF.Silu,
                             bias=L0.cb[:, k:k + 1])
            B.chain(s, "ac", i)
            s.activation(L0.XA[k][:, 3:TT], CSC[:, 16:16 + TT - 3], AF.Silu,
                         bias=L0.cb[:, k:k + 1]).then_inc(*B.inc("ac"))
    xa_tok = B.tok("ac")
    xa_toks = B.tok("ac")

    XAB = [B.MBF[:, 2048 + 512 * k:2048 + 512 * (k + 1)] for k in range(4)]

    @B.block.vector
    def _(v):
        B.wait(v, xa_tok)
        B.wait(v, B.tok("dv"))
        for k in range(4):
            i = v.tensor_copy(XAB[k], L0.XA[k][:].bitcast(F32))
            if k < 3:
                B.chain(v, "dv", i)
        i.then_inc(*B.inc("dv"))
    xab_tok = B.tok("dv")

    for o in range(9):
        ws.load(W["WpT"][o])

        @B.block.tensor
        def _(t, o=o):
            slab = ws.use(t)
            bk = o % 2
            B.wait_bank(t, bk)
            if o == 0:
                B.wait(t, xab_tok)
            for d in range(4):
                mm = t.matmul(B.bank[bk][:, 0:TT], slab[:, d * 128:(d + 1) * 128],
                              XAB[d], start=(d == 0), stop=(d == 3),
                              skip_group_check=True)
            mm.then_inc(*B.inc("pe"))
            ws.done(B.tok("pe"))
        mm_tok = B.tok("pe")

        @B.block.scalar
        def _(s, o=o, mm_tok=mm_tok):
            B.wait(s, mm_tok)
            if o == 0:
                s.copy(L0.PF[:], B.bank[o % 2][:, 0:TT]).then_inc(*B.inc("ac"))
            elif o <= 4:
                s.copy(L0.PB[o - 1][:], B.bank[o % 2][:, 0:TT]).then_inc(*B.inc("ac"))
            else:
                s.copy(L0.PC[o - 5][:], B.bank[o % 2][:, 0:TT]).then_inc(*B.inc("ac"))
        B.guard_bank(o % 2, B.tok("ac"))
    params_tok = B.tok("ac")

    @B.block.tensor
    def _(t):
        B.wait(t, params_tok)
        B.wait_bank(t, 2)
        t.matmul(B.bank[2][0:32, 0:TT], L0.wdt[:], L0.PF[:],
                 start=True, stop=True).then_inc(*B.inc("pe"))
    dmm_tok = B.tok("pe")
    ds = L0.dsc
    v_r = ds[0:32, 0:TT]
    a_r = ds[0:32, TT:2 * TT]
    dl = L0.DELTA[0:32, :]

    @B.block.vector
    def _(v):
        B.wait(v, dmm_tok)
        v.tensor_scalar(v_r, B.bank[2][0:32, 0:TT], L0.bdt[0:32, 0:1], None,
                        op0=ALU.add).then_inc(*B.inc("dv"))
    B.guard_bank(2, B.tok("dv"))
    dv_tok = B.tok("dv")

    @B.block.scalar
    def _(s):
        B.wait(s, dv_tok)
        i = s.activation(a_r, v_r, AF.Abs)
        B.chain(s, "ac", i)
        i = s.activation(a_r, a_r, AF.Exp, scale=-1.0)
        B.chain(s, "ac", i)
        i = s.activation(a_r, a_r, AF.Ln, bias=B.consts["c_eps"][0:32, 2:3])
        B.chain(s, "ac", i)
        s.activation(dl, v_r, AF.Relu).then_inc(*B.inc("ac"))
    ac_tok = B.tok("ac")

    @B.block.vector
    def _(v):
        B.wait(v, ac_tok)
        v.tensor_tensor(dl, dl.bitcast(F32), a_r, ALU.add).then_inc(*B.inc("dv"))
    delta_tok = B.tok("dv")

    l0exp = B.mr(14)[0:32, :]

    @B.block.gpsimd
    def _(g):
        B.wait(g, B.tok("pe"))
        g.dma_start(out=l0exp, in_=W["c_l0exp_d"]).then_inc(*B.inc("ld", 16))
    lexp_tok = B.tok("ld")
    ab_free = [None, None]
    for k in range(4):
        bk = 2 + (k % 2)

        @B.block.tensor
        def _(t, k=k, bk=bk):
            B.wait_bank(t, bk)
            B.wait(t, delta_tok)
            B.wait(t, lexp_tok)
            t.matmul(B.bank[bk][:, 0:TT],
                     l0exp[:, k * 128:(k + 1) * 128],
                     L0.DELTA[0:32, :],
                     start=True, stop=True).then_inc(*B.inc("pe"))
        db_tok = B.tok("pe")

        @B.block.scalar
        def _(s, k=k, bk=bk, db_tok=db_tok, ft=ab_free[k % 2]):
            B.wait(s, db_tok)
            if ft is not None:
                B.wait(s, ft)
            s.activation(L0.ABS[k % 2][:], B.bank[bk][:, 0:TT], AF.Exp,
                         scale=L0.acol[:, k:k + 1]).then_inc(*B.inc("ac"))
        ab_tok = B.tok("ac")

        @B.block.vector
        def _(v, k=k, bk=bk, ab_tok=ab_tok):
            B.wait(v, ab_tok)
            B.wait(v, B.tok("dv"))
            i = v.tensor_copy(L0.c1[:], B.bank[bk][:, 0:TT])
            B.chain(v, "dv", i)
            i = v.tensor_tensor(L0.c2[:], L0.c1[:], L0.PB[k][:], ALU.mult)
            B.chain(v, "dv", i)
            i = v.tensor_tensor(L0.DBS[k % 2][:], L0.c2[:],
                                L0.XA[k][:].bitcast(F32), ALU.mult)
            B.chain(v, "dv", i)
            i = v.tensor_tensor_scan(L0.PB[k][:], L0.ABS[k % 2][:], L0.DBS[k % 2][:],
                                     L0.SH[:, k:k + 1], op0=ALU.mult, op1=ALU.add)
            B.chain(v, "dv", i)
            i = v.tensor_copy(L0.SH[:, k:k + 1], L0.PB[k][:, TT - 1:TT])
            B.chain(v, "dv", i)
            i = v.tensor_tensor(L0.c2[:], L0.PB[k][:], L0.PC[k][:], ALU.mult)
            B.chain(v, "dv", i)
            i = v.scalar_tensor_tensor(L0.c1[:], L0.XA[k][:].bitcast(F32),
                                       L0.dcol[:, k:k + 1], L0.c2[:],
                                       op0=ALU.mult, op1=ALU.add)
            B.chain(v, "dv", i)
            v.tensor_tensor(L0.YF[k][:], L0.c1[:], L0.SZ[k][:].bitcast(F32),
                            ALU.mult).then_inc(*B.inc("dv"))
        B.guard_bank(bk, B.tok("dv"))
        ab_free[k % 2] = B.tok("dv")
    yf_tok = B.tok("dv")

    toks = []
    for o in range(NCH):
        ws.load(W["WoT"][o])

        @B.block.tensor
        def _(t, o=o):
            slab = ws.use(t)
            bk = o % 2
            B.wait_bank(t, bk)
            if o == 0:
                B.wait(t, yf_tok)
            for d in range(4):
                mm = t.matmul(B.bank[bk][:, 0:TT], slab[:, d * 128:(d + 1) * 128],
                              L0.YF[d][:], start=(d == 0), stop=(d == 3),
                              skip_group_check=True)
            mm.then_inc(*B.inc("pe"))
            ws.done(B.tok("pe"))
        mm_tok = B.tok("pe")

        @B.block.scalar
        def _(s, o=o, mm_tok=mm_tok):
            B.wait(s, mm_tok)
            s.copy(y_ch[o], B.bank[o % 2][:, 0:TT]).then_inc(*B.inc("ac"))
        B.guard_bank(o % 2, B.tok("ac"))
        toks.append(B.tok("ac"))
    return toks



class L0FRBufs:
    def __init__(self, B, L):
        mf = B.MF32
        self.rsc = mf[:, 9 * TT + 32:9 * TT + 48]
        self.t8 = mf[:, 9 * TT + 48:9 * TT + 56]
        self.cw4 = mf[:, 9 * TT:9 * TT + 4]
        self.CB0 = B.mf(4)
        self.crow = L.rows[0:1, 0:TT]
        self.wrt = B.sb("l0fr_wrt", [P, 128], BF16)


def emit_l0fr_consts(B, R, d):
    @B.block.gpsimd
    def _(g):
        g.dma_start(out=R.wrt[:], in_=d["l0f_wrt"]).then_inc(*B.inc("ld", 16))
    return B.tok("ld")


def emit_l0ffn_router(B, R, xn_ch, xn_toks):
    @B.block.tensor
    def _(t):
        B.wait_bank(t, 6)
        for dd in range(NCH):
            B.wait(t, xn_toks[dd])
        for tc in range(4):
            for dd in range(NCH):
                mm = t.matmul(B.bank[6][:, 8 * tc:8 * (tc + 1)],
                              xn_ch[dd][:, tc * 128:(tc + 1) * 128],
                              R.wrt[:, 8 * dd:8 * (dd + 1)],
                              start=(dd == 0), stop=(dd == NCH - 1),
                              skip_group_check=True)
        mm.then_inc(*B.inc("pe"))
    lg_tok = B.tok("pe")

    cols = []
    for tc in range(4):
        lg = B.bank[6][:, 8 * tc:8 * (tc + 1)]
        m1 = R.rsc[:, 4 * tc + 0:4 * tc + 1]
        m2 = R.rsc[:, 4 * tc + 1:4 * tc + 2]
        rs = R.rsc[:, 4 * tc + 2:4 * tc + 3]
        cw = R.cw4[:, tc:tc + 1]

        @B.block.vector
        def _(v, lg=lg, m1=m1, m2=m2, tc=tc):
            if tc == 0:
                B.wait(v, lg_tok)
            B.wait(v, B.tok("dv"))
            i = v.tensor_reduce(m1, lg, axis=AX.X, op=ALU.max)
            B.chain(v, "dv", i)
            i = v.tensor_scalar(R.t8[:], lg, m1, None, op0=ALU.is_ge)
            B.chain(v, "dv", i)
            i = v.scalar_tensor_tensor(R.t8[:], R.t8[:], -1e9, lg,
                                       op0=ALU.mult, op1=ALU.add)
            B.chain(v, "dv", i)
            i = v.tensor_reduce(m2, R.t8[:], axis=AX.X, op=ALU.max)
            B.chain(v, "dv", i)
            v.tensor_tensor(m2, m2, m1, ALU.subtract).then_inc(*B.inc("dv"))
        dv1 = B.tok("dv")

        @B.block.scalar
        def _(s, m2=m2, rs=rs, dv1=dv1):
            B.wait(s, dv1)
            s.activation(rs, m2, AF.Exp).then_inc(*B.inc("ac"))
        ac1 = B.tok("ac")

        @B.block.vector
        def _(v, rs=rs, m1=m1, cw=cw, ac1=ac1):
            B.wait(v, ac1)
            B.wait(v, B.tok("dv"))
            i = v.tensor_scalar(cw, rs, 1.0, None, op0=ALU.add)
            B.chain(v, "dv", i)
            i = v.reciprocal(cw, cw)
            B.chain(v, "dv", i)
            i = v.tensor_tensor(m1, rs, cw, ALU.mult)
            B.chain(v, "dv", i)
            v.tensor_tensor(cw, cw, m1, ALU.add).then_inc(*B.inc("dv"))
        cols.append(cw)
    B.guard_bank(6, B.tok("dv"))
    return emit_cbcast(B, R, cols, R.CB0[:])



class L1FBufs:
    def __init__(self, B, L):
        mf = B.MF32
        self.COMB = mf[:, 9 * TT:9 * TT + 32]
        self.rsc = mf[:, 9 * TT + 32:9 * TT + 48]
        self.t8 = mf[:, 9 * TT + 48:9 * TT + 56]
        self.CB = B.mf(4)
        self.SGB = B.mf(5)
        self.crow = L.rows[0:1, 0:TT]
        self.sgrow = L.rows[0:1, TT:2 * TT]
        self.wrt = B.sb("l1fs_wrt", [P, 16 * 8], BF16)
        self.wsgt = B.sb("l1fs_wsgt", [P, 16], BF16)


def emit_l1ffn_consts(B, F, d):
    @B.block.gpsimd
    def _(g):
        g.dma_start(out=F.wrt[:], in_=d["l1f_wrt"]).then_inc(*B.inc("ld", 16))
        g.dma_start(out=F.wsgt[:], in_=d["l1f_wsgt"]).then_inc(*B.inc("ld", 16))
    return B.tok("ld")


def emit_l1ffn_router(B, F, xn_ch, xn_toks):
    @B.block.tensor
    def _(t):
        B.wait_bank(t, 6)
        for d in range(NCH):
            B.wait(t, xn_toks[d])
        for tc in range(4):
            for d in range(NCH):
                mm = t.matmul(B.bank[6][:, 8 * tc:8 * (tc + 1)],
                              xn_ch[d][:, tc * 128:(tc + 1) * 128],
                              F.wrt[:, 8 * d:8 * (d + 1)],
                              start=(d == 0), stop=(d == NCH - 1),
                              skip_group_check=True)
        mm.then_inc(*B.inc("pe"))
    lg_tok = B.tok("pe")

    for tc in range(4):
        lg = B.bank[6][:, 8 * tc:8 * (tc + 1)]
        pp = F.COMB[:, 8 * tc:8 * (tc + 1)]
        m1 = F.rsc[:, 4 * tc + 0:4 * tc + 1]
        m2 = F.rsc[:, 4 * tc + 1:4 * tc + 2]
        rs = F.rsc[:, 4 * tc + 2:4 * tc + 3]

        @B.block.vector
        def _(v, lg=lg, pp=pp, m1=m1, tc=tc):
            if tc == 0:
                B.wait(v, lg_tok)
            else:
                B.wait(v, B.tok("ac"))
            B.wait(v, B.tok("dv"))
            i = v.tensor_reduce(m1, lg, axis=AX.X, op=ALU.max)
            B.chain(v, "dv", i)
            v.tensor_scalar(pp, lg, m1, None, op0=ALU.subtract).then_inc(*B.inc("dv"))
        dv1 = B.tok("dv")

        @B.block.scalar
        def _(s, pp=pp, dv1=dv1):
            B.wait(s, dv1)
            s.activation(pp, pp, AF.Exp).then_inc(*B.inc("ac"))
        ac1 = B.tok("ac")

        @B.block.vector
        def _(v, pp=pp, m1=m1, m2=m2, rs=rs, ac1=ac1):
            B.wait(v, ac1)
            i = v.tensor_reduce(rs, pp, axis=AX.X, op=ALU.add)
            B.chain(v, "dv", i)
            i = v.reciprocal(rs, rs)
            B.chain(v, "dv", i)
            i = v.tensor_scalar(pp, pp, rs, None, op0=ALU.mult)
            B.chain(v, "dv", i)
            i = v.tensor_reduce(m1, pp, axis=AX.X, op=ALU.max)
            B.chain(v, "dv", i)
            i = v.tensor_scalar(F.t8[:], pp, m1, None, op0=ALU.is_ge)
            B.chain(v, "dv", i)
            i = v.scalar_tensor_tensor(F.t8[:], F.t8[:], -1e9, pp,
                                       op0=ALU.mult, op1=ALU.add)
            B.chain(v, "dv", i)
            i = v.tensor_reduce(m2, F.t8[:], axis=AX.X, op=ALU.max)
            B.chain(v, "dv", i)
            i = v.tensor_scalar(F.t8[:], pp, m2, None, op0=ALU.is_ge)
            B.chain(v, "dv", i)
            v.tensor_tensor(pp, pp, F.t8[:], ALU.mult).then_inc(*B.inc("dv"))
    B.guard_bank(6, B.tok("dv"))
    return B.tok("dv")


def emit_cbcast(B, F, col_src_ap_list, dst_tile):
    @B.block.tensor
    def _(t):
        B.wait_bank(t, 6)
        for tc in range(4):
            mm = t.matmul(B.bank[6][0:1, tc * 128:(tc + 1) * 128],
                          col_src_ap_list[tc], B.consts["c_ident"][:],
                          start=True, stop=True, skip_group_check=True)
        mm.then_inc(*B.inc("pe"))
    r_tok = B.tok("pe")

    @B.block.vector
    def _(v):
        B.wait(v, r_tok)
        B.wait(v, B.tok("dv"))
        v.tensor_copy(F.crow[:], B.bank[6][0:1, 0:TT]).then_inc(*B.inc("dv"))
    B.guard_bank(6, B.tok("dv"))
    c_tok = B.tok("dv")

    @B.block.tensor
    def _(t):
        B.wait_bank(t, 3)
        B.wait(t, c_tok)
        t.matmul(B.bank[3][:, 0:TT], B.consts["c_ones_row"][:], F.crow[:],
                 start=True, stop=True).then_inc(*B.inc("pe"))
    b_tok = B.tok("pe")

    @B.block.vector
    def _(v):
        B.wait(v, b_tok)
        B.wait(v, B.tok("dv"))
        v.tensor_copy(dst_tile, B.bank[3][:, 0:TT]).then_inc(*B.inc("dv"))
    B.guard_bank(3, B.tok("dv"))
    return B.tok("dv")


def emit_gu_mlp(B, ws, xn_ch, xn_toks, y_ch, GU_dram, D_dram, n_inter_ch,
                gu_off, sgbufs, asup, scale_tile, scale_tok, first_writer,
                tag=""):
    SUP = 16
    n_sup = (n_inter_ch + SUP - 1) // SUP
    OGS = [(0, 6), (6, 12), (12, 16)]
    for sup in range(n_sup):
        i0 = sup * SUP
        i1 = min(i0 + SUP, n_inter_ch)
        ni = i1 - i0
        a_toks = []
        for ii, ic in enumerate(range(i0, i1)):
            ws.load(GU_dram[ic])
            ws.load(GU_dram[ic + gu_off])
            bg = B.bank[0 if ii % 2 == 0 else 2]
            bu = B.bank[1 if ii % 2 == 0 else 3]

            @B.block.tensor
            def _(t, bg=bg, bu=bu, ii=ii, sup=sup):
                slab = ws.use(t)
                B.wait_bank(t, 0 if ii % 2 == 0 else 2)
                for d in range(NCH):
                    if sup == 0 and ii == 0:
                        B.wait(t, xn_toks[d])
                    mm = t.matmul(bg[:, 0:TT], slab[:, d * 128:(d + 1) * 128],
                                  xn_ch[d], start=(d == 0), stop=(d == NCH - 1),
                                  skip_group_check=True)
                mm.then_inc(*B.inc("pe"))
                ws.done(B.tok("pe"))
                slab = ws.use(t)
                B.wait_bank(t, 1 if ii % 2 == 0 else 3)
                for d in range(NCH):
                    mm = t.matmul(bu[:, 0:TT], slab[:, d * 128:(d + 1) * 128],
                                  xn_ch[d], start=(d == 0), stop=(d == NCH - 1),
                                  skip_group_check=True)
                mm.then_inc(*B.inc("pe"))
                ws.done(B.tok("pe"))
            mm_tok = B.tok("pe")
            sg = sgbufs[ii % 2]

            @B.block.scalar
            def _(s, sg=sg, mm_tok=mm_tok, bg=bg):
                B.wait(s, mm_tok)
                s.activation(sg[:], bg[:, 0:TT], AF.Silu).then_inc(*B.inc("ac"))
            ac_tok = B.tok("ac")

            @B.block.vector
            def _(v, sg=sg, ac_tok=ac_tok, bu=bu, ii=ii):
                B.wait(v, ac_tok)
                if scale_tile is not None:
                    B.wait(v, scale_tok)
                    i = v.tensor_tensor(asup[ii], sg[:], bu[:, 0:TT], ALU.mult)
                    B.chain(v, "dv", i)
                    v.tensor_tensor(asup[ii], asup[ii], scale_tile,
                                    ALU.mult).then_inc(*B.inc("dv"))
                else:
                    v.tensor_tensor(asup[ii], sg[:], bu[:, 0:TT],
                                    ALU.mult).then_inc(*B.inc("dv"))
            a_toks.append(B.tok("dv"))
            B.guard_bank(0 if ii % 2 == 0 else 2, B.tok("ac"))
            B.guard_bank(1 if ii % 2 == 0 else 3, B.tok("dv"))

        for og_lo, og_hi in OGS:
            ogw = og_hi - og_lo
            ow = ogw * 128
            ii = 0
            while ii < ni:
                npair = 2 if ii + 1 < ni else 1
                ic = i0 + ii
                src = D_dram[ic:ic + npair, :, og_lo * 128:og_hi * 128] \
                    .transpose([1, 0, 2])
                ws.load_view(src, npair * ow)

                @B.block.tensor
                def _(t, ii=ii, ogw=ogw, ow=ow, npair=npair):
                    slab = ws.use(t)
                    if ii == 0:
                        for bi in range(ogw):
                            B.wait_bank(t, bi)
                    for a in range(npair):
                        B.wait(t, a_toks[ii + a])
                        for oo in range(ogw):
                            mm = t.matmul(B.bank[oo][:, 0:TT],
                                          slab[:, a * ow + oo * 128:
                                               a * ow + (oo + 1) * 128],
                                          asup[ii + a],
                                          start=(ii + a == 0),
                                          stop=(ii + a == ni - 1),
                                          skip_group_check=True)
                    mm.then_inc(*B.inc("pe"))
                    ws.done(B.tok("pe"))
                ii += npair
            down_tok = B.tok("pe")
            for oo in range(ogw):
                o = og_lo + oo

                @B.block.vector
                def _(v, oo=oo, o=o, down_tok=down_tok, sup=sup):
                    B.wait(v, down_tok)
                    if first_writer and sup == 0:
                        v.tensor_copy(y_ch[o], B.bank[oo][:, 0:TT]) \
                            .then_inc(*B.inc("dv"))
                    else:
                        v.tensor_tensor(y_ch[o], y_ch[o].bitcast(F32),
                                        B.bank[oo][:, 0:TT], ALU.add) \
                            .then_inc(*B.inc("dv"))
                B.guard_bank(oo, B.tok("dv"))
    return B.tok("dv")


def emit_l1ffn(B, F, ws, xn_ch, xn_toks, y_ch, d, sgbufs, asup, tile):
    r_tok = emit_l1ffn_router(B, F, xn_ch, xn_toks)

    @B.block.tensor
    def _(t):
        B.wait_bank(t, 3)
        for dd in range(NCH):
            mm = t.matmul(B.bank[3][0:1, 0:TT], F.wsgt[:, dd:dd + 1],
                          xn_ch[dd], start=(dd == 0), stop=(dd == NCH - 1),
                          skip_group_check=True)
        mm.then_inc(*B.inc("pe"))
    sg_tok = B.tok("pe")

    @B.block.scalar
    def _(s):
        B.wait(s, sg_tok)
        s.activation(F.sgrow[:], B.bank[3][0:1, 0:TT],
                     AF.Sigmoid).then_inc(*B.inc("ac"))
    B.guard_bank(3, B.tok("ac"))
    sgr_tok = B.tok("ac")

    @B.block.tensor
    def _(t):
        B.wait_bank(t, 3)
        B.wait(t, sgr_tok)
        t.matmul(B.bank[3][:, 0:TT], B.consts["c_ones_row"][:], F.sgrow[:],
                 start=True, stop=True).then_inc(*B.inc("pe"))
    sgb_tok = B.tok("pe")

    @B.block.vector
    def _(v):
        B.wait(v, sgb_tok)
        v.tensor_copy(F.SGB[:], B.bank[3][:, 0:TT]).then_inc(*B.inc("dv"))
    B.guard_bank(3, B.tok("dv"))
    sgbt = B.tok("dv")

    emit_gu_mlp(B, ws, xn_ch, xn_toks, y_ch, d["WsGU"], d["WsD"], 44, 44,
                sgbufs, asup, F.SGB[:], sgbt, first_writer=True)

    for e in range(8):
        cols = [F.COMB[:, 8 * tc + e:8 * tc + e + 1] for tc in range(4)]
        cb_tok = emit_cbcast(B, F, cols, F.CB[:])
        emit_gu_mlp(B, ws, xn_ch, xn_toks, y_ch, d[f"WeGU{e}"], d[f"WeD{e}"],
                    11, 11, sgbufs, asup, F.CB[:], cb_tok, first_writer=False)
    return B.tok("dv")



class L1ABufs:
    def __init__(self, B, yt_slabs, xnt_slabs):
        self.PRE = yt_slabs
        self.QKV = xnt_slabs
        self.SZ1 = [B.mr(8 + k) for k in range(8)]
        _ysl = [6, 7, 9]
        _yext = B.sb("ynf2", [P, 1024], BF16)
        self.YNF = [B.mf(_ysl[k // 2]).bitcast(BF16)[:, (k % 2) * TT:
                    (k % 2 + 1) * TT] for k in range(6)] + \
                   [_yext[:, 0:TT], _yext[:, TT:2 * TT]]
        self.TAIL = B.sb("a1_tail", [P, 48], F32)
        self.shld = B.new_sem("shld")
        self.BET = B.MF32[0:16, 10 * TT:11 * TT]
        self.GG = B.MF32[0:16, 11 * TT:12 * TT]
        self.gsc = B.SPR
        mb = B.MBF
        self.QBF = [mb[:, 128 * j:128 * (j + 1)] for j in range(4)]
        self.KBF = [mb[:, 512 + 128 * j:512 + 128 * (j + 1)] for j in range(4)]
        self.VBF = [mb[:, 1024 + 128 * j:1024 + 128 * (j + 1)] for j in range(8)]
        self.kT = mb[:, 2048:2560]
        self.qT = mb[:, 2560:3072]
        self.vT = mb[:, 3072:4096]
        self.Mbf = mb[:, 4096:4224]
        self.Mtbf = mb[:, 4224:4352]
        self.Atbf = mb[:, 4352:4480]
        self.Zs = [mb[:, 4480 + 128 * j:4480 + 128 * (j + 1)] for j in range(6)]
        self.Pb = [mb[:, 5248 + 128 * j:5248 + 128 * (j + 1)] for j in range(2)]
        self.Xap = [mb[:, 5504 + 64 * j:5504 + 64 * (j + 1)] for j in range(2)]
        self.kp = mb[:, 5632:5696]
        mf = B.MF32
        self.OT = B.mf(0, 2)
        self.EMS = [mf[:, 2 * TT:2 * TT + 128], mf[:, 2 * TT + 128:2 * TT + 256]]
        self.TM = mf[:, 2 * TT + 256:2 * TT + 384]
        self.G16 = mf[:, 3 * TT:3 * TT + 192]
        self.EX = mf[:, 3 * TT + 192:3 * TT + 240]
        self.LR = mf[:, 3 * TT + 240:3 * TT + 256]
        self.RSO = mf[:, 3 * TT + 256:3 * TT + 272]
        self.Rf = mf[:, 8 * TT:8 * TT + 64]
        self.t64 = mf[:, 8 * TT + 64:8 * TT + 128]
        self.GTOT = mf[0:1, 2 * TT + 1408:2 * TT + 1424]
        self.EGR = mf[0:1, 2 * TT + 1424:2 * TT + 1440]
        self.EGRp = mf[0:1, 2 * TT + 1440:2 * TT + 1456]
        self.EGB = mf[0:64, 2 * TT + 1472:2 * TT + 1488]
        self.S = B.sb("a1_s", [64, 1024], F32R)
        self.Shi = B.sb("a1_shi", [P, 512], F32R)
        self.cw1 = B.sb("a1_cw1", [P, 64], F32)
        self.nea = B.sb("a1_nea", [16, 1], F32)
        self.dtb = B.sb("a1_dtb", [16, 1], F32)
        self.nwc = B.sb("a1_nwc", [P, 8], F32)


def emit_l1attn_consts(B, A, d):
    szsem = B.new_sem("szld")

    @B.block.gpsimd
    def _(g):
        for src, dst in [(d["a1d_cw1"], A.cw1[:]), (d["a1d_alog"], A.nea[:]),
                         (d["a1d_dtb"], A.dtb[:]), (d["a1d_nwc"], A.nwc[:])]:
            g.dma_start(out=dst, in_=src).then_inc(*B.inc("ld", 16))
        zr = d["c_zeros_r"]
        g.dma_start(out=A.S[0:64, 0:512], in_=zr[0:64, :]).then_inc(*B.inc(szsem, 16))
        g.dma_start(out=A.S[0:64, 512:1024], in_=zr[0:64, :]).then_inc(*B.inc(szsem, 16))
        g.dma_start(out=A.Shi[:], in_=zr[:]).then_inc(*B.inc(szsem, 16))
    A._sz_tok = B.tok(szsem)
    ld_tok = B.tok("ld")

    @B.block.scalar
    def _(s):
        B.wait(s, ld_tok)
        s.activation(A.nea[:], A.nea[:], AF.Exp).then_inc(*B.inc("ac"))
    ac_tok = B.tok("ac")

    @B.block.vector
    def _(v):
        B.wait(v, ac_tok)
        i = v.tensor_scalar(A.nea[:], A.nea[:], -1.0, None, op0=ALU.mult)
        B.chain(v, "dv", i)
        v.memset(A.TAIL[:], 0.0).then_inc(*B.inc("dv"))
    return B.tok("ld"), B.tok("dv")


def _proj_evac(B, ws, Wdram, n_o, xn_ch, xn_toks, evac_fn, first_wait=None):
    for o in range(n_o):
        ws.load(Wdram[o])

        @B.block.tensor
        def _(t, o=o):
            slab = ws.use(t)
            bk = o % 2
            B.wait_bank(t, bk)
            if o == 0 and first_wait is not None:
                B.wait(t, first_wait)
            for dd in range(NCH):
                if o == 0 and xn_toks is not None:
                    B.wait(t, xn_toks[dd])
                mm = t.matmul(B.bank[bk][:, 0:TT], slab[:, dd * 128:(dd + 1) * 128],
                              xn_ch[dd], start=(dd == 0), stop=(dd == NCH - 1),
                              skip_group_check=True)
            mm.then_inc(*B.inc("pe"))
            ws.done(B.tok("pe"))
        mm_tok = B.tok("pe")
        evac_fn(o, B.bank[o % 2], mm_tok)


def emit_l1attn(B, A, ws, xn_ch, xn_toks, y_ch_out, d, tile):
    nc = B.nc
    def ev_qkv(o, bank, mm_tok):
        @B.block.scalar
        def _(s, o=o, bank=bank, mm_tok=mm_tok):
            B.wait(s, mm_tok)
            s.copy(A.PRE[o][:], bank[:, 0:TT]).then_inc(*B.inc("ac"))
        B.guard_bank(o % 2, B.tok("ac"))
    _proj_evac(B, ws, d["W1qkvT"], 16, xn_ch, xn_toks, ev_qkv)

    def ev_z(o, bank, mm_tok):
        @B.block.scalar
        def _(s, o=o, bank=bank, mm_tok=mm_tok):
            B.wait(s, mm_tok)
            s.activation(A.SZ1[o][:], bank[:, 0:TT], AF.Silu).then_inc(*B.inc("ac"))
        B.guard_bank(o % 2, B.tok("ac"))
    _proj_evac(B, ws, d["W1zT"], 8, xn_ch, None, ev_z)

    ws.load(d["a1d_wbt"])
    ws.load(d["a1d_wat"])

    @B.block.tensor
    def _(t):
        B.wait_bank(t, 2)
        B.wait_bank(t, 3)
        slab = ws.use(t)
        for dd in range(NCH):
            mb = t.matmul(B.bank[2][0:16, 0:TT], slab[:, 16 * dd:16 * (dd + 1)],
                          xn_ch[dd], start=(dd == 0), stop=(dd == NCH - 1),
                          skip_group_check=True)
        mb.then_inc(*B.inc("pe"))
        ws.done(B.tok("pe"))
        slab = ws.use(t)
        for dd in range(NCH):
            ma = t.matmul(B.bank[3][0:16, 0:TT], slab[:, 16 * dd:16 * (dd + 1)],
                          xn_ch[dd], start=(dd == 0), stop=(dd == NCH - 1),
                          skip_group_check=True)
        ma.then_inc(*B.inc("pe"))
        ws.done(B.tok("pe"))
    ba_tok = B.tok("pe")

    @B.block.scalar
    def _(s):
        B.wait(s, ba_tok)
        s.activation(A.BET, B.bank[2][0:16, 0:TT], AF.Sigmoid).then_inc(*B.inc("ac"))
    B.guard_bank(2, B.tok("ac"))
    bet_tok = B.tok("ac")

    v_r = A.gsc[0:16, 0:TT]
    a_r = A.gsc[0:16, TT:2 * TT]

    @B.block.vector
    def _(v):
        B.wait(v, ba_tok)
        v.tensor_scalar(v_r, B.bank[3][0:16, 0:TT], A.dtb[0:16, 0:1], None,
                        op0=ALU.add).then_inc(*B.inc("dv"))
    B.guard_bank(3, B.tok("dv"))
    g1 = B.tok("dv")

    @B.block.scalar
    def _(s):
        B.wait(s, g1)
        i = s.activation(a_r, v_r, AF.Abs)
        B.chain(s, "ac", i)
        i = s.activation(a_r, a_r, AF.Exp, scale=-1.0)
        B.chain(s, "ac", i)
        i = s.activation(a_r, a_r, AF.Ln, bias=B.consts["c_eps"][0:16, 2:3])
        B.chain(s, "ac", i)
        s.activation(A.GG, v_r, AF.Relu).then_inc(*B.inc("ac"))
    g2 = B.tok("ac")

    @B.block.vector
    def _(v):
        B.wait(v, g2)
        i = v.tensor_tensor(A.GG, A.GG, a_r, ALU.add)
        B.chain(v, "dv", i)
        v.tensor_scalar(A.GG, A.GG, A.nea[0:16, 0:1], None,
                        op0=ALU.mult).then_inc(*B.inc("dv"))
    gg_tok = B.tok("dv")
    A._gg_tok = gg_tok
    A._bet_tok = bet_tok

    qkv_toks = []
    for k in range(NCH):
        @B.block.vector
        def _(v, k=k):
            B.wait(v, B.tok("ac"))
            B.wait(v, B.tok("dv"))
            pre = A.PRE[k][:].bitcast(F32)
            cw = A.cw1
            i = v.tensor_copy(B.CSC[:, 0:3], A.TAIL[:, 3 * k:3 * k + 3])
            B.chain(v, "dv", i)
            i = v.tensor_copy(B.CSC[:, 3:6], A.PRE[k][:, 0:3].bitcast(F32))
            B.chain(v, "dv", i)
            i = v.tensor_copy(A.TAIL[:, 3 * k:3 * k + 3],
                              A.PRE[k][:, TT - 3:TT].bitcast(F32))
            B.chain(v, "dv", i)
            i = v.tensor_scalar(B.CSC[:, 8:11], B.CSC[:, 0:3],
                                cw[:, 4 * k:4 * k + 1], None, op0=ALU.mult)
            B.chain(v, "dv", i)
            for kk in range(1, 4):
                i = v.scalar_tensor_tensor(B.CSC[:, 8:11], B.CSC[:, kk:kk + 3],
                                           cw[:, 4 * k + kk:4 * k + kk + 1],
                                           B.CSC[:, 8:11], op0=ALU.mult, op1=ALU.add)
                B.chain(v, "dv", i)
            i = v.tensor_scalar(B.CSC[:, 16:16 + TT - 3], pre[:, 0:TT - 3],
                                cw[:, 4 * k:4 * k + 1], None, op0=ALU.mult)
            B.chain(v, "dv", i)
            for kk in range(1, 3):
                i = v.scalar_tensor_tensor(B.CSC[:, 16:16 + TT - 3],
                                           pre[:, kk:TT - 3 + kk],
                                           cw[:, 4 * k + kk:4 * k + kk + 1],
                                           B.CSC[:, 16:16 + TT - 3],
                                           op0=ALU.mult, op1=ALU.add)
                B.chain(v, "dv", i)
            v.scalar_tensor_tensor(B.CSC[:, 16:16 + TT - 3],
                                   pre[:, 3:TT], cw[:, 4 * k + 3:4 * k + 4],
                                   B.CSC[:, 16:16 + TT - 3],
                                   op0=ALU.mult, op1=ALU.add).then_inc(*B.inc("dv"))
        dvt = B.tok("dv")

        @B.block.scalar
        def _(s, k=k, dvt=dvt):
            B.wait(s, dvt)
            i = s.activation(A.QKV[k][:, 0:3], B.CSC[:, 8:11], AF.Silu)
            B.chain(s, "ac", i)
            s.activation(A.QKV[k][:, 3:TT], B.CSC[:, 16:16 + TT - 3],
                         AF.Silu).then_inc(*B.inc("ac"))
        qkv_toks.append(B.tok("ac"))
    return qkv_toks


def emit_l1attn_gdn(B, A, qkv_toks, gg_tok, bet_tok, s_tok_in):
    Ibf = B.consts["c_ident_bfr"]
    If32 = B.consts["c_ident"]
    onesr = B.consts["c_onesrow_f32"]
    onesc = B.consts["c_onescol_f32"]
    seg = lambda i: A.G16[:, 16 * i:16 * (i + 1)]
    GT, BT, G, LB, LRK16, LRQ16, GKP, GMB, GKM, NGKM, GA = [seg(i) for i in range(11)]
    EDEC = A.EX[:, 0:16]
    EO = A.EX[:, 16:32]
    EK = A.EX[:, 32:48]
    eps2 = B.consts["c_eps"][:, 1:2]

    s_tok = s_tok_in
    ynf_toks = []
    for cc in range(4):
        csl = slice(cc * 128, (cc + 1) * 128)

        @B.block.vector
        def _(v, cc=cc, csl=csl):
            B.wait(v, B.tok("dv"))
            B.wait(v, B.tok("pe"))
            for j in range(4):
                if cc == 0:
                    B.wait(v, qkv_toks[j])
                i = v.tensor_copy(A.QBF[j][:], A.QKV[j][:, csl].bitcast(F32))
                B.chain(v, "dv", i)
            for j in range(4):
                if cc == 0:
                    B.wait(v, qkv_toks[4 + j])
                i = v.tensor_copy(A.KBF[j][:], A.QKV[4 + j][:, csl].bitcast(F32))
                B.chain(v, "dv", i)
            for j in range(7):
                if cc == 0:
                    B.wait(v, qkv_toks[8 + j])
                i = v.tensor_copy(A.VBF[j][:], A.QKV[8 + j][:, csl].bitcast(F32))
                B.chain(v, "dv", i)
            if cc == 0:
                B.wait(v, qkv_toks[15])
            v.tensor_copy(A.VBF[7][:],
                          A.QKV[15][:, csl].bitcast(F32)).then_inc(*B.inc("dv"))
        bf_tok = B.tok("dv")

        for (srcs, dst, lbl) in ((A.QBF, A.qT, "q"), (A.KBF, A.kT, "k")):
            @B.block.tensor
            def _(t, srcs=srcs, cc=cc):
                B.wait_bank(t, 7)
                B.wait(t, bf_tok)
                for j in range(4):
                    mm = t.transpose(B.bank_bf[:, 128 * j:128 * (j + 1)],
                                     srcs[j][:], Ibf[:])
                mm.then_inc(*B.inc("pe"))
            tr_tok = B.tok("pe")

            @B.block.vector
            def _(v, dst=dst, tr_tok=tr_tok):
                B.wait(v, tr_tok)
                B.wait(v, B.tok("dv"))
                v.tensor_copy(dst, B.bank_bf[:, 0:512]).then_inc(*B.inc("dv"))
            B.guard_bank(7, B.tok("dv"))
        for half in range(2):
            @B.block.tensor
            def _(t, half=half):
                B.wait_bank(t, 7)
                for j in range(4):
                    mm = t.transpose(B.bank_bf[:, 128 * j:128 * (j + 1)],
                                     A.VBF[4 * half + j][:], Ibf[:])
                mm.then_inc(*B.inc("pe"))
            tr_tok = B.tok("pe")

            @B.block.vector
            def _(v, half=half, tr_tok=tr_tok):
                B.wait(v, tr_tok)
                B.wait(v, B.tok("dv"))
                v.tensor_copy(A.vT[:, 512 * half:512 * (half + 1)],
                              B.bank_bf[:, 0:512]).then_inc(*B.inc("dv"))
            B.guard_bank(7, B.tok("dv"))
        tr_done = B.tok("dv")

        @B.block.vector
        def _(v):
            B.wait(v, tr_done)
            csc = B.CSC[:, 0:512]
            i = v.tensor_tensor(csc, A.kT, A.kT, ALU.mult)
            B.chain(v, "dv", i)
            i = v.tensor_reduce(A.LR[:, 0:8],
                                csc.rearrange("p (h k) -> p h k", k=64),
                                axis=AX.X, op=ALU.add)
            B.chain(v, "dv", i)
            i = v.tensor_tensor(csc, A.qT, A.qT, ALU.mult)
            B.chain(v, "dv", i)
            v.tensor_reduce(A.LR[:, 8:16],
                            csc.rearrange("p (h k) -> p h k", k=64),
                            axis=AX.X, op=ALU.add).then_inc(*B.inc("dv"))
        ss_tok = B.tok("dv")

        @B.block.scalar
        def _(s, ss_tok=ss_tok):
            B.wait(s, ss_tok)
            i = s.activation(A.LR[:, 0:8], A.LR[:, 0:8], AF.Ln, bias=eps2)
            B.chain(s, "ac", i)
            s.activation(A.LR[:, 8:16], A.LR[:, 8:16], AF.Ln,
                         bias=eps2).then_inc(*B.inc("ac"))
        ln_tok = B.tok("ac")

        @B.block.tensor
        def _(t, cc=cc, csl=csl):
            B.wait_bank(t, 2)
            B.wait(t, gg_tok)
            B.wait(t, bet_tok)
            t.transpose(B.bank[2][:, 0:16], A.GG[:, csl],
                        If32[0:16, 0:16]).then_inc(*B.inc("pe"))
            t.transpose(B.bank[2][:, 16:32], A.BET[:, csl],
                        If32[0:16, 0:16]).then_inc(*B.inc("pe"))
        gtr_tok = B.tok("pe")

        @B.block.vector
        def _(v, gtr_tok=gtr_tok):
            B.wait(v, gtr_tok)
            B.wait(v, B.tok("dv"))
            i = v.tensor_copy(GT, B.bank[2][:, 0:16])
            B.chain(v, "dv", i)
            v.tensor_copy(BT, B.bank[2][:, 16:32]).then_inc(*B.inc("dv"))
        B.guard_bank(2, B.tok("dv"))
        gb_tok = B.tok("dv")

        @B.block.tensor
        def _(t, gb_tok=gb_tok):
            B.wait_bank(t, 2)
            B.wait(t, gb_tok)
            t.matmul(B.bank[2][:, 0:16], B.consts["c_tri_incl"][:], GT,
                     start=True, stop=True).then_inc(*B.inc("pe"))
            t.matmul(B.bank[2][0:1, 16:32], onesc[:], GT,
                     start=True, stop=True).then_inc(*B.inc("pe"))
        cum_tok = B.tok("pe")

        @B.block.vector
        def _(v, cum_tok=cum_tok):
            B.wait(v, cum_tok)
            B.wait(v, B.tok("dv"))
            i = v.tensor_copy(G, B.bank[2][:, 0:16])
            B.chain(v, "dv", i)
            v.tensor_copy(A.GTOT[:], B.bank[2][0:1, 16:32]).then_inc(*B.inc("dv"))
        B.guard_bank(2, B.tok("dv"))
        gc_tok = B.tok("dv")

        @B.block.scalar
        def _(s, gc_tok=gc_tok):
            B.wait(s, gc_tok)
            i = s.activation(LB, BT, AF.Ln)
            B.chain(s, "ac", i)
            s.activation(A.EGR[:], A.GTOT[:], AF.Exp).then_inc(*B.inc("ac"))
        lb_tok = B.tok("ac")

        @B.block.vector
        def _(v, lb_tok=lb_tok):
            B.wait(v, lb_tok)
            B.wait(v, B.tok("dv"))
            i = v.tensor_copy(A.EGRp[0:1, 0:8:2], A.EGR[0:1, 0:16:4])
            B.chain(v, "dv", i)
            i = v.tensor_copy(A.EGRp[0:1, 1:8:2], A.EGR[0:1, 1:16:4])
            B.chain(v, "dv", i)
            i = v.tensor_copy(A.EGRp[0:1, 8:16:2], A.EGR[0:1, 2:16:4])
            B.chain(v, "dv", i)
            v.tensor_copy(A.EGRp[0:1, 9:16:2],
                          A.EGR[0:1, 3:16:4]).then_inc(*B.inc("dv"))
        egp_tok = B.tok("dv")

        @B.block.vector
        def _(v, lb_tok=lb_tok, ln_tok=ln_tok):
            B.wait(v, lb_tok)
            B.wait(v, ln_tok)
            B.wait(v, B.tok("dv"))
            i = v.tensor_scalar(A.LR[:, 0:8], A.LR[:, 0:8], -0.5, None, op0=ALU.mult)
            B.chain(v, "dv", i)
            i = v.tensor_scalar(A.LR[:, 8:16], A.LR[:, 8:16], -0.5,
                                -0.5 * float(np.log(64.0)), op0=ALU.mult, op1=ALU.add)
            B.chain(v, "dv", i)
            i = v.tensor_copy(LRK16[:, 0:16:2], A.LR[:, 0:8])
            B.chain(v, "dv", i)
            i = v.tensor_copy(LRK16[:, 1:16:2], A.LR[:, 0:8])
            B.chain(v, "dv", i)
            i = v.tensor_copy(LRQ16[:, 0:16:2], A.LR[:, 8:16])
            B.chain(v, "dv", i)
            i = v.tensor_copy(LRQ16[:, 1:16:2], A.LR[:, 8:16])
            B.chain(v, "dv", i)
            i = v.tensor_tensor(GKP, G, LRK16, ALU.add)
            B.chain(v, "dv", i)
            i = v.tensor_tensor(GMB, GKP, LB, ALU.add)
            B.chain(v, "dv", i)
            i = v.tensor_tensor(GKM, G, LRK16, ALU.subtract)
            B.chain(v, "dv", i)
            i = v.tensor_scalar(NGKM, GKM, -1.0, None, op0=ALU.mult)
            B.chain(v, "dv", i)
            v.tensor_tensor(GA, G, LRQ16, ALU.add).then_inc(*B.inc("dv"))
        cols_tok = B.tok("dv")

        @B.block.tensor
        def _(t, cols_tok=cols_tok, egp_tok=egp_tok):
            B.wait_bank(t, 2)
            B.wait(t, cols_tok)
            B.wait(t, egp_tok)
            t.matmul(B.bank[2][0:64, 0:16], onesr[0:1, 0:64], A.EGRp[:],
                     start=True, stop=True).then_inc(*B.inc("pe"))
            t.matmul(B.bank[2][:, 16:32], onesr[:], A.GTOT[:],
                     start=True, stop=True).then_inc(*B.inc("pe"))
        eg_tok = B.tok("pe")

        @B.block.vector
        def _(v, eg_tok=eg_tok):
            B.wait(v, eg_tok)
            B.wait(v, B.tok("dv"))
            i = v.tensor_copy(A.EGB[:], B.bank[2][0:64, 0:16])
            B.chain(v, "dv", i)
            v.tensor_tensor(EDEC, B.bank[2][:, 16:32], GKM,
                            ALU.subtract).then_inc(*B.inc("dv"))
        B.guard_bank(2, B.tok("dv"))
        ed_tok = B.tok("dv")

        @B.block.scalar
        def _(s, ed_tok=ed_tok):
            B.wait(s, ed_tok)
            i = s.activation(EDEC, EDEC, AF.Exp)
            B.chain(s, "ac", i)
            i = s.activation(EO, GA, AF.Exp)
            B.chain(s, "ac", i)
            s.activation(EK, GKP, AF.Exp).then_inc(*B.inc("ac"))
        ex_tok = B.tok("ac")

        CBv = B.MF32[:, 4 * TT:5 * TT].bitcast(BF16)
        S5 = B.MF32
        fsets = [
            dict(Mbf=A.Mbf, Mtbf=A.Mtbf, Atbf=A.Atbf, TM=A.TM, TMA=A.TMA2 if False else B.MF32[:, 2 * TT + 384:2 * TT + 512],
                 EMS0=A.EMS[0], EMS1=A.EMS[1]),
            dict(Mbf=CBv[:, 0:128], Mtbf=CBv[:, 128:256], Atbf=CBv[:, 256:384],
                 TM=S5[:, 5 * TT:5 * TT + 128], TMA=S5[:, 5 * TT + 128:5 * TT + 256],
                 EMS0=S5[:, 5 * TT + 256:5 * TT + 384],
                 EMS1=S5[:, 5 * TT + 384:5 * TT + 512]),
        ]
        sigma_free = [None, None]
        tm_free = [None, None]
        et_toks = [None] * 16
        et2_toks = [None] * 16
        gram_toks = [None] * 16

        def head_params(h):
            hp = h // 2
            j = hp // 2
            base = (hp % 2) * 64
            grp = hp % 2
            slot = (hp // 2) * 2 + (h % 2)
            scol = 512 * grp + 64 * slot
            return hp, j, base, grp, slot, scol

        def emit_front(h):
            sg = h % 2
            fs = fsets[sg]
            hp, j, base, grp, slot, scol = head_params(h)
            ksl = A.KBF[j][base:base + 64, :]
            qsl = A.QBF[j][base:base + 64, :]

            @B.block.tensor
            def _(t, ksl=ksl, qsl=qsl, h=h):
                B.wait_bank(t, 0)
                B.wait_bank(t, 1)
                if h == 0:
                    B.wait(t, ex_tok)
                t.matmul(B.bank[0][:, 0:128], ksl, ksl, start=True,
                         stop=True).then_inc(*B.inc("pe"))
                t.matmul(B.bank[1][:, 0:128], ksl, qsl, start=True,
                         stop=True).then_inc(*B.inc("pe"))
            gram_toks[h] = B.tok("pe")

            @B.block.vector
            def _(v, h=h, fs=fs, ft=tm_free[sg]):
                B.wait(v, B.tok("dv"))
                if ft is not None:
                    B.wait(v, ft)
                i = v.tensor_scalar(fs["TM"], B.consts["c_onek_bf"][:],
                                    NGKM[:, h:h + 1], None, op0=ALU.mult)
                B.chain(v, "dv", i)
                v.tensor_scalar(fs["TMA"], B.consts["c_onek_bf"][:],
                                GA[:, h:h + 1], None,
                                op0=ALU.mult).then_inc(*B.inc("dv"))
            p_tok = B.tok("dv")

            @B.block.tensor
            def _(t, p_tok=p_tok, fs=fs):
                B.wait_bank(t, 2)
                B.wait(t, p_tok)
                t.transpose(B.bank[2][:, 0:128], fs["TM"], If32[:]) \
                    .then_inc(*B.inc("pe"))
                t.transpose(B.bank[2][:, 128:256], fs["TMA"], If32[:]) \
                    .then_inc(*B.inc("pe"))
            ptr_tok = B.tok("pe")

            @B.block.vector
            def _(v, ptr_tok=ptr_tok, fs=fs):
                B.wait(v, ptr_tok)
                B.wait(v, B.tok("dv"))
                i = v.tensor_tensor(fs["EMS0"], B.bank[2][:, 0:128],
                                    B.consts["c_pen_m"][:], ALU.add)
                B.chain(v, "dv", i)
                v.tensor_tensor(fs["EMS1"], B.bank[2][:, 128:256],
                                B.consts["c_pen_at"][:],
                                ALU.add).then_inc(*B.inc("dv"))
            B.guard_bank(2, B.tok("dv"))
            pm_tok = B.tok("dv")

            @B.block.scalar
            def _(s, pm_tok=pm_tok, h=h, fs=fs):
                B.wait(s, pm_tok)
                i = s.activation(fs["TM"], fs["EMS0"], AF.Exp,
                                 scale=1.0, bias=GMB[:, h:h + 1])
                B.chain(s, "ac", i)
                s.activation(fs["TMA"], fs["EMS1"], AF.Exp, scale=1.0,
                             bias=NGKM[:, h:h + 1]).then_inc(*B.inc("ac"))
            e_tok = B.tok("ac")

            @B.block.vector
            def _(v, e_tok=e_tok, fs=fs, h=h, sf=sigma_free[sg]):
                B.wait(v, e_tok)
                B.wait(v, gram_toks[h])
                B.wait(v, B.tok("dv"))
                if sf is not None:
                    B.wait(v, sf)
                i = v.tensor_tensor(fs["Mbf"], fs["TM"], B.bank[0][:, 0:128],
                                    ALU.mult)
                B.chain(v, "dv", i)
                v.tensor_tensor(fs["Atbf"], fs["TMA"], B.bank[1][:, 0:128],
                                ALU.mult).then_inc(*B.inc("dv"))
            B.guard_bank(0, B.tok("dv"))
            B.guard_bank(1, B.tok("dv"))
            tm_free[sg] = B.tok("dv")
            et_toks[h] = B.tok("dv")

        def emit_front_b(h):
            sg = h % 2
            fs = fsets[sg]

            @B.block.tensor
            def _(t, et_tok=et_toks[h], fs=fs):
                B.wait_bank(t, 7)
                B.wait(t, et_tok)
                t.transpose(B.bank_bf[:, 0:128], fs["Mbf"], Ibf[:]) \
                    .then_inc(*B.inc("pe"))
            mt_tok = B.tok("pe")

            @B.block.vector
            def _(v, mt_tok=mt_tok, fs=fs):
                B.wait(v, mt_tok)
                B.wait(v, B.tok("dv"))
                v.tensor_copy(fs["Mtbf"], B.bank_bf[:, 0:128]) \
                    .then_inc(*B.inc("dv"))
            B.guard_bank(7, B.tok("dv"))
            et2_toks[h] = B.tok("dv")

        def emit_back(h):
            nonlocal_state = {}
            sg = h % 2
            fs = fsets[sg]
            hp, j, base, grp, slot, scol = head_params(h)
            kfr = A.QKV[4 + j][base:base + 64, csl]
            qfr = A.QKV[j][base:base + 64, csl]
            if grp == 0:
                Ssl = A.S[0:64, scol:scol + 64]
            else:
                Ssl = A.Shi[64:128, 64 * slot:64 * slot + 64]

            zlist = [fs["Mtbf"]] + A.Zs[0:6]
            plist = [fs["Mbf"], A.Pb[0], A.Pb[1], A.Pb[0], A.Pb[1], A.Pb[0]]
            lev_tok = et2_toks[h]
            for jj in range(6):
                @B.block.tensor
                def _(t, jj=jj, lev_tok=lev_tok):
                    B.wait_bank(t, 3)
                    if jj < 5:
                        B.wait_bank(t, 4)
                    B.wait(t, lev_tok)
                    t.matmul(B.bank[3][:, 0:128], plist[jj][:], zlist[jj][:],
                             start=True, stop=True).then_inc(*B.inc("pe"))
                    if jj < 5:
                        t.matmul(B.bank[4][:, 0:128], zlist[jj][:], plist[jj][:],
                                 start=True, stop=True).then_inc(*B.inc("pe"))
                zt = B.tok("pe")

                @B.block.vector
                def _(v, jj=jj, zt=zt):
                    B.wait(v, zt)
                    B.wait(v, B.tok("dv"))
                    i = v.tensor_copy(zlist[jj + 1][:], B.bank[3][:, 0:128])
                    if jj < 5:
                        B.chain(v, "dv", i)
                        i = v.tensor_copy(plist[jj + 1][:], B.bank[4][:, 0:128])
                    i.then_inc(*B.inc("dv"))
                B.guard_bank(3, B.tok("dv"))
                if jj < 5:
                    B.guard_bank(4, B.tok("dv"))
                lev_tok = B.tok("dv")
            z_tok = B.tok("dv")

            @B.block.tensor
            def _(t, kfr=kfr, qfr=qfr, Ssl=Ssl, s_tok=s_tok):
                B.wait_bank(t, 3)
                B.wait_bank(t, 4)
                if s_tok is not None:
                    for st_ in s_tok:
                        B.wait(t, st_)
                t.matmul(B.bank[3][:, 0:64], kfr, Ssl, start=True,
                         stop=True).then_inc(*B.inc("pe"))
                t.matmul(B.bank[4][:, 0:64], qfr, Ssl, start=True,
                         stop=True).then_inc(*B.inc("pe"))
            ks_tok = B.tok("pe")

            @B.block.vector
            def _(v, h=h, ks_tok=ks_tok):
                B.wait(v, ks_tok)
                B.wait(v, B.tok("dv"))
                i = v.tensor_scalar(A.t64[:], B.bank[3][:, 0:64], EK[:, h:h + 1],
                                    None, op0=ALU.mult)
                B.chain(v, "dv", i)
                i = v.tensor_tensor(A.Rf[:], A.vT[:, 64 * h:64 * (h + 1)],
                                    A.t64[:], ALU.subtract)
                B.chain(v, "dv", i)
                v.tensor_scalar(A.Xap[0][:], A.Rf[:], BT[:, h:h + 1], None,
                                op0=ALU.mult).then_inc(*B.inc("dv"))
            B.guard_bank(3, B.tok("dv"))
            r_tok = B.tok("dv")

            xidx = 0
            for jj in range(6, -1, -1):
                @B.block.tensor
                def _(t, jj=jj, xidx=xidx, r_tok=r_tok):
                    B.wait_bank(t, 3)
                    if jj == 6:
                        B.wait(t, r_tok)
                        B.wait(t, z_tok)
                    t.matmul(B.bank[3][:, 0:64], zlist[jj][:], A.Xap[xidx][:],
                             start=True, stop=True).then_inc(*B.inc("pe"))
                ap_tok = B.tok("pe")

                @B.block.vector
                def _(v, jj=jj, xidx=xidx, ap_tok=ap_tok):
                    B.wait(v, ap_tok)
                    B.wait(v, B.tok("dv"))
                    op = ALU.subtract if jj == 0 else ALU.add
                    v.tensor_tensor(A.Xap[1 - xidx][:], A.Xap[xidx][:],
                                    B.bank[3][:, 0:64], op).then_inc(*B.inc("dv"))
                B.guard_bank(3, B.tok("dv"))
                xidx = 1 - xidx
            delta_tok = B.tok("dv")
            Xfin = A.Xap[xidx]

            @B.block.tensor
            def _(t, delta_tok=delta_tok, fs=fs):
                B.wait_bank(t, 3)
                B.wait(t, delta_tok)
                t.matmul(B.bank[3][:, 0:64], fs["Atbf"], Xfin[:],
                         start=True, stop=True).then_inc(*B.inc("pe"))
            o2_tok = B.tok("pe")
            sigma_free[sg] = o2_tok

            @B.block.vector
            def _(v, h=h, o2_tok=o2_tok, hp=hp):
                B.wait(v, o2_tok)
                B.wait(v, B.tok("dv"))
                i = v.tensor_scalar(A.t64[:], B.bank[4][:, 0:64], EO[:, h:h + 1],
                                    None, op0=ALU.mult)
                B.chain(v, "dv", i)
                i = v.tensor_tensor(A.OT[:, 64 * h:64 * (h + 1)], A.t64[:],
                                    B.bank[3][:, 0:64], ALU.add)
                B.chain(v, "dv", i)
                v.tensor_scalar(A.kp[:], A.kT[:, 64 * hp:64 * (hp + 1)],
                                EDEC[:, h:h + 1], None,
                                op0=ALU.mult).then_inc(*B.inc("dv"))
            B.guard_bank(3, B.tok("dv"))
            B.guard_bank(4, B.tok("dv"))
            kp_tok = B.tok("dv")

            sbank = 5 if scol < 512 else 6
            soff = scol % 512
            first_sb = sbank not in seen_sbank
            seen_sbank.add(sbank)

            @B.block.tensor
            def _(t, sbank=sbank, soff=soff, kp_tok=kp_tok, first_sb=first_sb):
                if first_sb:
                    B.wait_bank(t, sbank)
                B.wait(t, kp_tok)
                t.matmul(B.bank[sbank][0:64, soff:soff + 64],
                         A.kp[:], Xfin[:], start=True,
                         stop=True).then_inc(*B.inc("pe"))

        seen_sbank = set()
        emit_front(0)
        emit_front_b(0)
        for h in range(16):
            if h + 1 < 16:
                emit_front(h + 1)
            emit_back(h)
            if h + 1 < 16:
                emit_front_b(h + 1)
        head_done = B.tok("pe")

        @B.block.vector
        def _(v, head_done=head_done):
            B.wait(v, head_done)
            B.wait(v, B.tok("dv"))
            S3 = A.S[:].rearrange("p (h k) -> p h k", k=64)
            i = v.tensor_tensor(S3, A.S[:].bitcast(F32)
                                .rearrange("p (h k) -> p h k", k=64),
                                A.EGB[:].unsqueeze(2).broadcast_to([64, 16, 64]),
                                ALU.mult)
            B.chain(v, "dv", i)
            i = v.tensor_tensor(A.S[0:64, 0:512], A.S[0:64, 0:512].bitcast(F32),
                                B.bank[5][0:64, 0:512], ALU.add)
            B.chain(v, "dv", i)
            v.tensor_tensor(A.S[0:64, 512:1024], A.S[0:64, 512:1024].bitcast(F32),
                            B.bank[6][0:64, 0:512], ALU.add).then_inc(*B.inc("dv"))
        B.guard_bank(5, B.tok("dv"))
        B.guard_bank(6, B.tok("dv"))
        supd_tok = B.tok("dv")

        @B.block.gpsimd
        def _(g, supd_tok=supd_tok):
            B.wait(g, supd_tok)
            g.dma_start(out=A.Shi[64:128, 0:512],
                        in_=A.S[0:64, 512:1024]).then_inc(*B.inc(A.shld, 16))
        s_tok = [supd_tok, B.tok(A.shld)]

        @B.block.vector
        def _(v):
            B.wait(v, B.tok("dv"))
            csc = B.CSC[:, 0:512]
            i = v.tensor_tensor(csc, A.OT[:, 0:512], A.OT[:, 0:512], ALU.mult)
            B.chain(v, "dv", i)
            i = v.tensor_reduce(A.RSO[:, 0:8],
                                csc.rearrange("p (h k) -> p h k", k=64),
                                axis=AX.X, op=ALU.add)
            B.chain(v, "dv", i)
            i = v.tensor_tensor(csc, A.OT[:, 512:1024], A.OT[:, 512:1024], ALU.mult)
            B.chain(v, "dv", i)
            v.tensor_reduce(A.RSO[:, 8:16],
                            csc.rearrange("p (h k) -> p h k", k=64),
                            axis=AX.X, op=ALU.add).then_inc(*B.inc("dv"))
        rs_tok = B.tok("dv")

        @B.block.scalar
        def _(s, rs_tok=rs_tok):
            B.wait(s, rs_tok)
            i = s.activation(A.RSO[:], A.RSO[:], AF.Ln, scale=1.0 / 64.0, bias=eps2)
            B.chain(s, "ac", i)
            s.activation(A.RSO[:], A.RSO[:], AF.Exp, scale=-0.5).then_inc(*B.inc("ac"))
        rso_tok = B.tok("ac")

        @B.block.vector
        def _(v, rso_tok=rso_tok):
            B.wait(v, rso_tok)
            B.wait(v, B.tok("dv"))
            O3 = A.OT[:].rearrange("p (h k) -> p h k", k=64)
            v.tensor_tensor(O3, O3, A.RSO[:].unsqueeze(2).broadcast_to([128, 16, 64]),
                            ALU.mult).then_inc(*B.inc("dv"))
        yn_tok = B.tok("dv")

        for ch in range(8):
            @B.block.tensor
            def _(t, ch=ch, yn_tok=yn_tok):
                B.wait_bank(t, 2)
                B.wait(t, yn_tok)
                t.transpose(B.bank[2][:, 0:128], A.OT[:, 128 * ch:128 * (ch + 1)],
                            If32[:]).then_inc(*B.inc("pe"))
            ytr_tok = B.tok("pe")

            @B.block.vector
            def _(v, ch=ch, csl=csl, ytr_tok=ytr_tok):
                B.wait(v, ytr_tok)
                B.wait(v, B.tok("dv"))
                v.scalar_tensor_tensor(A.YNF[ch][:, csl], B.bank[2][:, 0:128],
                                       A.nwc[:, ch:ch + 1],
                                       A.SZ1[ch][:, csl], op0=ALU.mult,
                                       op1=ALU.mult).then_inc(*B.inc("dv"))
            B.guard_bank(2, B.tok("dv"))
        ynf_toks.append(B.tok("dv"))
    return ynf_toks[-1], s_tok


def emit_l1attn_full(B, A, ws, xn_ch, xn_toks, d, tile, s_tok_in):
    qkv_toks_pre = emit_l1attn(B, A, ws, xn_ch, xn_toks, None, d, tile)
    gg_tok = A._gg_tok
    bet_tok = A._bet_tok
    ynf_tok, s_tok = emit_l1attn_gdn(B, A, qkv_toks_pre, gg_tok, bet_tok, s_tok_in)

    y_toks = []
    for o in range(NCH):
        ws.load(d["W1oT"][o])

        @B.block.tensor
        def _(t, o=o, ynf_tok=ynf_tok):
            slab = ws.use(t)
            bk = o % 2
            B.wait_bank(t, bk)
            if o == 0:
                B.wait(t, ynf_tok)
            for ch in range(8):
                mm = t.matmul(B.bank[bk][:, 0:TT], slab[:, ch * 128:(ch + 1) * 128],
                              A.YNF[ch][:], start=(ch == 0), stop=(ch == 7),
                              skip_group_check=True)
            mm.then_inc(*B.inc("pe"))
            ws.done(B.tok("pe"))
        mm_tok = B.tok("pe")

        @B.block.scalar
        def _(s, o=o, mm_tok=mm_tok):
            B.wait(s, mm_tok)
            s.copy(A.PRE[o][:], B.bank[o % 2][:, 0:TT]).then_inc(*B.inc("ac"))
        B.guard_bank(o % 2, B.tok("ac"))
        y_toks.append(B.tok("ac"))
    return [p[:] for p in A.PRE], y_toks, s_tok
from contextlib import ExitStack
from concourse.bass_utils import run_bass_kernel_spmd

CN_ALL = ("c_ones_row", "c_ones_col", "c_eps", "c_ident", "c_ident_bfr",
          "c_tri_incl", "c_pen_m", "c_pen_at",
          "c_onescol_f32", "c_onesrow_f32", "c_onek_bf", "c_zeros_r",
          "c_l0exp")

NTILES = 4


def _bf(a):
    import ml_dtypes
    return np.asarray(a).astype(ml_dtypes.bfloat16)


def _weights_l0a(z):
    return {
        "WxT": _bf(tile_lhsT_by_o(np.ascontiguousarray(z["L0_attn_W_in_x"].T))),
        "WzT": _bf(tile_lhsT_by_o(np.ascontiguousarray(z["L0_attn_W_in_z"].T))),
        "WpT": _bf(tile_lhsT_by_o(np.ascontiguousarray(z["L0_attn_W_param"].T))),
        "WoT": _bf(tile_lhsT_by_o(np.ascontiguousarray(z["L0_attn_W_out"].T))),
        "l0_wdt": np.ascontiguousarray(z["L0_attn_W_dt"].T),
        "l0_cw": np.ascontiguousarray(
            z["L0_attn_conv_w"][:, 0, :].reshape(4, 128, 4).transpose(1, 0, 2)
            .reshape(128, 16)),
        "l0_cb": np.ascontiguousarray(z["L0_attn_conv_b"].reshape(4, 128).T),
        "l0_alog": np.ascontiguousarray(z["L0_attn_A_log"].reshape(512)
                                        .reshape(4, 128).T),
        "l0_dcol": np.ascontiguousarray(z["L0_attn_D"].reshape(4, 128).T),
        "l0_bdt": np.ascontiguousarray(z["L0_attn_b_dt"].reshape(32, 1)),
    }


W_L0A_DT = {"WxT": BF16, "WzT": BF16, "WpT": BF16, "WoT": BF16,
            "l0_wdt": F32R, "l0_cw": F32, "l0_cb": F32, "l0_alog": F32,
            "l0_dcol": F32, "l0_bdt": F32}
W_L0A_SHP = {"WxT": [4, 128, 2048], "WzT": [4, 128, 2048], "WpT": [9, 128, 512],
             "WoT": [16, 128, 512], "l0_wdt": [128, 32], "l0_cw": [128, 16],
             "l0_cb": [128, 4], "l0_alog": [128, 4], "l0_dcol": [128, 4],
             "l0_bdt": [32, 1]}


def _weights_l0f(z):
    return {
        "Wg": _bf(tile_lhsT_by_o(np.ascontiguousarray(z["L0_ffn_sh_gate"].T))),
        "Wu": _bf(tile_lhsT_by_o(np.ascontiguousarray(z["L0_ffn_sh_up"].T))),
        "Wd": _bf(tile_lhsT_by_k(np.ascontiguousarray(z["L0_ffn_sh_down"].T))),
        "Rg": _bf(tile_lhsT_by_o(np.ascontiguousarray(z["L0_ffn_r_gate"].T))),
        "Ru": _bf(tile_lhsT_by_o(np.ascontiguousarray(z["L0_ffn_r_up"].T))),
        "Rd": _bf(tile_lhsT_by_k(np.ascontiguousarray(z["L0_ffn_r_down"].T))),
        "l0f_wrt": _bf(np.ascontiguousarray(
            z["L0_ffn_router_weight"].T.reshape(16, 128, 8)
            .transpose(1, 0, 2).reshape(128, 128))),
    }


W_L0F_SHP = {"Wg": [64, 128, 2048], "Wu": [64, 128, 2048], "Wd": [64, 128, 2048],
             "Rg": [8, 128, 2048], "Ru": [8, 128, 2048], "Rd": [8, 128, 2048],
             "l0f_wrt": [128, 128]}
W_L0F_DT = {n: BF16 for n in W_L0F_SHP}


def _weights_l1a(z):
    return {
        "W1qkvT": _bf(tile_lhsT_by_o(np.ascontiguousarray(z["L1_attn_W_qkv"].T))),
        "W1zT": _bf(tile_lhsT_by_o(np.ascontiguousarray(z["L1_attn_W_z"].T))),
        "W1oT": _bf(tile_lhsT_by_o(np.ascontiguousarray(z["L1_attn_W_out"].T))),
        "a1d_wbt": _bf(np.ascontiguousarray(
            z["L1_attn_W_b"].T.reshape(16, 128, 16).transpose(1, 0, 2).reshape(128, 256))),
        "a1d_wat": _bf(np.ascontiguousarray(
            z["L1_attn_W_a"].T.reshape(16, 128, 16).transpose(1, 0, 2).reshape(128, 256))),
        "a1d_cw1": np.ascontiguousarray(
            z["L1_attn_conv_w"][:, 0, :].reshape(16, 128, 4).transpose(1, 0, 2)
            .reshape(128, 64)),
        "a1d_alog": np.ascontiguousarray(z["L1_attn_A_log"].reshape(16, 1)),
        "a1d_dtb": np.ascontiguousarray(z["L1_attn_dt_bias"].reshape(16, 1)),
        "a1d_nwc": np.ascontiguousarray(
            np.tile(z["L1_attn_norm_weight"], 16).reshape(8, 128).T),
    }


W_L1A_DT = {"W1qkvT": BF16, "W1zT": BF16, "W1oT": BF16, "a1d_wbt": BF16,
            "a1d_wat": BF16, "a1d_cw1": F32, "a1d_alog": F32, "a1d_dtb": F32,
            "a1d_nwc": F32}
W_L1A_SHP = {"W1qkvT": [16, 128, 2048], "W1zT": [8, 128, 2048],
             "W1oT": [16, 128, 1024], "a1d_wbt": [128, 256],
             "a1d_wat": [128, 256], "a1d_cw1": [128, 64], "a1d_alog": [16, 1],
             "a1d_dtb": [16, 1], "a1d_nwc": [128, 8]}


def _weights_l1f(z):
    W = {"WsGU": _bf(tile_lhsT_by_o(np.ascontiguousarray(z["L1_ffn_Ws_gate_up"]))),
         "WsD": _bf(tile_lhsT_by_k(np.ascontiguousarray(z["L1_ffn_Ws_down"])))}
    for e in range(8):
        W[f"WeGU{e}"] = _bf(tile_lhsT_by_o(np.ascontiguousarray(z["L1_ffn_We_gate_up"][e])))
        W[f"WeD{e}"] = _bf(tile_lhsT_by_k(np.ascontiguousarray(z["L1_ffn_We_down"][e])))
    W["l1f_wrt"] = _bf(np.ascontiguousarray(
        z["L1_ffn_W_router"].T.reshape(16, 128, 8).transpose(1, 0, 2).reshape(128, 128)))
    W["l1f_wsgt"] = _bf(np.ascontiguousarray(
        z["L1_ffn_W_sgate"].T.reshape(16, 128).T))
    return W


W_L1F_SHP = {"WsGU": [88, 128, 2048], "WsD": [44, 128, 2048],
             "l1f_wrt": [128, 128], "l1f_wsgt": [128, 16]}
for _e in range(8):
    W_L1F_SHP[f"WeGU{_e}"] = [22, 128, 2048]
    W_L1F_SHP[f"WeD{_e}"] = [11, 128, 2048]
W_L1F_DT = {n: BF16 for n in W_L1F_SHP}


def _std_frame(nc, wshapes, wdt, consts_np, cn):
    x_in = nc.dram_tensor("x_in", [NTILES, 128, NCH * TT], F32R,
                          kind="ExternalInput").ap()
    x_out = nc.dram_tensor("x_out", [NTILES, 128, NCH * TT], F32R,
                           kind="ExternalOutput").ap()
    d = {n: nc.dram_tensor(n, wshapes[n], wdt.get(n, F32R),
                           kind="ExternalInput").ap() for n in wshapes}
    dconsts = {n: nc.dram_tensor(n, list(consts_np[n].shape), CONST_DTYPES[n],
                                 kind="ExternalInput").ap() for n in cn}
    return x_in, x_out, d, dconsts


def _frame_body(stack, nc, cn, dconsts, consts_np):
    bb = B(nc, stack)
    staged = [n for n in cn if n not in ("c_zeros_r", "c_l0exp")]
    for name in staged:
        h = bb.sb("s_" + name, list(consts_np[name].shape), CONST_DTYPES[name])
        bb.consts[name] = h

    @bb.block.gpsimd
    def _(g):
        for name in staged:
            g.dma_start(out=bb.consts[name][:], in_=dconsts[name]) \
                .then_inc(*bb.inc("ld", 16))

    xt = bb.sb("xt", [P, NCH * TT], F32R)
    xnt = bb.sb("xnt", [P, NCH * TT], F32R)
    yt = bb.sb("yt", [P, NCH * TT], F32R)
    xnb = bb.sb("xnb", [P, NCH * TT], BF16)
    x_ch = [xt[:, k * TT:(k + 1) * TT] for k in range(NCH)]
    xn_ch = [xnt[:, k * TT:(k + 1) * TT] for k in range(NCH)]
    y_ch = [yt[:, k * TT:(k + 1) * TT] for k in range(NCH)]
    xnb_ch = [xnb[:, k * TT:(k + 1) * TT] for k in range(NCH)]
    L = LNBufs(bb)
    return bb, xt, x_ch, xn_ch, xnb_ch, y_ch, L


_CN_BASE = ("c_ones_row", "c_ones_col", "c_eps")


def _per_tile_io(bb, xt, tile, x_in, extra_tok=None):
    @bb.block.gpsimd
    def _(g, tile=tile):
        if tile > 0:
            bb.wait(g, bb.tok("st"))
        elif extra_tok is not None:
            bb.wait(g, extra_tok)
        g.dma_start(out=xt[:, 0:NCH * TT], in_=x_in[tile]) \
            .then_inc(*bb.inc("ld", 16))
    return bb.tok("ld")


def _store_tile(bb, xt, tile, x_out, res_toks):
    @bb.block.gpsimd
    def _(g, tile=tile):
        bb.wait(g, res_toks[-1])
        g.dma_start(out=x_out[tile], in_=xt[:, 0:NCH * TT]) \
            .then_inc(*bb.inc("st", 16))


def _finish(bb):
    @bb.block.gpsimd
    def _(g):
        g.wait_ge(bb.sem["st"], bb.cnt["st"])


def build_l0a(consts_np):
    cn = _CN_BASE + ("c_l0exp",)
    nc = bass.Bass("TRN2", target_bir_lowering=False)
    x_in, x_out, d, dconsts = _std_frame(nc, W_L0A_SHP, W_L0A_DT, consts_np, cn)
    with ExitStack() as stack:
        bb, xt, x_ch, xn_ch, xnb_ch, y_ch, L = _frame_body(stack, nc, cn, dconsts, consts_np)
        L0 = L0Bufs(bb, L)
        ws = WStream(bb)
        d = dict(d)
        d["c_l0exp_d"] = dconsts["c_l0exp"]
        ldt, dvt = emit_l0attn_consts(bb, L0, d)
        for engd in (bb.block.vector, bb.block.scalar, bb.block.tensor):
            @engd
            def _(e):
                bb.wait(e, ldt)
                bb.wait(e, dvt)
                bb.wait(e, A._sz_tok)
        for tile in range(NTILES):
            ld_tok = _per_tile_io(bb, xt, tile, x_in, ldt)
            xn_toks = emit_ln_xn(bb, L, x_ch, xnb_ch, scr_ch=xn_ch, pre_toks=[ld_tok] * NCH)
            a_toks = emit_l0attn(bb, L0, ws, xnb_ch, xn_toks, y_ch, d, tile)
            res_toks = emit_ln_residual(bb, L, y_ch, x_ch, pre_toks=a_toks)
            _store_tile(bb, xt, tile, x_out, res_toks)
        _finish(bb)
    return nc


def build_l0f(consts_np):
    cn = _CN_BASE + ("c_ident",)
    nc = bass.Bass("TRN2", target_bir_lowering=False)
    x_in, x_out, d, dconsts = _std_frame(nc, W_L0F_SHP, W_L0F_DT, consts_np, cn)
    with ExitStack() as stack:
        bb, xt, x_ch, xn_ch, xnb_ch, y_ch, L = _frame_body(stack, nc, cn, dconsts, consts_np)
        ws = WStream(bb)
        sgbufs = [bb.mf(6), bb.mf(7)]
        _asl = [0, 1, 2, 3, 8, 10, 11, 12]
        asup = [bb.mf(_asl[i // 2]).bitcast(BF16)[:, (i % 2) * TT:
                (i % 2 + 1) * TT] for i in range(16)]
        R = L0FRBufs(bb, L)
        emit_l0fr_consts(bb, R, d)
        const_tok = bb.tok("ld")
        for tile in range(NTILES):
            ld_tok = _per_tile_io(bb, xt, tile, x_in, const_tok)
            xn_toks = emit_ln_xn(bb, L, x_ch, xnb_ch, scr_ch=xn_ch, pre_toks=[ld_tok] * NCH)
            cb_tok = emit_l0ffn_router(bb, R, xnb_ch, xn_toks)
            emit_swiglu_mlp(bb, ws, xnb_ch, xn_toks, y_ch, d["Wg"], d["Wu"],
                            d["Wd"], 64, sgbufs, asup, accumulate=False)
            f_tok = emit_swiglu_mlp(bb, ws, xnb_ch, xn_toks, y_ch, d["Rg"],
                                    d["Ru"], d["Rd"], 8, sgbufs, asup,
                                    accumulate=True, scale_tile=R.CB0[:],
                                    scale_tok=cb_tok)
            res_toks = emit_ln_residual(bb, L, y_ch, x_ch, pre_toks=[f_tok] * NCH)
            _store_tile(bb, xt, tile, x_out, res_toks)
        _finish(bb)
    return nc


def build_l1a(consts_np):
    cn = _CN_BASE + ("c_ident", "c_ident_bfr", "c_tri_incl", "c_pen_m",
                     "c_pen_at", "c_onescol_f32", "c_onesrow_f32",
                     "c_onek_bf", "c_zeros_r")
    nc = bass.Bass("TRN2", target_bir_lowering=False)
    x_in, x_out, d, dconsts = _std_frame(nc, W_L1A_SHP, W_L1A_DT, consts_np, cn)
    with ExitStack() as stack:
        bb, xt, x_ch, xn_ch, xnb_ch, y_ch, L = _frame_body(stack, nc, cn, dconsts, consts_np)
        A = L1ABufs(bb, y_ch, xn_ch)
        ws = WStream(bb)
        d2 = dict(d)
        d2["c_zeros_r"] = dconsts["c_zeros_r"]
        ldt, dvt = emit_l1attn_consts(bb, A, d2)
        for engd in (bb.block.vector, bb.block.scalar, bb.block.tensor):
            @engd
            def _(e):
                bb.wait(e, ldt)
                bb.wait(e, dvt)
                bb.wait(e, A._sz_tok)
        s_tok = [dvt]
        for tile in range(NTILES):
            ld_tok = _per_tile_io(bb, xt, tile, x_in, ldt)
            xn_toks = emit_ln_xn(bb, L, x_ch, xnb_ch, scr_ch=xn_ch, pre_toks=[ld_tok] * NCH)
            y_list, y_toks, s_tok = emit_l1attn_full(bb, A, ws, xnb_ch, xn_toks,
                                                     d2, tile, s_tok)
            res_toks = emit_ln_residual(bb, L, y_list, x_ch, pre_toks=y_toks)
            _store_tile(bb, xt, tile, x_out, res_toks)
        _finish(bb)
    return nc


def build_l1f(consts_np):
    cn = _CN_BASE + ("c_ident",)
    nc = bass.Bass("TRN2", target_bir_lowering=False)
    x_in, x_out, d, dconsts = _std_frame(nc, W_L1F_SHP, W_L1F_DT, consts_np, cn)
    with ExitStack() as stack:
        bb, xt, x_ch, xn_ch, xnb_ch, y_ch, L = _frame_body(stack, nc, cn, dconsts, consts_np)
        F = L1FBufs(bb, L)
        ws = WStream(bb)
        sgbufs = [bb.mf(6), bb.mf(7)]
        _asl = [0, 1, 2, 3, 8, 10, 11, 12]
        asup = [bb.mf(_asl[i // 2]).bitcast(BF16)[:, (i % 2) * TT:
                (i % 2 + 1) * TT] for i in range(16)]
        ldt = emit_l1ffn_consts(bb, F, d)
        for engd in (bb.block.vector, bb.block.scalar, bb.block.tensor):
            @engd
            def _(e):
                bb.wait(e, ldt)
        for tile in range(NTILES):
            ld_tok = _per_tile_io(bb, xt, tile, x_in, ldt)
            xn_toks = emit_ln_xn(bb, L, x_ch, xnb_ch, scr_ch=xn_ch, pre_toks=[ld_tok] * NCH)
            f_tok = emit_l1ffn(bb, F, ws, xnb_ch, xn_toks, y_ch, d, sgbufs,
                               asup, tile)
            res_toks = emit_ln_residual(bb, L, y_ch, x_ch, pre_toks=[f_tok] * NCH)
            _store_tile(bb, xt, tile, x_out, res_toks)
        _finish(bb)
    return nc


def _to_tiles(xb):
    xf = np.ascontiguousarray(xb.T)
    return np.stack([
        xf[:, t * TT:(t + 1) * TT].reshape(NCH, 128, TT).transpose(1, 0, 2)
        .reshape(128, NCH * TT) for t in range(NTILES)])


def _from_tiles(out):
    return np.concatenate(
        [out[t].reshape(128, NCH, TT).transpose(1, 0, 2).reshape(2048, TT)
         for t in range(NTILES)], axis=1).T


def kernel_unfused(**inputs):
    z = inputs
    consts_np = host_consts()
    x = np.asarray(z["L0_attn_hidden_states"], dtype=np.float32)
    cores = list(range(8))
    xs = [_to_tiles(x[b]) for b in range(8)]

    launches = [
        (build_l0a, _weights_l0a, _CN_BASE + ("c_l0exp",)),
        (build_l0f, _weights_l0f, _CN_BASE),
        (build_l1a, _weights_l1a, _CN_BASE + ("c_ident", "c_ident_bfr",
                                              "c_tri_incl", "c_pen_m",
                                              "c_pen_at",
                                              "c_onescol_f32", "c_onesrow_f32",
                                              "c_onek_bf", "c_zeros_r")),
        (build_l1f, _weights_l1f, _CN_BASE + ("c_ident",)),
    ]
    for build_fn, wfn, cn in launches:
        W = wfn(z)
        nc = build_fn(consts_np)
        base = dict(W)
        for n in cn:
            base[n] = consts_np[n]
        in_maps = [dict(base, x_in=xs[b]) for b in range(8)]
        res = run_bass_kernel_spmd(nc, in_maps, core_ids=cores)
        xs = [np.asarray(res.results[b]["x_out"]) for b in range(8)]

    out = np.stack([_from_tiles(xs[b]) for b in range(8)]).astype(np.float32)
    return out



def build_fused(consts_np):
    cn = ("c_ones_row", "c_ones_col", "c_eps", "c_ident", "c_ident_bfr",
          "c_tri_incl", "c_pen_m", "c_pen_at", "c_onescol_f32",
          "c_onesrow_f32", "c_onek_bf", "c_zeros_r", "c_l0exp")
    nc = bass.Bass("TRN2", target_bir_lowering=False)
    wshapes = {}
    wshapes.update(W_L0A_SHP)
    wshapes.update(W_L0F_SHP)
    wshapes.update(W_L1A_SHP)
    wshapes.update(W_L1F_SHP)
    wdt = {}
    wdt.update(W_L0A_DT)
    wdt.update(W_L1A_DT)
    wdt.update(W_L0F_DT)
    wdt.update(W_L1F_DT)
    x_in, x_out, d, dconsts = _std_frame(nc, wshapes, wdt, consts_np, cn)
    with ExitStack() as stack:
        bb, xt, x_ch, xn_ch, xnb_ch, y_ch, L = _frame_body(stack, nc, cn, dconsts, consts_np)
        L0 = L0Bufs(bb, L)
        A = L1ABufs(bb, y_ch, xn_ch)
        F = L1FBufs(bb, L)
        R = L0FRBufs(bb, L)
        ws = WStream(bb)
        sgbufs = [bb.mf(6), bb.mf(7)]
        _asl = [0, 1, 2, 3, 8, 10, 11, 12]
        asup = [bb.mf(_asl[i // 2]).bitcast(BF16)[:, (i % 2) * TT:
                (i % 2 + 1) * TT] for i in range(16)]
        d2 = dict(d)
        d2["c_zeros_r"] = dconsts["c_zeros_r"]
        d2["c_l0exp_d"] = dconsts["c_l0exp"]
        emit_l0fr_consts(bb, R, d2)
        ldt0, dvt0 = emit_l0attn_consts(bb, L0, d2)
        ldt1, dvt1 = emit_l1attn_consts(bb, A, d2)
        ldt2 = emit_l1ffn_consts(bb, F, d2)
        for engd in (bb.block.vector, bb.block.scalar, bb.block.tensor):
            @engd
            def _(e):
                bb.wait(e, ldt2)
                bb.wait(e, dvt0)
                bb.wait(e, dvt1)
                bb.wait(e, A._sz_tok)
        s_tok = [dvt1]
        for tile in range(NTILES):
            ld_tok = _per_tile_io(bb, xt, tile, x_in, ldt2)
            xn_toks = emit_ln_xn(bb, L, x_ch, xnb_ch, scr_ch=xn_ch, pre_toks=[ld_tok] * NCH)
            a_toks = emit_l0attn(bb, L0, ws, xnb_ch, xn_toks, y_ch, d2, tile)
            res_toks = emit_ln_residual(bb, L, y_ch, x_ch, pre_toks=a_toks)
            xn_toks = emit_ln_xn(bb, L, x_ch, xnb_ch, scr_ch=xn_ch, pre_toks=res_toks)
            cb_tok = emit_l0ffn_router(bb, R, xnb_ch, xn_toks)
            emit_swiglu_mlp(bb, ws, xnb_ch, xn_toks, y_ch, d2["Wg"], d2["Wu"],
                            d2["Wd"], 64, sgbufs, asup, accumulate=False)
            f_tok = emit_swiglu_mlp(bb, ws, xnb_ch, xn_toks, y_ch, d2["Rg"],
                                    d2["Ru"], d2["Rd"], 8, sgbufs, asup,
                                    accumulate=True, scale_tile=R.CB0[:],
                                    scale_tok=cb_tok)
            res_toks = emit_ln_residual(bb, L, y_ch, x_ch, pre_toks=[f_tok] * NCH)
            xn_toks = emit_ln_xn(bb, L, x_ch, xnb_ch, scr_ch=xn_ch, pre_toks=res_toks)
            y_list, y_toks, s_tok = emit_l1attn_full(bb, A, ws, xnb_ch, xn_toks,
                                                     d2, tile, s_tok)
            res_toks = emit_ln_residual(bb, L, y_list, x_ch, pre_toks=y_toks)
            xn_toks = emit_ln_xn(bb, L, x_ch, xnb_ch, scr_ch=xn_ch, pre_toks=res_toks)
            f_tok = emit_l1ffn(bb, F, ws, xnb_ch, xn_toks, y_ch, d2, sgbufs,
                               asup, tile)
            res_toks = emit_ln_residual(bb, L, y_ch, x_ch, pre_toks=[f_tok] * NCH)
            _store_tile(bb, xt, tile, x_out, res_toks)
        _finish(bb)
    return nc


def kernel(**inputs):
    z = inputs
    consts_np = host_consts()
    x = np.asarray(z["L0_attn_hidden_states"], dtype=np.float32)
    xs = [_to_tiles(x[b]) for b in range(8)]
    base = {}
    for wfn in (_weights_l0a, _weights_l0f, _weights_l1a, _weights_l1f):
        base.update(wfn(z))
    base.update({n: consts_np[n] for n in
                 ("c_ones_row", "c_ones_col", "c_eps", "c_ident", "c_ident_bfr",
                  "c_tri_incl", "c_pen_m", "c_pen_at",
                  "c_onescol_f32", "c_onesrow_f32", "c_onek_bf", "c_zeros_r",
                  "c_l0exp")})
    nc = build_fused(consts_np)
    in_maps = [dict(base, x_in=xs[b]) for b in range(8)]
    res = run_bass_kernel_spmd(nc, in_maps, core_ids=list(range(8)))
    out = np.stack([_from_tiles(np.asarray(res.results[b]["x_out"]))
                    for b in range(8)]).astype(np.float32)
    return out


# version 14 (final)
# speedup vs baseline: 1.1520x; 1.0091x over previous
import sys
sys.path.insert(0, "/opt/trn_rl_repo")
import numpy as np
import concourse.bass as bass
import concourse.mybir as mybir

F32 = mybir.dt.float32
F32R = mybir.dt.float32r
BF16 = mybir.dt.bfloat16
AF = mybir.ActivationFunctionType
ALU = mybir.AluOpType
AX = mybir.AxisListType

P = 128
D = 2048
NCH = 16
TT = 512
LN_EPS = 1e-5



def tile_lhsT_by_o(WT):
    K, O = WT.shape
    Kc, Oc = K // 128, O // 128
    return np.ascontiguousarray(
        WT.reshape(Kc, 128, Oc, 128).transpose(2, 1, 0, 3).reshape(Oc, 128, Kc * 128))


def tile_lhsT_by_k(WT):
    K = WT.shape[0]
    return np.ascontiguousarray(WT.reshape(K // 128, 128, WT.shape[1]))


def host_consts():
    import ml_dtypes
    i = np.arange(128)
    E = np.zeros((32, 512), np.float32)
    E[np.arange(512) // 16, np.arange(512)] = 1.0
    return {
        "c_ones_row": np.ones((1, 128), np.float32),
        "c_ones_col": np.ones((128, 1), np.float32),
        "c_ident": np.eye(128, dtype=np.float32),
        "c_ident_bf": np.eye(128, dtype=np.float32),
        "c_tri_incl": (i[:, None] <= i[None, :]).astype(np.float32),
        "c_pen_m": (-1e4 * (i[:, None] <= i[None, :])).astype(ml_dtypes.bfloat16),
        "c_pen_at": (-1e4 * (i[:, None] > i[None, :])).astype(ml_dtypes.bfloat16),
        "c_l0exp": E,
        "c_eps": np.tile(np.array([1e-5, 1e-6, 1.0, 0.0], np.float32), (128, 1)),
        "c_onescol_f32": np.ones((128, 1), np.float32),
        "c_onesrow_f32": np.ones((1, 128), np.float32),
        "c_ident_bfr": np.eye(128).astype(ml_dtypes.bfloat16),
        "c_onek_bf": np.ones((128, 128)).astype(ml_dtypes.bfloat16),
        "c_zeros_r": np.zeros((128, 512), np.float32),
    }


CONST_DTYPES = {
    "c_ones_row": F32R, "c_ones_col": F32R, "c_ident": F32,
    "c_ident_bf": F32, "c_tri_incl": F32, "c_pen_m": BF16,
    "c_pen_at": BF16, "c_l0exp": F32R, "c_eps": F32,
    "c_onescol_f32": F32, "c_onesrow_f32": F32, "c_ident_bfr": BF16,
    "c_onek_bf": BF16, "c_zeros_r": F32R,
}



class B:
    def __init__(self, nc, stack):
        self.nc = nc
        self.stack = stack
        self.sem = {}
        self.cnt = {}
        for name in ("ld", "wld", "wcred", "pe", "dv", "ac", "po", "st"):
            self.sem[name] = stack.enter_context(nc.semaphore(name))
            self.cnt[name] = 0

        self.block = stack.enter_context(nc.Block())
        self.MF32 = stack.enter_context(nc.sbuf_tensor("MF32", [128, 16 * TT], F32))
        self.MR32 = stack.enter_context(nc.sbuf_tensor("MR32", [128, 16 * TT], F32R))
        self.MBF = stack.enter_context(nc.sbuf_tensor("MBF", [128, 5696], BF16))
        self.bank = [stack.enter_context(nc.psum_tensor(f"bank{i}", [128, 512], F32))
                     for i in range(7)]
        self.bank_bf = stack.enter_context(nc.psum_tensor("bankbf", [128, 512], BF16))
        self.bank_guard = [None] * 8
        self.consts = {}
        self.SPR = self.MF32[0:32, 6 * TT:8 * TT]

    def sb(self, name, shape, dtype=F32):
        return self.stack.enter_context(self.nc.sbuf_tensor(name, shape, dtype))

    def mf(self, s0, w=1):
        return self.MF32[:, s0 * TT:(s0 + w) * TT]

    @property
    def CSC(self):
        return self.MF32[:, 12 * TT:12 * TT + TT + 16]

    def mr(self, s0, w=1):
        return self.MR32[:, s0 * TT:(s0 + w) * TT]

    def new_sem(self, name):
        self.sem[name] = self.stack.enter_context(self.nc.semaphore(name))
        self.cnt[name] = 0
        return name

    def inc(self, name, n=1):
        self.cnt[name] += n
        return self.sem[name], n

    def tok(self, name):
        return (self.sem[name], self.cnt[name])

    def wait(self, eng, token):
        sem, val = token
        if val > 0:
            eng.wait_ge(sem, val)

    def chain(self, eng, name, instr):
        instr.then_inc(*self.inc(name))
        eng.wait_ge(self.sem[name], self.cnt[name])

    def wait_bank(self, eng, bi):
        t = self.bank_guard[bi]
        if t is not None:
            self.wait(eng, t)
            self.bank_guard[bi] = None

    def guard_bank(self, bi, token):
        self.bank_guard[bi] = token


class WStream:

    def __init__(self, B, name="wbuf", width=2048, nbuf=2):
        self.B = B
        self.width = width
        self.nbuf = nbuf
        self.bufs = [B.sb(f"{name}{i}", [128, width], BF16) for i in range(nbuf)]
        self.sems = [B.new_sem(f"{name}_s{i}") for i in range(nbuf)]
        self.gsems = [B.new_sem(f"{name}_g{i}") for i in range(nbuf)]
        self.free_tok = [None] * nbuf
        self.n_loaded = 0
        self.n_used = 0

    def load(self, src_ap):
        bb = self.B
        i = self.n_loaded
        self.n_loaded += 1
        w = src_ap.shape[-1]
        buf = self.bufs[i % self.nbuf]
        sname = self.sems[i % self.nbuf]
        free_tok = self.free_tok[i % self.nbuf]

        def body(q):
            if free_tok is not None:
                bb.wait(q, free_tok)
            q.dma_start(out=buf[:, 0:w], in_=src_ap).then_inc(*bb.inc(sname, 16))

        if i % 2 == 0:
            @bb.block.sync
            def _(sync):
                body(sync)
        else:
            @bb.block.gpsimd
            def _(g):
                body(g)
        return None

    def load_view(self, src_ap, w):
        bb = self.B
        i = self.n_loaded
        self.n_loaded += 1
        buf = self.bufs[i % self.nbuf]
        sname = self.sems[i % self.nbuf]
        gname = self.gsems[i % self.nbuf]
        free_tok = self.free_tok[i % self.nbuf]
        a = src_ap.shape[1]
        dst = buf[:, 0:w].rearrange("p (a q) -> p a q", a=a)

        def body(q):
            if free_tok is not None:
                bb.wait(q, free_tok)
            q.dma_start(out=dst, in_=src_ap).then_inc(*bb.inc(sname, 16))

        if i % 2 == 0:
            @bb.block.sync
            def _(sync):
                body(sync)
        else:
            @bb.block.gpsimd
            def _(g):
                body(g)

    def use(self, eng):
        i = self.n_used
        eng.wait_ge(self.B.sem[self.sems[i % self.nbuf]],
                    16 * (i // self.nbuf + 1))
        return self.bufs[i % self.nbuf]

    def done(self, token):
        self.free_tok[self.n_used % self.nbuf] = token
        self.n_used += 1



class LNBufs:

    def __init__(self, B):
        self.sq = [B.mr(12 + i) for i in range(4)]
        self.rows = B.MR32[0:1, 12 * TT:14 * TT]
        self.m_row = B.MF32[0:1, 2 * TT + 384:2 * TT + 896]
        self.q_row = B.MF32[0:1, 2 * TT + 896:2 * TT + 1408]
        self.rsb = B.mf(13).bitcast(F32)
        self.mrb = B.mf(14).bitcast(F32)
        self.t1 = B.mf(15).bitcast(F32)


def emit_ln_core(B, L, x_ch, tag, pre_toks=None):
    nc = B.nc
    ones_c = B.consts["c_ones_col"]
    bsum, bsq = B.bank[4], B.bank[5]
    pe0 = B.cnt["pe"]

    for k in range(NCH):
        slab = L.sq[k % 4]

        @B.block.vector
        def _(v, k=k, slab=slab):
            if pre_toks is not None:
                B.wait(v, pre_toks[k])
            if k >= 4:
                v.wait_ge(B.sem["pe"], pe0 + 2 * (k - 4) + 2)
            v.tensor_tensor(slab[:], x_ch[k].bitcast(F32), x_ch[k].bitcast(F32),
                            ALU.mult).then_inc(*B.inc("dv"))
        sq_tok = B.tok("dv")

        @B.block.tensor
        def _(t, k=k, slab=slab, sq_tok=sq_tok):
            if k == 0:
                B.wait_bank(t, 4)
                B.wait_bank(t, 5)
            B.wait(t, sq_tok)
            t.matmul(bsum[0:1, 0:TT], ones_c[:], x_ch[k],
                     start=(k == 0), stop=(k == NCH - 1),
                     skip_group_check=True).then_inc(*B.inc("pe"))
            t.matmul(bsq[0:1, 0:TT], ones_c[:], slab[:],
                     start=(k == 0), stop=(k == NCH - 1),
                     skip_group_check=True).then_inc(*B.inc("pe"))
    stats_tok = B.tok("pe")

    m_row = L.m_row
    q_row = L.q_row
    rs_row = L.rows[0:1, 0:TT]
    mr_row = L.rows[0:1, TT:2 * TT]

    @B.block.vector
    def _(v):
        B.wait(v, stats_tok)
        i1 = v.tensor_scalar(m_row, bsum[0:1, 0:TT], 1.0 / D, None, op0=ALU.mult)
        i2 = v.tensor_scalar(q_row, bsq[0:1, 0:TT], 1.0 / D, None, op0=ALU.mult)
        B.chain(v, "dv", i2)
        i3 = v.tensor_tensor(rs_row, m_row.bitcast(F32), m_row.bitcast(F32), ALU.mult)
        B.chain(v, "dv", i3)
        v.tensor_tensor(q_row, q_row.bitcast(F32), rs_row.bitcast(F32),
                        ALU.subtract).then_inc(*B.inc("dv"))
    dv_tok = B.tok("dv")

    @B.block.scalar
    def _(s):
        B.wait(s, dv_tok)
        i1 = s.activation(q_row, q_row.bitcast(F32), AF.Ln,
                          bias=B.consts["c_eps"][0:1, 0:1])
        B.chain(s, "ac", i1)
        s.activation(rs_row, q_row.bitcast(F32), AF.Exp, scale=-0.5).then_inc(*B.inc("ac"))
    ac2 = B.tok("ac")

    @B.block.vector
    def _(v):
        B.wait(v, ac2)
        v.tensor_tensor(mr_row, m_row.bitcast(F32), rs_row.bitcast(F32),
                        ALU.mult).then_inc(*B.inc("dv"))
    dv2 = B.tok("dv")

    ones_r = B.consts["c_ones_row"]

    @B.block.tensor
    def _(t):
        B.wait(t, dv2)
        t.matmul(B.bank[4][:, 0:TT], ones_r[:], rs_row, start=True, stop=True)
        t.matmul(B.bank[5][:, 0:TT], ones_r[:], mr_row, start=True, stop=True) \
            .then_inc(*B.inc("pe"))
    pe2 = B.tok("pe")

    @B.block.vector
    def _(v):
        B.wait(v, pe2)
        v.tensor_copy(L.rsb[:], B.bank[4][:, 0:TT])
        v.tensor_copy(L.mrb[:], B.bank[5][:, 0:TT]).then_inc(*B.inc("dv"))
    B.guard_bank(4, B.tok("dv"))
    B.guard_bank(5, B.tok("dv"))
    return B.tok("dv")


def emit_ln_xn(B, L, x_ch, xn_ch, scr_ch=None, tag="", pre_toks=None):
    ready = emit_ln_core(B, L, x_ch, tag, pre_toks=pre_toks)
    toks = []
    for k in range(NCH):
        @B.block.vector
        def _(v, k=k):
            if k == 0:
                B.wait(v, ready)
            mid = xn_ch[k] if scr_ch is None else scr_ch[k]
            i1 = v.tensor_tensor(mid, x_ch[k].bitcast(F32), L.rsb[:],
                                 ALU.mult)
            B.chain(v, "dv", i1)
            v.tensor_tensor(xn_ch[k], mid.bitcast(F32), L.mrb[:],
                            ALU.subtract).then_inc(*B.inc("dv"))
        toks.append(B.tok("dv"))
    return toks


def emit_ln_residual(B, L, y_ch, x_ch, tag="", pre_toks=None):
    ready = emit_ln_core(B, L, y_ch, tag, pre_toks=pre_toks)
    toks = []
    for k in range(NCH):
        @B.block.vector
        def _(v, k=k):
            if k == 0:
                B.wait(v, ready)
            i1 = v.tensor_tensor(y_ch[k], y_ch[k].bitcast(F32), L.rsb[:],
                                 ALU.mult)
            B.chain(v, "dv", i1)
            i2 = v.tensor_tensor(y_ch[k], y_ch[k].bitcast(F32), L.mrb[:],
                                 ALU.subtract)
            B.chain(v, "dv", i2)
            v.tensor_tensor(x_ch[k], x_ch[k].bitcast(F32), y_ch[k].bitcast(F32),
                            ALU.add).then_inc(*B.inc("dv"))
        toks.append(B.tok("dv"))
    return toks



def emit_swiglu_mlp(B, ws, xn_ch, xn_toks, y_ch, Wg_dram, Wu_dram, Wd_dram,
                    n_inter_ch, sgbufs, asup, accumulate, tag="",
                    scale_tile=None, scale_tok=None):
    SUP = 16
    n_sup = (n_inter_ch + SUP - 1) // SUP
    OGS = [(0, 6), (6, 12), (12, 16)]

    for sup in range(n_sup):
        i0 = sup * SUP
        i1 = min(i0 + SUP, n_inter_ch)
        ni = i1 - i0
        a_toks = []
        for ii, ic in enumerate(range(i0, i1)):
            ws.load(Wg_dram[ic])
            ws.load(Wu_dram[ic])
            bg = B.bank[0 if ii % 2 == 0 else 2]
            bu = B.bank[1 if ii % 2 == 0 else 3]

            @B.block.tensor
            def _(t, bg=bg, bu=bu, ii=ii):
                slab = ws.use(t)
                B.wait_bank(t, 0 if ii % 2 == 0 else 2)
                for d in range(NCH):
                    if sup == 0 and ii == 0:
                        B.wait(t, xn_toks[d])
                    mm = t.matmul(bg[:, 0:TT], slab[:, d * 128:(d + 1) * 128],
                                  xn_ch[d], start=(d == 0), stop=(d == NCH - 1),
                                  skip_group_check=True)
                mm.then_inc(*B.inc("pe"))
                ws.done(B.tok("pe"))
                slab = ws.use(t)
                B.wait_bank(t, 1 if ii % 2 == 0 else 3)
                for d in range(NCH):
                    mm = t.matmul(bu[:, 0:TT], slab[:, d * 128:(d + 1) * 128],
                                  xn_ch[d], start=(d == 0), stop=(d == NCH - 1),
                                  skip_group_check=True)
                mm.then_inc(*B.inc("pe"))
                ws.done(B.tok("pe"))
            mm_tok = B.tok("pe")
            sg = sgbufs[ii % 2]

            @B.block.scalar
            def _(s, sg=sg, mm_tok=mm_tok, bg=bg):
                B.wait(s, mm_tok)
                s.activation(sg[:], bg[:, 0:TT], AF.Silu).then_inc(*B.inc("ac"))
            ac_tok = B.tok("ac")

            @B.block.vector
            def _(v, sg=sg, ac_tok=ac_tok, bu=bu, ii=ii):
                B.wait(v, ac_tok)
                if scale_tile is not None:
                    B.wait(v, scale_tok)
                    i = v.tensor_tensor(asup[ii], sg[:], bu[:, 0:TT], ALU.mult)
                    B.chain(v, "dv", i)
                    v.tensor_tensor(asup[ii], asup[ii], scale_tile,
                                    ALU.mult).then_inc(*B.inc("dv"))
                else:
                    v.tensor_tensor(asup[ii], sg[:], bu[:, 0:TT],
                                    ALU.mult).then_inc(*B.inc("dv"))
            a_toks.append(B.tok("dv"))
            B.guard_bank(0 if ii % 2 == 0 else 2, B.tok("ac"))
            B.guard_bank(1 if ii % 2 == 0 else 3, B.tok("dv"))

        for og_lo, og_hi in OGS:
            ogw = og_hi - og_lo
            ow = ogw * 128
            ii = 0
            while ii < ni:
                npair = 2 if ii + 1 < ni else 1
                ic = i0 + ii
                src = Wd_dram[ic:ic + npair, :, og_lo * 128:og_hi * 128] \
                    .transpose([1, 0, 2])
                ws.load_view(src, npair * ow)

                @B.block.tensor
                def _(t, ii=ii, ogw=ogw, ow=ow, npair=npair):
                    slab = ws.use(t)
                    if ii == 0:
                        for bi in range(ogw):
                            B.wait_bank(t, bi)
                    for a in range(npair):
                        B.wait(t, a_toks[ii + a])
                        for oo in range(ogw):
                            mm = t.matmul(B.bank[oo][:, 0:TT],
                                          slab[:, a * ow + oo * 128:
                                               a * ow + (oo + 1) * 128],
                                          asup[ii + a],
                                          start=(ii + a == 0),
                                          stop=(ii + a == ni - 1),
                                          skip_group_check=True)
                    mm.then_inc(*B.inc("pe"))
                    ws.done(B.tok("pe"))
                ii += npair
            down_tok = B.tok("pe")
            for oo in range(ogw):
                o = og_lo + oo

                @B.block.vector
                def _(v, oo=oo, o=o, down_tok=down_tok):
                    B.wait(v, down_tok)
                    if accumulate or sup > 0:
                        v.tensor_tensor(y_ch[o], y_ch[o].bitcast(F32),
                                        B.bank[oo][:, 0:TT], ALU.add) \
                            .then_inc(*B.inc("dv"))
                    else:
                        v.tensor_copy(y_ch[o], B.bank[oo][:, 0:TT]) \
                            .then_inc(*B.inc("dv"))
                B.guard_bank(oo, B.tok("dv"))
    return B.tok("dv")



class L0Bufs:
    def __init__(self, B, L):
        self.XP = [B.mr(4 + k) for k in range(4)]
        self.SZ = [B.mr(8 + k) for k in range(4)]
        self.XA = [B.mr(k) for k in range(4)]
        self.YF = [B.MBF[:, 512 * k:512 * (k + 1)] for k in range(4)]
        self.PF = B.mr(12)
        self.PB = [B.mf(k) for k in range(4)]
        self.PC = [B.mf(4 + k) for k in range(4)]
        self.DBS = [B.mf(8 + k) for k in range(2)]
        self.ABS = [B.mf(10 + k) for k in range(2)]
        self.c1 = B.mf(12)
        self.c2 = B.mf(13)
        self.TAIL0 = B.sb("l0_tail", [P, 12], F32)
        self.dsc = B.SPR
        self.DELTA = B.mr(13)
        self.SH = B.sb("l0_sh", [P, 4], F32)
        self.wdt = B.sb("l0s_wdt", [P, 32], F32R)
        self.cw = B.sb("l0s_cw", [P, 4 * 4], F32)
        self.cb = B.sb("l0s_cb", [P, 4], F32)
        self.acol = B.sb("l0s_acol", [P, 4], F32)
        self.dcol = B.sb("l0s_dcol", [P, 4], F32)
        self.bdt = B.sb("l0s_bdt", [32, 1], F32)


def emit_l0attn_consts(B, L0, d):
    l0c = B.new_sem("l0c")

    @B.block.gpsimd
    def _(g):
        for src, dst in [(d["l0_wdt"], L0.wdt[:]), (d["l0_cw"], L0.cw[:]),
                         (d["l0_cb"], L0.cb[:]), (d["l0_alog"], L0.dcol[:]),
                         (d["l0_dcol"], None), (d["l0_bdt"], L0.bdt[:])]:
            if dst is None:
                continue
            g.dma_start(out=dst, in_=src).then_inc(*B.inc(l0c, 16))
        g.dma_start(out=L0.c1[:, 0:4], in_=d["l0_dcol"]).then_inc(*B.inc(l0c, 16))
    ld_tok = B.tok(l0c)

    @B.block.scalar
    def _(s):
        B.wait(s, ld_tok)
        s.activation(L0.acol[:], L0.dcol[:], AF.Exp).then_inc(*B.inc("ac"))
    ac_tok = B.tok("ac")

    @B.block.vector
    def _(v):
        B.wait(v, ac_tok)
        i = v.tensor_scalar(L0.acol[:], L0.acol[:], -1.0, None, op0=ALU.mult)
        B.chain(v, "dv", i)
        i = v.tensor_copy(L0.dcol[:], L0.c1[:, 0:4])
        B.chain(v, "dv", i)
        i = v.memset(L0.SH[:], 0.0)
        B.chain(v, "dv", i)
        v.memset(L0.TAIL0[:], 0.0).then_inc(*B.inc("dv"))
    return B.tok(l0c), B.tok("dv")


def emit_l0attn(B, L0, ws, xn_ch, xn_toks, y_ch, W, tile, tag=""):
    tail_tok = B.tok("dv")

    for o in range(4):
        ws.load(W["WxT"][o])

        @B.block.tensor
        def _(t, o=o):
            slab = ws.use(t)
            bk = o % 2
            B.wait_bank(t, bk)
            if o == 0:
                B.wait(t, tail_tok)
            for d in range(NCH):
                if o == 0:
                    B.wait(t, xn_toks[d])
                mm = t.matmul(B.bank[bk][:, 0:TT], slab[:, d * 128:(d + 1) * 128],
                              xn_ch[d], start=(d == 0), stop=(d == NCH - 1),
                              skip_group_check=True)
            mm.then_inc(*B.inc("pe"))
            ws.done(B.tok("pe"))
        mm_tok = B.tok("pe")

        @B.block.scalar
        def _(s, o=o, mm_tok=mm_tok):
            B.wait(s, mm_tok)
            s.copy(L0.XP[o], B.bank[o % 2][:, 0:TT]).then_inc(*B.inc("ac"))
        B.guard_bank(o % 2, B.tok("ac"))
    for o in range(4):
        ws.load(W["WzT"][o])

        @B.block.tensor
        def _(t, o=o):
            slab = ws.use(t)
            bk = o % 2
            B.wait_bank(t, bk)
            for d in range(NCH):
                mm = t.matmul(B.bank[bk][:, 0:TT], slab[:, d * 128:(d + 1) * 128],
                              xn_ch[d], start=(d == 0), stop=(d == NCH - 1),
                              skip_group_check=True)
            mm.then_inc(*B.inc("pe"))
            ws.done(B.tok("pe"))
        mm_tok = B.tok("pe")

        @B.block.scalar
        def _(s, o=o, mm_tok=mm_tok):
            B.wait(s, mm_tok)
            s.activation(L0.SZ[o][:], B.bank[o % 2][:, 0:TT],
                         AF.Silu).then_inc(*B.inc("ac"))
        B.guard_bank(o % 2, B.tok("ac"))
    xproj_tok = B.tok("ac")

    CSC = B.CSC
    for k in range(4):
        @B.block.vector
        def _(v, k=k):
            if k == 0:
                B.wait(v, xproj_tok)
            B.wait(v, B.tok("dv"))
            B.wait(v, B.tok("ac"))
            pre = L0.XP[k].bitcast(F32)
            cw = L0.cw
            i = v.tensor_copy(CSC[:, 0:3], L0.TAIL0[:, 3 * k:3 * k + 3])
            B.chain(v, "dv", i)
            i = v.tensor_copy(CSC[:, 3:6], pre[:, 0:3])
            B.chain(v, "dv", i)
            i = v.tensor_copy(L0.TAIL0[:, 3 * k:3 * k + 3], pre[:, TT - 3:TT])
            B.chain(v, "dv", i)
            i = v.tensor_scalar(CSC[:, 8:11], CSC[:, 0:3],
                                cw[:, 4 * k:4 * k + 1], None, op0=ALU.mult)
            B.chain(v, "dv", i)
            for kk in range(1, 4):
                i = v.scalar_tensor_tensor(CSC[:, 8:11], CSC[:, kk:kk + 3],
                                           cw[:, 4 * k + kk:4 * k + kk + 1],
                                           CSC[:, 8:11], op0=ALU.mult, op1=ALU.add)
                B.chain(v, "dv", i)
            i = v.tensor_scalar(CSC[:, 16:16 + TT - 3], pre[:, 0:TT - 3],
                                cw[:, 4 * k:4 * k + 1], None, op0=ALU.mult)
            B.chain(v, "dv", i)
            for kk in range(1, 3):
                i = v.scalar_tensor_tensor(CSC[:, 16:16 + TT - 3],
                                           pre[:, kk:TT - 3 + kk],
                                           cw[:, 4 * k + kk:4 * k + kk + 1],
                                           CSC[:, 16:16 + TT - 3],
                                           op0=ALU.mult, op1=ALU.add)
                B.chain(v, "dv", i)
            v.scalar_tensor_tensor(CSC[:, 16:16 + TT - 3], pre[:, 3:TT],
                                   cw[:, 4 * k + 3:4 * k + 4],
                                   CSC[:, 16:16 + TT - 3],
                                   op0=ALU.mult, op1=ALU.add).then_inc(*B.inc("dv"))
        dv_tok = B.tok("dv")

        @B.block.scalar
        def _(s, k=k, dv_tok=dv_tok):
            B.wait(s, dv_tok)
            i = s.activation(L0.XA[k][:, 0:3], CSC[:, 8:11], AF.Silu,
                             bias=L0.cb[:, k:k + 1])
            B.chain(s, "ac", i)
            s.activation(L0.XA[k][:, 3:TT], CSC[:, 16:16 + TT - 3], AF.Silu,
                         bias=L0.cb[:, k:k + 1]).then_inc(*B.inc("ac"))
    xa_tok = B.tok("ac")
    xa_toks = B.tok("ac")

    XAB = [B.MBF[:, 2048 + 512 * k:2048 + 512 * (k + 1)] for k in range(4)]

    @B.block.vector
    def _(v):
        B.wait(v, xa_tok)
        B.wait(v, B.tok("dv"))
        for k in range(4):
            i = v.tensor_copy(XAB[k], L0.XA[k][:].bitcast(F32))
            if k < 3:
                B.chain(v, "dv", i)
        i.then_inc(*B.inc("dv"))
    xab_tok = B.tok("dv")

    for o in range(9):
        ws.load(W["WpT"][o])

        @B.block.tensor
        def _(t, o=o):
            slab = ws.use(t)
            bk = o % 2
            B.wait_bank(t, bk)
            if o == 0:
                B.wait(t, xab_tok)
            for d in range(4):
                mm = t.matmul(B.bank[bk][:, 0:TT], slab[:, d * 128:(d + 1) * 128],
                              XAB[d], start=(d == 0), stop=(d == 3),
                              skip_group_check=True)
            mm.then_inc(*B.inc("pe"))
            ws.done(B.tok("pe"))
        mm_tok = B.tok("pe")

        @B.block.scalar
        def _(s, o=o, mm_tok=mm_tok):
            B.wait(s, mm_tok)
            if o == 0:
                s.copy(L0.PF[:], B.bank[o % 2][:, 0:TT]).then_inc(*B.inc("ac"))
            elif o <= 4:
                s.copy(L0.PB[o - 1][:], B.bank[o % 2][:, 0:TT]).then_inc(*B.inc("ac"))
            else:
                s.copy(L0.PC[o - 5][:], B.bank[o % 2][:, 0:TT]).then_inc(*B.inc("ac"))
        B.guard_bank(o % 2, B.tok("ac"))
    params_tok = B.tok("ac")

    @B.block.tensor
    def _(t):
        B.wait(t, params_tok)
        B.wait_bank(t, 2)
        t.matmul(B.bank[2][0:32, 0:TT], L0.wdt[:], L0.PF[:],
                 start=True, stop=True).then_inc(*B.inc("pe"))
    dmm_tok = B.tok("pe")
    ds = L0.dsc
    v_r = ds[0:32, 0:TT]
    a_r = ds[0:32, TT:2 * TT]
    dl = L0.DELTA[0:32, :]

    @B.block.vector
    def _(v):
        B.wait(v, dmm_tok)
        v.tensor_scalar(v_r, B.bank[2][0:32, 0:TT], L0.bdt[0:32, 0:1], None,
                        op0=ALU.add).then_inc(*B.inc("dv"))
    B.guard_bank(2, B.tok("dv"))
    dv_tok = B.tok("dv")

    @B.block.scalar
    def _(s):
        B.wait(s, dv_tok)
        i = s.activation(a_r, v_r, AF.Abs)
        B.chain(s, "ac", i)
        i = s.activation(a_r, a_r, AF.Exp, scale=-1.0)
        B.chain(s, "ac", i)
        i = s.activation(a_r, a_r, AF.Ln, bias=B.consts["c_eps"][0:32, 2:3])
        B.chain(s, "ac", i)
        s.activation(dl, v_r, AF.Relu).then_inc(*B.inc("ac"))
    ac_tok = B.tok("ac")

    @B.block.vector
    def _(v):
        B.wait(v, ac_tok)
        v.tensor_tensor(dl, dl.bitcast(F32), a_r, ALU.add).then_inc(*B.inc("dv"))
    delta_tok = B.tok("dv")

    l0exp = B.mr(14)[0:32, :]

    @B.block.gpsimd
    def _(g):
        B.wait(g, B.tok("pe"))
        g.dma_start(out=l0exp, in_=W["c_l0exp_d"]).then_inc(*B.inc("ld", 16))
    lexp_tok = B.tok("ld")
    ab_free = [None, None]
    for k in range(4):
        bk = 2 + (k % 2)

        @B.block.tensor
        def _(t, k=k, bk=bk):
            B.wait_bank(t, bk)
            B.wait(t, delta_tok)
            B.wait(t, lexp_tok)
            t.matmul(B.bank[bk][:, 0:TT],
                     l0exp[:, k * 128:(k + 1) * 128],
                     L0.DELTA[0:32, :],
                     start=True, stop=True).then_inc(*B.inc("pe"))
        db_tok = B.tok("pe")

        @B.block.scalar
        def _(s, k=k, bk=bk, db_tok=db_tok, ft=ab_free[k % 2]):
            B.wait(s, db_tok)
            if ft is not None:
                B.wait(s, ft)
            s.activation(L0.ABS[k % 2][:], B.bank[bk][:, 0:TT], AF.Exp,
                         scale=L0.acol[:, k:k + 1]).then_inc(*B.inc("ac"))
        ab_tok = B.tok("ac")

        @B.block.vector
        def _(v, k=k, bk=bk, ab_tok=ab_tok):
            B.wait(v, ab_tok)
            B.wait(v, B.tok("dv"))
            i = v.tensor_copy(L0.c1[:], B.bank[bk][:, 0:TT])
            B.chain(v, "dv", i)
            i = v.tensor_tensor(L0.c2[:], L0.c1[:], L0.PB[k][:], ALU.mult)
            B.chain(v, "dv", i)
            i = v.tensor_tensor(L0.DBS[k % 2][:], L0.c2[:],
                                L0.XA[k][:].bitcast(F32), ALU.mult)
            B.chain(v, "dv", i)
            i = v.tensor_tensor_scan(L0.PB[k][:], L0.ABS[k % 2][:], L0.DBS[k % 2][:],
                                     L0.SH[:, k:k + 1], op0=ALU.mult, op1=ALU.add)
            B.chain(v, "dv", i)
            i = v.tensor_copy(L0.SH[:, k:k + 1], L0.PB[k][:, TT - 1:TT])
            B.chain(v, "dv", i)
            i = v.tensor_tensor(L0.c2[:], L0.PB[k][:], L0.PC[k][:], ALU.mult)
            B.chain(v, "dv", i)
            i = v.scalar_tensor_tensor(L0.c1[:], L0.XA[k][:].bitcast(F32),
                                       L0.dcol[:, k:k + 1], L0.c2[:],
                                       op0=ALU.mult, op1=ALU.add)
            B.chain(v, "dv", i)
            v.tensor_tensor(L0.YF[k][:], L0.c1[:], L0.SZ[k][:].bitcast(F32),
                            ALU.mult).then_inc(*B.inc("dv"))
        B.guard_bank(bk, B.tok("dv"))
        ab_free[k % 2] = B.tok("dv")
    yf_tok = B.tok("dv")

    toks = []
    for o in range(NCH):
        ws.load(W["WoT"][o])

        @B.block.tensor
        def _(t, o=o):
            slab = ws.use(t)
            bk = o % 2
            B.wait_bank(t, bk)
            if o == 0:
                B.wait(t, yf_tok)
            for d in range(4):
                mm = t.matmul(B.bank[bk][:, 0:TT], slab[:, d * 128:(d + 1) * 128],
                              L0.YF[d][:], start=(d == 0), stop=(d == 3),
                              skip_group_check=True)
            mm.then_inc(*B.inc("pe"))
            ws.done(B.tok("pe"))
        mm_tok = B.tok("pe")

        @B.block.scalar
        def _(s, o=o, mm_tok=mm_tok):
            B.wait(s, mm_tok)
            s.copy(y_ch[o], B.bank[o % 2][:, 0:TT]).then_inc(*B.inc("ac"))
        B.guard_bank(o % 2, B.tok("ac"))
        toks.append(B.tok("ac"))
    return toks



class L0FRBufs:
    def __init__(self, B, L):
        mf = B.MF32
        self.rsc = mf[:, 9 * TT + 32:9 * TT + 48]
        self.t8 = mf[:, 9 * TT + 48:9 * TT + 56]
        self.cw4 = mf[:, 9 * TT:9 * TT + 4]
        self.CB0 = B.mf(4)
        self.crow = L.rows[0:1, 0:TT]
        self.wrt = B.sb("l0fr_wrt", [P, 128], BF16)


def emit_l0fr_consts(B, R, d):
    @B.block.gpsimd
    def _(g):
        g.dma_start(out=R.wrt[:], in_=d["l0f_wrt"]).then_inc(*B.inc("ld", 16))
    return B.tok("ld")


def emit_l0ffn_router(B, R, xn_ch, xn_toks):
    @B.block.tensor
    def _(t):
        B.wait_bank(t, 6)
        for dd in range(NCH):
            B.wait(t, xn_toks[dd])
        for tc in range(4):
            for dd in range(NCH):
                mm = t.matmul(B.bank[6][:, 8 * tc:8 * (tc + 1)],
                              xn_ch[dd][:, tc * 128:(tc + 1) * 128],
                              R.wrt[:, 8 * dd:8 * (dd + 1)],
                              start=(dd == 0), stop=(dd == NCH - 1),
                              skip_group_check=True)
        mm.then_inc(*B.inc("pe"))
    lg_tok = B.tok("pe")

    cols = []
    for tc in range(4):
        lg = B.bank[6][:, 8 * tc:8 * (tc + 1)]
        m1 = R.rsc[:, 4 * tc + 0:4 * tc + 1]
        m2 = R.rsc[:, 4 * tc + 1:4 * tc + 2]
        rs = R.rsc[:, 4 * tc + 2:4 * tc + 3]
        cw = R.cw4[:, tc:tc + 1]

        @B.block.vector
        def _(v, lg=lg, m1=m1, m2=m2, tc=tc):
            if tc == 0:
                B.wait(v, lg_tok)
            B.wait(v, B.tok("dv"))
            i = v.tensor_reduce(m1, lg, axis=AX.X, op=ALU.max)
            B.chain(v, "dv", i)
            i = v.tensor_scalar(R.t8[:], lg, m1, None, op0=ALU.is_ge)
            B.chain(v, "dv", i)
            i = v.scalar_tensor_tensor(R.t8[:], R.t8[:], -1e9, lg,
                                       op0=ALU.mult, op1=ALU.add)
            B.chain(v, "dv", i)
            i = v.tensor_reduce(m2, R.t8[:], axis=AX.X, op=ALU.max)
            B.chain(v, "dv", i)
            v.tensor_tensor(m2, m2, m1, ALU.subtract).then_inc(*B.inc("dv"))
        dv1 = B.tok("dv")

        @B.block.scalar
        def _(s, m2=m2, rs=rs, dv1=dv1):
            B.wait(s, dv1)
            s.activation(rs, m2, AF.Exp).then_inc(*B.inc("ac"))
        ac1 = B.tok("ac")

        @B.block.vector
        def _(v, rs=rs, m1=m1, cw=cw, ac1=ac1):
            B.wait(v, ac1)
            B.wait(v, B.tok("dv"))
            i = v.tensor_scalar(cw, rs, 1.0, None, op0=ALU.add)
            B.chain(v, "dv", i)
            i = v.reciprocal(cw, cw)
            B.chain(v, "dv", i)
            i = v.tensor_tensor(m1, rs, cw, ALU.mult)
            B.chain(v, "dv", i)
            v.tensor_tensor(cw, cw, m1, ALU.add).then_inc(*B.inc("dv"))
        cols.append(cw)
    B.guard_bank(6, B.tok("dv"))
    return emit_cbcast(B, R, cols, R.CB0[:])



class L1FBufs:
    def __init__(self, B, L):
        mf = B.MF32
        self.COMB = mf[:, 9 * TT:9 * TT + 32]
        self.rsc = mf[:, 9 * TT + 32:9 * TT + 48]
        self.t8 = mf[:, 9 * TT + 48:9 * TT + 56]
        self.CB = B.mf(4)
        self.SGB = B.mf(5)
        self.crow = L.rows[0:1, 0:TT]
        self.sgrow = L.rows[0:1, TT:2 * TT]
        self.wrt = B.sb("l1fs_wrt", [P, 16 * 8], BF16)
        self.wsgt = B.sb("l1fs_wsgt", [P, 16], BF16)


def emit_l1ffn_consts(B, F, d):
    @B.block.gpsimd
    def _(g):
        g.dma_start(out=F.wrt[:], in_=d["l1f_wrt"]).then_inc(*B.inc("ld", 16))
        g.dma_start(out=F.wsgt[:], in_=d["l1f_wsgt"]).then_inc(*B.inc("ld", 16))
    return B.tok("ld")


def emit_l1ffn_router(B, F, xn_ch, xn_toks):
    @B.block.tensor
    def _(t):
        B.wait_bank(t, 6)
        for d in range(NCH):
            B.wait(t, xn_toks[d])
        for tc in range(4):
            for d in range(NCH):
                mm = t.matmul(B.bank[6][:, 8 * tc:8 * (tc + 1)],
                              xn_ch[d][:, tc * 128:(tc + 1) * 128],
                              F.wrt[:, 8 * d:8 * (d + 1)],
                              start=(d == 0), stop=(d == NCH - 1),
                              skip_group_check=True)
        mm.then_inc(*B.inc("pe"))
    lg_tok = B.tok("pe")

    for tc in range(4):
        lg = B.bank[6][:, 8 * tc:8 * (tc + 1)]
        pp = F.COMB[:, 8 * tc:8 * (tc + 1)]
        m1 = F.rsc[:, 4 * tc + 0:4 * tc + 1]
        m2 = F.rsc[:, 4 * tc + 1:4 * tc + 2]
        rs = F.rsc[:, 4 * tc + 2:4 * tc + 3]

        @B.block.vector
        def _(v, lg=lg, pp=pp, m1=m1, tc=tc):
            if tc == 0:
                B.wait(v, lg_tok)
            else:
                B.wait(v, B.tok("ac"))
            B.wait(v, B.tok("dv"))
            i = v.tensor_reduce(m1, lg, axis=AX.X, op=ALU.max)
            B.chain(v, "dv", i)
            v.tensor_scalar(pp, lg, m1, None, op0=ALU.subtract).then_inc(*B.inc("dv"))
        dv1 = B.tok("dv")

        @B.block.scalar
        def _(s, pp=pp, dv1=dv1):
            B.wait(s, dv1)
            s.activation(pp, pp, AF.Exp).then_inc(*B.inc("ac"))
        ac1 = B.tok("ac")

        @B.block.vector
        def _(v, pp=pp, m1=m1, m2=m2, rs=rs, ac1=ac1):
            B.wait(v, ac1)
            i = v.tensor_reduce(rs, pp, axis=AX.X, op=ALU.add)
            B.chain(v, "dv", i)
            i = v.reciprocal(rs, rs)
            B.chain(v, "dv", i)
            i = v.tensor_scalar(pp, pp, rs, None, op0=ALU.mult)
            B.chain(v, "dv", i)
            i = v.tensor_reduce(m1, pp, axis=AX.X, op=ALU.max)
            B.chain(v, "dv", i)
            i = v.tensor_scalar(F.t8[:], pp, m1, None, op0=ALU.is_ge)
            B.chain(v, "dv", i)
            i = v.scalar_tensor_tensor(F.t8[:], F.t8[:], -1e9, pp,
                                       op0=ALU.mult, op1=ALU.add)
            B.chain(v, "dv", i)
            i = v.tensor_reduce(m2, F.t8[:], axis=AX.X, op=ALU.max)
            B.chain(v, "dv", i)
            i = v.tensor_scalar(F.t8[:], pp, m2, None, op0=ALU.is_ge)
            B.chain(v, "dv", i)
            v.tensor_tensor(pp, pp, F.t8[:], ALU.mult).then_inc(*B.inc("dv"))
    B.guard_bank(6, B.tok("dv"))
    return B.tok("dv")


def emit_cbcast(B, F, col_src_ap_list, dst_tile):
    @B.block.tensor
    def _(t):
        B.wait_bank(t, 6)
        for tc in range(4):
            mm = t.matmul(B.bank[6][0:1, tc * 128:(tc + 1) * 128],
                          col_src_ap_list[tc], B.consts["c_ident"][:],
                          start=True, stop=True, skip_group_check=True)
        mm.then_inc(*B.inc("pe"))
    r_tok = B.tok("pe")

    @B.block.vector
    def _(v):
        B.wait(v, r_tok)
        B.wait(v, B.tok("dv"))
        v.tensor_copy(F.crow[:], B.bank[6][0:1, 0:TT]).then_inc(*B.inc("dv"))
    B.guard_bank(6, B.tok("dv"))
    c_tok = B.tok("dv")

    @B.block.tensor
    def _(t):
        B.wait_bank(t, 3)
        B.wait(t, c_tok)
        t.matmul(B.bank[3][:, 0:TT], B.consts["c_ones_row"][:], F.crow[:],
                 start=True, stop=True).then_inc(*B.inc("pe"))
    b_tok = B.tok("pe")

    @B.block.vector
    def _(v):
        B.wait(v, b_tok)
        B.wait(v, B.tok("dv"))
        v.tensor_copy(dst_tile, B.bank[3][:, 0:TT]).then_inc(*B.inc("dv"))
    B.guard_bank(3, B.tok("dv"))
    return B.tok("dv")


def emit_gu_mlp(B, ws, xn_ch, xn_toks, y_ch, GU_dram, D_dram, n_inter_ch,
                gu_off, sgbufs, asup, scale_tile, scale_tok, first_writer,
                tag=""):
    SUP = 16
    n_sup = (n_inter_ch + SUP - 1) // SUP
    OGS = [(0, 6), (6, 12), (12, 16)]
    for sup in range(n_sup):
        i0 = sup * SUP
        i1 = min(i0 + SUP, n_inter_ch)
        ni = i1 - i0
        a_toks = []
        for ii, ic in enumerate(range(i0, i1)):
            ws.load(GU_dram[ic])
            ws.load(GU_dram[ic + gu_off])
            bg = B.bank[0 if ii % 2 == 0 else 2]
            bu = B.bank[1 if ii % 2 == 0 else 3]

            @B.block.tensor
            def _(t, bg=bg, bu=bu, ii=ii, sup=sup):
                slab = ws.use(t)
                B.wait_bank(t, 0 if ii % 2 == 0 else 2)
                for d in range(NCH):
                    if sup == 0 and ii == 0:
                        B.wait(t, xn_toks[d])
                    mm = t.matmul(bg[:, 0:TT], slab[:, d * 128:(d + 1) * 128],
                                  xn_ch[d], start=(d == 0), stop=(d == NCH - 1),
                                  skip_group_check=True)
                mm.then_inc(*B.inc("pe"))
                ws.done(B.tok("pe"))
                slab = ws.use(t)
                B.wait_bank(t, 1 if ii % 2 == 0 else 3)
                for d in range(NCH):
                    mm = t.matmul(bu[:, 0:TT], slab[:, d * 128:(d + 1) * 128],
                                  xn_ch[d], start=(d == 0), stop=(d == NCH - 1),
                                  skip_group_check=True)
                mm.then_inc(*B.inc("pe"))
                ws.done(B.tok("pe"))
            mm_tok = B.tok("pe")
            sg = sgbufs[ii % 2]

            @B.block.scalar
            def _(s, sg=sg, mm_tok=mm_tok, bg=bg):
                B.wait(s, mm_tok)
                s.activation(sg[:], bg[:, 0:TT], AF.Silu).then_inc(*B.inc("ac"))
            ac_tok = B.tok("ac")

            @B.block.vector
            def _(v, sg=sg, ac_tok=ac_tok, bu=bu, ii=ii):
                B.wait(v, ac_tok)
                if scale_tile is not None:
                    B.wait(v, scale_tok)
                    i = v.tensor_tensor(asup[ii], sg[:], bu[:, 0:TT], ALU.mult)
                    B.chain(v, "dv", i)
                    v.tensor_tensor(asup[ii], asup[ii], scale_tile,
                                    ALU.mult).then_inc(*B.inc("dv"))
                else:
                    v.tensor_tensor(asup[ii], sg[:], bu[:, 0:TT],
                                    ALU.mult).then_inc(*B.inc("dv"))
            a_toks.append(B.tok("dv"))
            B.guard_bank(0 if ii % 2 == 0 else 2, B.tok("ac"))
            B.guard_bank(1 if ii % 2 == 0 else 3, B.tok("dv"))

        for og_lo, og_hi in OGS:
            ogw = og_hi - og_lo
            ow = ogw * 128
            ii = 0
            while ii < ni:
                npair = 2 if ii + 1 < ni else 1
                ic = i0 + ii
                src = D_dram[ic:ic + npair, :, og_lo * 128:og_hi * 128] \
                    .transpose([1, 0, 2])
                ws.load_view(src, npair * ow)

                @B.block.tensor
                def _(t, ii=ii, ogw=ogw, ow=ow, npair=npair):
                    slab = ws.use(t)
                    if ii == 0:
                        for bi in range(ogw):
                            B.wait_bank(t, bi)
                    for a in range(npair):
                        B.wait(t, a_toks[ii + a])
                        for oo in range(ogw):
                            mm = t.matmul(B.bank[oo][:, 0:TT],
                                          slab[:, a * ow + oo * 128:
                                               a * ow + (oo + 1) * 128],
                                          asup[ii + a],
                                          start=(ii + a == 0),
                                          stop=(ii + a == ni - 1),
                                          skip_group_check=True)
                    mm.then_inc(*B.inc("pe"))
                    ws.done(B.tok("pe"))
                ii += npair
            down_tok = B.tok("pe")
            for oo in range(ogw):
                o = og_lo + oo

                @B.block.vector
                def _(v, oo=oo, o=o, down_tok=down_tok, sup=sup):
                    B.wait(v, down_tok)
                    if first_writer and sup == 0:
                        v.tensor_copy(y_ch[o], B.bank[oo][:, 0:TT]) \
                            .then_inc(*B.inc("dv"))
                    else:
                        v.tensor_tensor(y_ch[o], y_ch[o].bitcast(F32),
                                        B.bank[oo][:, 0:TT], ALU.add) \
                            .then_inc(*B.inc("dv"))
                B.guard_bank(oo, B.tok("dv"))
    return B.tok("dv")


def emit_l1ffn(B, F, ws, xn_ch, xn_toks, y_ch, d, sgbufs, asup, tile):
    r_tok = emit_l1ffn_router(B, F, xn_ch, xn_toks)

    @B.block.tensor
    def _(t):
        B.wait_bank(t, 3)
        for dd in range(NCH):
            mm = t.matmul(B.bank[3][0:1, 0:TT], F.wsgt[:, dd:dd + 1],
                          xn_ch[dd], start=(dd == 0), stop=(dd == NCH - 1),
                          skip_group_check=True)
        mm.then_inc(*B.inc("pe"))
    sg_tok = B.tok("pe")

    @B.block.scalar
    def _(s):
        B.wait(s, sg_tok)
        s.activation(F.sgrow[:], B.bank[3][0:1, 0:TT],
                     AF.Sigmoid).then_inc(*B.inc("ac"))
    B.guard_bank(3, B.tok("ac"))
    sgr_tok = B.tok("ac")

    @B.block.tensor
    def _(t):
        B.wait_bank(t, 3)
        B.wait(t, sgr_tok)
        t.matmul(B.bank[3][:, 0:TT], B.consts["c_ones_row"][:], F.sgrow[:],
                 start=True, stop=True).then_inc(*B.inc("pe"))
    sgb_tok = B.tok("pe")

    @B.block.vector
    def _(v):
        B.wait(v, sgb_tok)
        v.tensor_copy(F.SGB[:], B.bank[3][:, 0:TT]).then_inc(*B.inc("dv"))
    B.guard_bank(3, B.tok("dv"))
    sgbt = B.tok("dv")

    emit_gu_mlp(B, ws, xn_ch, xn_toks, y_ch, d["WsGU"], d["WsD"], 44, 44,
                sgbufs, asup, F.SGB[:], sgbt, first_writer=True)

    for e in range(8):
        cols = [F.COMB[:, 8 * tc + e:8 * tc + e + 1] for tc in range(4)]
        cb_tok = emit_cbcast(B, F, cols, F.CB[:])
        emit_gu_mlp(B, ws, xn_ch, xn_toks, y_ch, d[f"WeGU{e}"], d[f"WeD{e}"],
                    11, 11, sgbufs, asup, F.CB[:], cb_tok, first_writer=False)
    return B.tok("dv")



class L1ABufs:
    def __init__(self, B, yt_slabs, xnt_slabs):
        self.PRE = yt_slabs
        self.QKV = xnt_slabs
        self.SZ1 = [B.mr(8 + k) for k in range(8)]
        _ysl = [6, 7, 9]
        _yext = B.sb("ynf2", [P, 1024], BF16)
        self.YNF = [B.mf(_ysl[k // 2]).bitcast(BF16)[:, (k % 2) * TT:
                    (k % 2 + 1) * TT] for k in range(6)] + \
                   [_yext[:, 0:TT], _yext[:, TT:2 * TT]]
        self.TAIL = B.sb("a1_tail", [P, 48], F32)
        self.shld = B.new_sem("shld")
        self.BET = B.MF32[0:16, 10 * TT:11 * TT]
        self.GG = B.MF32[0:16, 11 * TT:12 * TT]
        self.gsc = B.SPR
        mb = B.MBF
        self.QBF = [mb[:, 128 * j:128 * (j + 1)] for j in range(4)]
        self.KBF = [mb[:, 512 + 128 * j:512 + 128 * (j + 1)] for j in range(4)]
        self.VBF = [mb[:, 1024 + 128 * j:1024 + 128 * (j + 1)] for j in range(8)]
        self.kT = mb[:, 2048:2560]
        self.qT = mb[:, 2560:3072]
        self.vT = mb[:, 3072:4096]
        self.Mbf = mb[:, 4096:4224]
        self.Mtbf = mb[:, 4224:4352]
        self.Atbf = mb[:, 4352:4480]
        self.Zs = [mb[:, 4480 + 128 * j:4480 + 128 * (j + 1)] for j in range(6)]
        self.Pb = [mb[:, 5248 + 128 * j:5248 + 128 * (j + 1)] for j in range(2)]
        self.Xap = [mb[:, 5504 + 64 * j:5504 + 64 * (j + 1)] for j in range(2)]
        self.kp = mb[:, 5632:5696]
        mf = B.MF32
        self.OT = B.mf(0, 2)
        self.EMS = [mf[:, 2 * TT:2 * TT + 128], mf[:, 2 * TT + 128:2 * TT + 256]]
        self.TM = mf[:, 2 * TT + 256:2 * TT + 384]
        self.G16 = mf[:, 3 * TT:3 * TT + 192]
        self.EX = mf[:, 3 * TT + 192:3 * TT + 240]
        self.LR = mf[:, 3 * TT + 240:3 * TT + 256]
        self.RSO = mf[:, 3 * TT + 256:3 * TT + 272]
        self.Rf = mf[:, 8 * TT:8 * TT + 64]
        self.t64 = mf[:, 8 * TT + 64:8 * TT + 128]
        self.GTOT = mf[0:1, 2 * TT + 1408:2 * TT + 1424]
        self.EGR = mf[0:1, 2 * TT + 1424:2 * TT + 1440]
        self.EGRp = mf[0:1, 2 * TT + 1440:2 * TT + 1456]
        self.EGB = mf[0:64, 2 * TT + 1472:2 * TT + 1488]
        self.S = B.sb("a1_s", [64, 1024], F32R)
        self.Shi = B.sb("a1_shi", [P, 512], F32R)
        self.cw1 = B.sb("a1_cw1", [P, 64], F32)
        self.nea = B.sb("a1_nea", [16, 1], F32)
        self.dtb = B.sb("a1_dtb", [16, 1], F32)
        self.nwc = B.sb("a1_nwc", [P, 8], F32)


def emit_l1attn_consts(B, A, d):
    szsem = B.new_sem("szld")
    l1c = B.new_sem("l1c")

    @B.block.gpsimd
    def _(g):
        for src, dst in [(d["a1d_cw1"], A.cw1[:]), (d["a1d_alog"], A.nea[:]),
                         (d["a1d_dtb"], A.dtb[:]), (d["a1d_nwc"], A.nwc[:])]:
            g.dma_start(out=dst, in_=src).then_inc(*B.inc(l1c, 16))
        zr = d["c_zeros_r"]
        g.dma_start(out=A.S[0:64, 0:512], in_=zr[0:64, :]).then_inc(*B.inc(szsem, 16))
        g.dma_start(out=A.S[0:64, 512:1024], in_=zr[0:64, :]).then_inc(*B.inc(szsem, 16))
        g.dma_start(out=A.Shi[:], in_=zr[:]).then_inc(*B.inc(szsem, 16))
    A._sz_tok = B.tok(szsem)
    ld_tok = B.tok(l1c)

    @B.block.scalar
    def _(s):
        B.wait(s, ld_tok)
        s.activation(A.nea[:], A.nea[:], AF.Exp).then_inc(*B.inc("ac"))
    ac_tok = B.tok("ac")

    @B.block.vector
    def _(v):
        B.wait(v, ac_tok)
        i = v.tensor_scalar(A.nea[:], A.nea[:], -1.0, None, op0=ALU.mult)
        B.chain(v, "dv", i)
        v.memset(A.TAIL[:], 0.0).then_inc(*B.inc("dv"))
    return B.tok(l1c), B.tok("dv")


def _proj_evac(B, ws, Wdram, n_o, xn_ch, xn_toks, evac_fn, first_wait=None):
    for o in range(n_o):
        ws.load(Wdram[o])

        @B.block.tensor
        def _(t, o=o):
            slab = ws.use(t)
            bk = o % 2
            B.wait_bank(t, bk)
            if o == 0 and first_wait is not None:
                B.wait(t, first_wait)
            for dd in range(NCH):
                if o == 0 and xn_toks is not None:
                    B.wait(t, xn_toks[dd])
                mm = t.matmul(B.bank[bk][:, 0:TT], slab[:, dd * 128:(dd + 1) * 128],
                              xn_ch[dd], start=(dd == 0), stop=(dd == NCH - 1),
                              skip_group_check=True)
            mm.then_inc(*B.inc("pe"))
            ws.done(B.tok("pe"))
        mm_tok = B.tok("pe")
        evac_fn(o, B.bank[o % 2], mm_tok)


def emit_l1attn(B, A, ws, xn_ch, xn_toks, y_ch_out, d, tile):
    nc = B.nc
    def ev_qkv(o, bank, mm_tok):
        @B.block.scalar
        def _(s, o=o, bank=bank, mm_tok=mm_tok):
            B.wait(s, mm_tok)
            s.copy(A.PRE[o][:], bank[:, 0:TT]).then_inc(*B.inc("ac"))
        B.guard_bank(o % 2, B.tok("ac"))
    _proj_evac(B, ws, d["W1qkvT"], 16, xn_ch, xn_toks, ev_qkv)

    def ev_z(o, bank, mm_tok):
        @B.block.scalar
        def _(s, o=o, bank=bank, mm_tok=mm_tok):
            B.wait(s, mm_tok)
            s.activation(A.SZ1[o][:], bank[:, 0:TT], AF.Silu).then_inc(*B.inc("ac"))
        B.guard_bank(o % 2, B.tok("ac"))
    _proj_evac(B, ws, d["W1zT"], 8, xn_ch, None, ev_z)

    ws.load(d["a1d_wbt"])
    ws.load(d["a1d_wat"])

    @B.block.tensor
    def _(t):
        B.wait_bank(t, 2)
        B.wait_bank(t, 3)
        slab = ws.use(t)
        for dd in range(NCH):
            mb = t.matmul(B.bank[2][0:16, 0:TT], slab[:, 16 * dd:16 * (dd + 1)],
                          xn_ch[dd], start=(dd == 0), stop=(dd == NCH - 1),
                          skip_group_check=True)
        mb.then_inc(*B.inc("pe"))
        ws.done(B.tok("pe"))
        slab = ws.use(t)
        for dd in range(NCH):
            ma = t.matmul(B.bank[3][0:16, 0:TT], slab[:, 16 * dd:16 * (dd + 1)],
                          xn_ch[dd], start=(dd == 0), stop=(dd == NCH - 1),
                          skip_group_check=True)
        ma.then_inc(*B.inc("pe"))
        ws.done(B.tok("pe"))
    ba_tok = B.tok("pe")

    @B.block.scalar
    def _(s):
        B.wait(s, ba_tok)
        s.activation(A.BET, B.bank[2][0:16, 0:TT], AF.Sigmoid).then_inc(*B.inc("ac"))
    B.guard_bank(2, B.tok("ac"))
    bet_tok = B.tok("ac")

    v_r = A.gsc[0:16, 0:TT]
    a_r = A.gsc[0:16, TT:2 * TT]

    @B.block.vector
    def _(v):
        B.wait(v, ba_tok)
        v.tensor_scalar(v_r, B.bank[3][0:16, 0:TT], A.dtb[0:16, 0:1], None,
                        op0=ALU.add).then_inc(*B.inc("dv"))
    B.guard_bank(3, B.tok("dv"))
    g1 = B.tok("dv")

    @B.block.scalar
    def _(s):
        B.wait(s, g1)
        i = s.activation(a_r, v_r, AF.Abs)
        B.chain(s, "ac", i)
        i = s.activation(a_r, a_r, AF.Exp, scale=-1.0)
        B.chain(s, "ac", i)
        i = s.activation(a_r, a_r, AF.Ln, bias=B.consts["c_eps"][0:16, 2:3])
        B.chain(s, "ac", i)
        s.activation(A.GG, v_r, AF.Relu).then_inc(*B.inc("ac"))
    g2 = B.tok("ac")

    @B.block.vector
    def _(v):
        B.wait(v, g2)
        i = v.tensor_tensor(A.GG, A.GG, a_r, ALU.add)
        B.chain(v, "dv", i)
        v.tensor_scalar(A.GG, A.GG, A.nea[0:16, 0:1], None,
                        op0=ALU.mult).then_inc(*B.inc("dv"))
    gg_tok = B.tok("dv")
    A._gg_tok = gg_tok
    A._bet_tok = bet_tok

    qkv_toks = []
    for k in range(NCH):
        @B.block.vector
        def _(v, k=k):
            B.wait(v, B.tok("ac"))
            B.wait(v, B.tok("dv"))
            pre = A.PRE[k][:].bitcast(F32)
            cw = A.cw1
            i = v.tensor_copy(B.CSC[:, 0:3], A.TAIL[:, 3 * k:3 * k + 3])
            B.chain(v, "dv", i)
            i = v.tensor_copy(B.CSC[:, 3:6], A.PRE[k][:, 0:3].bitcast(F32))
            B.chain(v, "dv", i)
            i = v.tensor_copy(A.TAIL[:, 3 * k:3 * k + 3],
                              A.PRE[k][:, TT - 3:TT].bitcast(F32))
            B.chain(v, "dv", i)
            i = v.tensor_scalar(B.CSC[:, 8:11], B.CSC[:, 0:3],
                                cw[:, 4 * k:4 * k + 1], None, op0=ALU.mult)
            B.chain(v, "dv", i)
            for kk in range(1, 4):
                i = v.scalar_tensor_tensor(B.CSC[:, 8:11], B.CSC[:, kk:kk + 3],
                                           cw[:, 4 * k + kk:4 * k + kk + 1],
                                           B.CSC[:, 8:11], op0=ALU.mult, op1=ALU.add)
                B.chain(v, "dv", i)
            i = v.tensor_scalar(B.CSC[:, 16:16 + TT - 3], pre[:, 0:TT - 3],
                                cw[:, 4 * k:4 * k + 1], None, op0=ALU.mult)
            B.chain(v, "dv", i)
            for kk in range(1, 3):
                i = v.scalar_tensor_tensor(B.CSC[:, 16:16 + TT - 3],
                                           pre[:, kk:TT - 3 + kk],
                                           cw[:, 4 * k + kk:4 * k + kk + 1],
                                           B.CSC[:, 16:16 + TT - 3],
                                           op0=ALU.mult, op1=ALU.add)
                B.chain(v, "dv", i)
            v.scalar_tensor_tensor(B.CSC[:, 16:16 + TT - 3],
                                   pre[:, 3:TT], cw[:, 4 * k + 3:4 * k + 4],
                                   B.CSC[:, 16:16 + TT - 3],
                                   op0=ALU.mult, op1=ALU.add).then_inc(*B.inc("dv"))
        dvt = B.tok("dv")

        @B.block.scalar
        def _(s, k=k, dvt=dvt):
            B.wait(s, dvt)
            i = s.activation(A.QKV[k][:, 0:3], B.CSC[:, 8:11], AF.Silu)
            B.chain(s, "ac", i)
            s.activation(A.QKV[k][:, 3:TT], B.CSC[:, 16:16 + TT - 3],
                         AF.Silu).then_inc(*B.inc("ac"))
        qkv_toks.append(B.tok("ac"))
    return qkv_toks


def emit_l1attn_gdn(B, A, qkv_toks, gg_tok, bet_tok, s_tok_in):
    Ibf = B.consts["c_ident_bfr"]
    If32 = B.consts["c_ident"]
    onesr = B.consts["c_onesrow_f32"]
    onesc = B.consts["c_onescol_f32"]
    seg = lambda i: A.G16[:, 16 * i:16 * (i + 1)]
    GT, BT, G, LB, LRK16, LRQ16, GKP, GMB, GKM, NGKM, GA = [seg(i) for i in range(11)]
    EDEC = A.EX[:, 0:16]
    EO = A.EX[:, 16:32]
    EK = A.EX[:, 32:48]
    eps2 = B.consts["c_eps"][:, 1:2]

    s_tok = s_tok_in
    ynf_toks = []
    for cc in range(4):
        csl = slice(cc * 128, (cc + 1) * 128)

        @B.block.vector
        def _(v, cc=cc, csl=csl):
            B.wait(v, B.tok("dv"))
            B.wait(v, B.tok("pe"))
            for j in range(4):
                if cc == 0:
                    B.wait(v, qkv_toks[j])
                i = v.tensor_copy(A.QBF[j][:], A.QKV[j][:, csl].bitcast(F32))
                B.chain(v, "dv", i)
            for j in range(4):
                if cc == 0:
                    B.wait(v, qkv_toks[4 + j])
                i = v.tensor_copy(A.KBF[j][:], A.QKV[4 + j][:, csl].bitcast(F32))
                B.chain(v, "dv", i)
            for j in range(7):
                if cc == 0:
                    B.wait(v, qkv_toks[8 + j])
                i = v.tensor_copy(A.VBF[j][:], A.QKV[8 + j][:, csl].bitcast(F32))
                B.chain(v, "dv", i)
            if cc == 0:
                B.wait(v, qkv_toks[15])
            v.tensor_copy(A.VBF[7][:],
                          A.QKV[15][:, csl].bitcast(F32)).then_inc(*B.inc("dv"))
        bf_tok = B.tok("dv")

        for (srcs, dst, lbl) in ((A.QBF, A.qT, "q"), (A.KBF, A.kT, "k")):
            @B.block.tensor
            def _(t, srcs=srcs, cc=cc):
                B.wait_bank(t, 7)
                B.wait(t, bf_tok)
                for j in range(4):
                    mm = t.transpose(B.bank_bf[:, 128 * j:128 * (j + 1)],
                                     srcs[j][:], Ibf[:])
                mm.then_inc(*B.inc("pe"))
            tr_tok = B.tok("pe")

            @B.block.vector
            def _(v, dst=dst, tr_tok=tr_tok):
                B.wait(v, tr_tok)
                B.wait(v, B.tok("dv"))
                v.tensor_copy(dst, B.bank_bf[:, 0:512]).then_inc(*B.inc("dv"))
            B.guard_bank(7, B.tok("dv"))
        for half in range(2):
            @B.block.tensor
            def _(t, half=half):
                B.wait_bank(t, 7)
                for j in range(4):
                    mm = t.transpose(B.bank_bf[:, 128 * j:128 * (j + 1)],
                                     A.VBF[4 * half + j][:], Ibf[:])
                mm.then_inc(*B.inc("pe"))
            tr_tok = B.tok("pe")

            @B.block.vector
            def _(v, half=half, tr_tok=tr_tok):
                B.wait(v, tr_tok)
                B.wait(v, B.tok("dv"))
                v.tensor_copy(A.vT[:, 512 * half:512 * (half + 1)],
                              B.bank_bf[:, 0:512]).then_inc(*B.inc("dv"))
            B.guard_bank(7, B.tok("dv"))
        tr_done = B.tok("dv")

        @B.block.vector
        def _(v):
            B.wait(v, tr_done)
            csc = B.CSC[:, 0:512]
            i = v.tensor_tensor(csc, A.kT, A.kT, ALU.mult)
            B.chain(v, "dv", i)
            i = v.tensor_reduce(A.LR[:, 0:8],
                                csc.rearrange("p (h k) -> p h k", k=64),
                                axis=AX.X, op=ALU.add)
            B.chain(v, "dv", i)
            i = v.tensor_tensor(csc, A.qT, A.qT, ALU.mult)
            B.chain(v, "dv", i)
            v.tensor_reduce(A.LR[:, 8:16],
                            csc.rearrange("p (h k) -> p h k", k=64),
                            axis=AX.X, op=ALU.add).then_inc(*B.inc("dv"))
        ss_tok = B.tok("dv")

        @B.block.scalar
        def _(s, ss_tok=ss_tok):
            B.wait(s, ss_tok)
            i = s.activation(A.LR[:, 0:8], A.LR[:, 0:8], AF.Ln, bias=eps2)
            B.chain(s, "ac", i)
            s.activation(A.LR[:, 8:16], A.LR[:, 8:16], AF.Ln,
                         bias=eps2).then_inc(*B.inc("ac"))
        ln_tok = B.tok("ac")

        @B.block.tensor
        def _(t, cc=cc, csl=csl):
            B.wait_bank(t, 2)
            B.wait(t, gg_tok)
            B.wait(t, bet_tok)
            t.transpose(B.bank[2][:, 0:16], A.GG[:, csl],
                        If32[0:16, 0:16]).then_inc(*B.inc("pe"))
            t.transpose(B.bank[2][:, 16:32], A.BET[:, csl],
                        If32[0:16, 0:16]).then_inc(*B.inc("pe"))
        gtr_tok = B.tok("pe")

        @B.block.vector
        def _(v, gtr_tok=gtr_tok):
            B.wait(v, gtr_tok)
            B.wait(v, B.tok("dv"))
            i = v.tensor_copy(GT, B.bank[2][:, 0:16])
            B.chain(v, "dv", i)
            v.tensor_copy(BT, B.bank[2][:, 16:32]).then_inc(*B.inc("dv"))
        B.guard_bank(2, B.tok("dv"))
        gb_tok = B.tok("dv")

        @B.block.tensor
        def _(t, gb_tok=gb_tok):
            B.wait_bank(t, 2)
            B.wait(t, gb_tok)
            t.matmul(B.bank[2][:, 0:16], B.consts["c_tri_incl"][:], GT,
                     start=True, stop=True).then_inc(*B.inc("pe"))
            t.matmul(B.bank[2][0:1, 16:32], onesc[:], GT,
                     start=True, stop=True).then_inc(*B.inc("pe"))
        cum_tok = B.tok("pe")

        @B.block.vector
        def _(v, cum_tok=cum_tok):
            B.wait(v, cum_tok)
            B.wait(v, B.tok("dv"))
            i = v.tensor_copy(G, B.bank[2][:, 0:16])
            B.chain(v, "dv", i)
            v.tensor_copy(A.GTOT[:], B.bank[2][0:1, 16:32]).then_inc(*B.inc("dv"))
        B.guard_bank(2, B.tok("dv"))
        gc_tok = B.tok("dv")

        @B.block.scalar
        def _(s, gc_tok=gc_tok):
            B.wait(s, gc_tok)
            i = s.activation(LB, BT, AF.Ln)
            B.chain(s, "ac", i)
            s.activation(A.EGR[:], A.GTOT[:], AF.Exp).then_inc(*B.inc("ac"))
        lb_tok = B.tok("ac")

        @B.block.vector
        def _(v, lb_tok=lb_tok):
            B.wait(v, lb_tok)
            B.wait(v, B.tok("dv"))
            i = v.tensor_copy(A.EGRp[0:1, 0:8:2], A.EGR[0:1, 0:16:4])
            B.chain(v, "dv", i)
            i = v.tensor_copy(A.EGRp[0:1, 1:8:2], A.EGR[0:1, 1:16:4])
            B.chain(v, "dv", i)
            i = v.tensor_copy(A.EGRp[0:1, 8:16:2], A.EGR[0:1, 2:16:4])
            B.chain(v, "dv", i)
            v.tensor_copy(A.EGRp[0:1, 9:16:2],
                          A.EGR[0:1, 3:16:4]).then_inc(*B.inc("dv"))
        egp_tok = B.tok("dv")

        @B.block.vector
        def _(v, lb_tok=lb_tok, ln_tok=ln_tok):
            B.wait(v, lb_tok)
            B.wait(v, ln_tok)
            B.wait(v, B.tok("dv"))
            i = v.tensor_scalar(A.LR[:, 0:8], A.LR[:, 0:8], -0.5, None, op0=ALU.mult)
            B.chain(v, "dv", i)
            i = v.tensor_scalar(A.LR[:, 8:16], A.LR[:, 8:16], -0.5,
                                -0.5 * float(np.log(64.0)), op0=ALU.mult, op1=ALU.add)
            B.chain(v, "dv", i)
            i = v.tensor_copy(LRK16[:, 0:16:2], A.LR[:, 0:8])
            B.chain(v, "dv", i)
            i = v.tensor_copy(LRK16[:, 1:16:2], A.LR[:, 0:8])
            B.chain(v, "dv", i)
            i = v.tensor_copy(LRQ16[:, 0:16:2], A.LR[:, 8:16])
            B.chain(v, "dv", i)
            i = v.tensor_copy(LRQ16[:, 1:16:2], A.LR[:, 8:16])
            B.chain(v, "dv", i)
            i = v.tensor_tensor(GKP, G, LRK16, ALU.add)
            B.chain(v, "dv", i)
            i = v.tensor_tensor(GMB, GKP, LB, ALU.add)
            B.chain(v, "dv", i)
            i = v.tensor_tensor(GKM, G, LRK16, ALU.subtract)
            B.chain(v, "dv", i)
            i = v.tensor_scalar(NGKM, GKM, -1.0, None, op0=ALU.mult)
            B.chain(v, "dv", i)
            v.tensor_tensor(GA, G, LRQ16, ALU.add).then_inc(*B.inc("dv"))
        cols_tok = B.tok("dv")

        @B.block.tensor
        def _(t, cols_tok=cols_tok, egp_tok=egp_tok):
            B.wait_bank(t, 2)
            B.wait(t, cols_tok)
            B.wait(t, egp_tok)
            t.matmul(B.bank[2][0:64, 0:16], onesr[0:1, 0:64], A.EGRp[:],
                     start=True, stop=True).then_inc(*B.inc("pe"))
            t.matmul(B.bank[2][:, 16:32], onesr[:], A.GTOT[:],
                     start=True, stop=True).then_inc(*B.inc("pe"))
        eg_tok = B.tok("pe")

        @B.block.vector
        def _(v, eg_tok=eg_tok):
            B.wait(v, eg_tok)
            B.wait(v, B.tok("dv"))
            i = v.tensor_copy(A.EGB[:], B.bank[2][0:64, 0:16])
            B.chain(v, "dv", i)
            v.tensor_tensor(EDEC, B.bank[2][:, 16:32], GKM,
                            ALU.subtract).then_inc(*B.inc("dv"))
        B.guard_bank(2, B.tok("dv"))
        ed_tok = B.tok("dv")

        @B.block.scalar
        def _(s, ed_tok=ed_tok):
            B.wait(s, ed_tok)
            i = s.activation(EDEC, EDEC, AF.Exp)
            B.chain(s, "ac", i)
            i = s.activation(EO, GA, AF.Exp)
            B.chain(s, "ac", i)
            s.activation(EK, GKP, AF.Exp).then_inc(*B.inc("ac"))
        ex_tok = B.tok("ac")

        CBv = B.MF32[:, 4 * TT:5 * TT].bitcast(BF16)
        S5 = B.MF32
        fsets = [
            dict(Mbf=A.Mbf, Mtbf=A.Mtbf, Atbf=A.Atbf, TM=A.TM, TMA=A.TMA2 if False else B.MF32[:, 2 * TT + 384:2 * TT + 512],
                 EMS0=A.EMS[0], EMS1=A.EMS[1]),
            dict(Mbf=CBv[:, 0:128], Mtbf=CBv[:, 128:256], Atbf=CBv[:, 256:384],
                 TM=S5[:, 5 * TT:5 * TT + 128], TMA=S5[:, 5 * TT + 128:5 * TT + 256],
                 EMS0=S5[:, 5 * TT + 256:5 * TT + 384],
                 EMS1=S5[:, 5 * TT + 384:5 * TT + 512]),
        ]
        sigma_free = [None, None]
        tm_free = [None, None]
        et_toks = [None] * 16
        et2_toks = [None] * 16
        gram_toks = [None] * 16

        def head_params(h):
            hp = h // 2
            j = hp // 2
            base = (hp % 2) * 64
            grp = hp % 2
            slot = (hp // 2) * 2 + (h % 2)
            scol = 512 * grp + 64 * slot
            return hp, j, base, grp, slot, scol

        def emit_front(h):
            sg = h % 2
            fs = fsets[sg]
            hp, j, base, grp, slot, scol = head_params(h)
            ksl = A.KBF[j][base:base + 64, :]
            qsl = A.QBF[j][base:base + 64, :]

            @B.block.tensor
            def _(t, ksl=ksl, qsl=qsl, h=h):
                B.wait_bank(t, 0)
                B.wait_bank(t, 1)
                if h == 0:
                    B.wait(t, ex_tok)
                t.matmul(B.bank[0][:, 0:128], ksl, ksl, start=True,
                         stop=True).then_inc(*B.inc("pe"))
                t.matmul(B.bank[1][:, 0:128], ksl, qsl, start=True,
                         stop=True).then_inc(*B.inc("pe"))
            gram_toks[h] = B.tok("pe")

            @B.block.vector
            def _(v, h=h, fs=fs, ft=tm_free[sg]):
                B.wait(v, B.tok("dv"))
                if ft is not None:
                    B.wait(v, ft)
                i = v.tensor_scalar(fs["TM"], B.consts["c_onek_bf"][:],
                                    NGKM[:, h:h + 1], None, op0=ALU.mult)
                B.chain(v, "dv", i)
                v.tensor_scalar(fs["TMA"], B.consts["c_onek_bf"][:],
                                GA[:, h:h + 1], None,
                                op0=ALU.mult).then_inc(*B.inc("dv"))
            p_tok = B.tok("dv")

            @B.block.tensor
            def _(t, p_tok=p_tok, fs=fs):
                B.wait_bank(t, 2)
                B.wait(t, p_tok)
                t.transpose(B.bank[2][:, 0:128], fs["TM"], If32[:]) \
                    .then_inc(*B.inc("pe"))
                t.transpose(B.bank[2][:, 128:256], fs["TMA"], If32[:]) \
                    .then_inc(*B.inc("pe"))
            ptr_tok = B.tok("pe")

            @B.block.vector
            def _(v, ptr_tok=ptr_tok, fs=fs):
                B.wait(v, ptr_tok)
                B.wait(v, B.tok("dv"))
                i = v.tensor_tensor(fs["EMS0"], B.bank[2][:, 0:128],
                                    B.consts["c_pen_m"][:], ALU.add)
                B.chain(v, "dv", i)
                v.tensor_tensor(fs["EMS1"], B.bank[2][:, 128:256],
                                B.consts["c_pen_at"][:],
                                ALU.add).then_inc(*B.inc("dv"))
            B.guard_bank(2, B.tok("dv"))
            pm_tok = B.tok("dv")

            @B.block.scalar
            def _(s, pm_tok=pm_tok, h=h, fs=fs):
                B.wait(s, pm_tok)
                i = s.activation(fs["TM"], fs["EMS0"], AF.Exp,
                                 scale=1.0, bias=GMB[:, h:h + 1])
                B.chain(s, "ac", i)
                s.activation(fs["TMA"], fs["EMS1"], AF.Exp, scale=1.0,
                             bias=NGKM[:, h:h + 1]).then_inc(*B.inc("ac"))
            e_tok = B.tok("ac")

            @B.block.vector
            def _(v, e_tok=e_tok, fs=fs, h=h, sf=sigma_free[sg]):
                B.wait(v, e_tok)
                B.wait(v, gram_toks[h])
                B.wait(v, B.tok("dv"))
                if sf is not None:
                    B.wait(v, sf)
                i = v.tensor_tensor(fs["Mbf"], fs["TM"], B.bank[0][:, 0:128],
                                    ALU.mult)
                B.chain(v, "dv", i)
                v.tensor_tensor(fs["Atbf"], fs["TMA"], B.bank[1][:, 0:128],
                                ALU.mult).then_inc(*B.inc("dv"))
            B.guard_bank(0, B.tok("dv"))
            B.guard_bank(1, B.tok("dv"))
            tm_free[sg] = B.tok("dv")
            et_toks[h] = B.tok("dv")

        def emit_front_b(h):
            sg = h % 2
            fs = fsets[sg]

            @B.block.tensor
            def _(t, et_tok=et_toks[h], fs=fs):
                B.wait_bank(t, 7)
                B.wait(t, et_tok)
                t.transpose(B.bank_bf[:, 0:128], fs["Mbf"], Ibf[:]) \
                    .then_inc(*B.inc("pe"))
            mt_tok = B.tok("pe")

            @B.block.vector
            def _(v, mt_tok=mt_tok, fs=fs):
                B.wait(v, mt_tok)
                B.wait(v, B.tok("dv"))
                v.tensor_copy(fs["Mtbf"], B.bank_bf[:, 0:128]) \
                    .then_inc(*B.inc("dv"))
            B.guard_bank(7, B.tok("dv"))
            et2_toks[h] = B.tok("dv")

        def emit_back(h):
            nonlocal_state = {}
            sg = h % 2
            fs = fsets[sg]
            hp, j, base, grp, slot, scol = head_params(h)
            kfr = A.QKV[4 + j][base:base + 64, csl]
            qfr = A.QKV[j][base:base + 64, csl]
            if grp == 0:
                Ssl = A.S[0:64, scol:scol + 64]
            else:
                Ssl = A.Shi[64:128, 64 * slot:64 * slot + 64]

            zlist = [fs["Mtbf"]] + A.Zs[0:6]
            plist = [fs["Mbf"], A.Pb[0], A.Pb[1], A.Pb[0], A.Pb[1], A.Pb[0]]
            lev_tok = et2_toks[h]
            for jj in range(6):
                @B.block.tensor
                def _(t, jj=jj, lev_tok=lev_tok):
                    B.wait_bank(t, 3)
                    if jj < 5:
                        B.wait_bank(t, 4)
                    B.wait(t, lev_tok)
                    t.matmul(B.bank[3][:, 0:128], plist[jj][:], zlist[jj][:],
                             start=True, stop=True).then_inc(*B.inc("pe"))
                    if jj < 5:
                        t.matmul(B.bank[4][:, 0:128], zlist[jj][:], plist[jj][:],
                                 start=True, stop=True).then_inc(*B.inc("pe"))
                zt = B.tok("pe")

                @B.block.vector
                def _(v, jj=jj, zt=zt):
                    B.wait(v, zt)
                    B.wait(v, B.tok("dv"))
                    i = v.tensor_copy(zlist[jj + 1][:], B.bank[3][:, 0:128])
                    if jj < 5:
                        B.chain(v, "dv", i)
                        i = v.tensor_copy(plist[jj + 1][:], B.bank[4][:, 0:128])
                    i.then_inc(*B.inc("dv"))
                B.guard_bank(3, B.tok("dv"))
                if jj < 5:
                    B.guard_bank(4, B.tok("dv"))
                lev_tok = B.tok("dv")
            z_tok = B.tok("dv")

            @B.block.tensor
            def _(t, kfr=kfr, qfr=qfr, Ssl=Ssl, s_tok=s_tok):
                B.wait_bank(t, 3)
                B.wait_bank(t, 4)
                if s_tok is not None:
                    for st_ in s_tok:
                        B.wait(t, st_)
                t.matmul(B.bank[3][:, 0:64], kfr, Ssl, start=True,
                         stop=True).then_inc(*B.inc("pe"))
                t.matmul(B.bank[4][:, 0:64], qfr, Ssl, start=True,
                         stop=True).then_inc(*B.inc("pe"))
            ks_tok = B.tok("pe")

            @B.block.vector
            def _(v, h=h, ks_tok=ks_tok):
                B.wait(v, ks_tok)
                B.wait(v, B.tok("dv"))
                i = v.tensor_scalar(A.t64[:], B.bank[3][:, 0:64], EK[:, h:h + 1],
                                    None, op0=ALU.mult)
                B.chain(v, "dv", i)
                i = v.tensor_tensor(A.Rf[:], A.vT[:, 64 * h:64 * (h + 1)],
                                    A.t64[:], ALU.subtract)
                B.chain(v, "dv", i)
                v.tensor_scalar(A.Xap[0][:], A.Rf[:], BT[:, h:h + 1], None,
                                op0=ALU.mult).then_inc(*B.inc("dv"))
            B.guard_bank(3, B.tok("dv"))
            r_tok = B.tok("dv")

            xidx = 0
            for jj in range(6, -1, -1):
                @B.block.tensor
                def _(t, jj=jj, xidx=xidx, r_tok=r_tok):
                    B.wait_bank(t, 3)
                    if jj == 6:
                        B.wait(t, r_tok)
                        B.wait(t, z_tok)
                    t.matmul(B.bank[3][:, 0:64], zlist[jj][:], A.Xap[xidx][:],
                             start=True, stop=True).then_inc(*B.inc("pe"))
                ap_tok = B.tok("pe")

                @B.block.vector
                def _(v, jj=jj, xidx=xidx, ap_tok=ap_tok):
                    B.wait(v, ap_tok)
                    B.wait(v, B.tok("dv"))
                    op = ALU.subtract if jj == 0 else ALU.add
                    v.tensor_tensor(A.Xap[1 - xidx][:], A.Xap[xidx][:],
                                    B.bank[3][:, 0:64], op).then_inc(*B.inc("dv"))
                B.guard_bank(3, B.tok("dv"))
                xidx = 1 - xidx
            delta_tok = B.tok("dv")
            Xfin = A.Xap[xidx]

            @B.block.tensor
            def _(t, delta_tok=delta_tok, fs=fs):
                B.wait_bank(t, 3)
                B.wait(t, delta_tok)
                t.matmul(B.bank[3][:, 0:64], fs["Atbf"], Xfin[:],
                         start=True, stop=True).then_inc(*B.inc("pe"))
            o2_tok = B.tok("pe")
            sigma_free[sg] = o2_tok

            @B.block.vector
            def _(v, h=h, o2_tok=o2_tok, hp=hp):
                B.wait(v, o2_tok)
                B.wait(v, B.tok("dv"))
                i = v.tensor_scalar(A.t64[:], B.bank[4][:, 0:64], EO[:, h:h + 1],
                                    None, op0=ALU.mult)
                B.chain(v, "dv", i)
                i = v.tensor_tensor(A.OT[:, 64 * h:64 * (h + 1)], A.t64[:],
                                    B.bank[3][:, 0:64], ALU.add)
                B.chain(v, "dv", i)
                v.tensor_scalar(A.kp[:], A.kT[:, 64 * hp:64 * (hp + 1)],
                                EDEC[:, h:h + 1], None,
                                op0=ALU.mult).then_inc(*B.inc("dv"))
            B.guard_bank(3, B.tok("dv"))
            B.guard_bank(4, B.tok("dv"))
            kp_tok = B.tok("dv")

            sbank = 5 if scol < 512 else 6
            soff = scol % 512
            first_sb = sbank not in seen_sbank
            seen_sbank.add(sbank)

            @B.block.tensor
            def _(t, sbank=sbank, soff=soff, kp_tok=kp_tok, first_sb=first_sb):
                if first_sb:
                    B.wait_bank(t, sbank)
                B.wait(t, kp_tok)
                t.matmul(B.bank[sbank][0:64, soff:soff + 64],
                         A.kp[:], Xfin[:], start=True,
                         stop=True).then_inc(*B.inc("pe"))

        seen_sbank = set()
        emit_front(0)
        emit_front_b(0)
        for h in range(16):
            if h + 1 < 16:
                emit_front(h + 1)
            emit_back(h)
            if h + 1 < 16:
                emit_front_b(h + 1)
        head_done = B.tok("pe")

        @B.block.vector
        def _(v, head_done=head_done):
            B.wait(v, head_done)
            B.wait(v, B.tok("dv"))
            S3 = A.S[:].rearrange("p (h k) -> p h k", k=64)
            i = v.tensor_tensor(S3, A.S[:].bitcast(F32)
                                .rearrange("p (h k) -> p h k", k=64),
                                A.EGB[:].unsqueeze(2).broadcast_to([64, 16, 64]),
                                ALU.mult)
            B.chain(v, "dv", i)
            i = v.tensor_tensor(A.S[0:64, 0:512], A.S[0:64, 0:512].bitcast(F32),
                                B.bank[5][0:64, 0:512], ALU.add)
            B.chain(v, "dv", i)
            v.tensor_tensor(A.S[0:64, 512:1024], A.S[0:64, 512:1024].bitcast(F32),
                            B.bank[6][0:64, 0:512], ALU.add).then_inc(*B.inc("dv"))
        B.guard_bank(5, B.tok("dv"))
        B.guard_bank(6, B.tok("dv"))
        supd_tok = B.tok("dv")

        @B.block.gpsimd
        def _(g, supd_tok=supd_tok):
            B.wait(g, supd_tok)
            g.dma_start(out=A.Shi[64:128, 0:512],
                        in_=A.S[0:64, 512:1024]).then_inc(*B.inc(A.shld, 16))
        s_tok = [supd_tok, B.tok(A.shld)]

        @B.block.vector
        def _(v):
            B.wait(v, B.tok("dv"))
            csc = B.CSC[:, 0:512]
            i = v.tensor_tensor(csc, A.OT[:, 0:512], A.OT[:, 0:512], ALU.mult)
            B.chain(v, "dv", i)
            i = v.tensor_reduce(A.RSO[:, 0:8],
                                csc.rearrange("p (h k) -> p h k", k=64),
                                axis=AX.X, op=ALU.add)
            B.chain(v, "dv", i)
            i = v.tensor_tensor(csc, A.OT[:, 512:1024], A.OT[:, 512:1024], ALU.mult)
            B.chain(v, "dv", i)
            v.tensor_reduce(A.RSO[:, 8:16],
                            csc.rearrange("p (h k) -> p h k", k=64),
                            axis=AX.X, op=ALU.add).then_inc(*B.inc("dv"))
        rs_tok = B.tok("dv")

        @B.block.scalar
        def _(s, rs_tok=rs_tok):
            B.wait(s, rs_tok)
            i = s.activation(A.RSO[:], A.RSO[:], AF.Ln, scale=1.0 / 64.0, bias=eps2)
            B.chain(s, "ac", i)
            s.activation(A.RSO[:], A.RSO[:], AF.Exp, scale=-0.5).then_inc(*B.inc("ac"))
        rso_tok = B.tok("ac")

        @B.block.vector
        def _(v, rso_tok=rso_tok):
            B.wait(v, rso_tok)
            B.wait(v, B.tok("dv"))
            O3 = A.OT[:].rearrange("p (h k) -> p h k", k=64)
            v.tensor_tensor(O3, O3, A.RSO[:].unsqueeze(2).broadcast_to([128, 16, 64]),
                            ALU.mult).then_inc(*B.inc("dv"))
        yn_tok = B.tok("dv")

        for ch in range(8):
            @B.block.tensor
            def _(t, ch=ch, yn_tok=yn_tok):
                B.wait_bank(t, 2)
                B.wait(t, yn_tok)
                t.transpose(B.bank[2][:, 0:128], A.OT[:, 128 * ch:128 * (ch + 1)],
                            If32[:]).then_inc(*B.inc("pe"))
            ytr_tok = B.tok("pe")

            @B.block.vector
            def _(v, ch=ch, csl=csl, ytr_tok=ytr_tok):
                B.wait(v, ytr_tok)
                B.wait(v, B.tok("dv"))
                v.scalar_tensor_tensor(A.YNF[ch][:, csl], B.bank[2][:, 0:128],
                                       A.nwc[:, ch:ch + 1],
                                       A.SZ1[ch][:, csl], op0=ALU.mult,
                                       op1=ALU.mult).then_inc(*B.inc("dv"))
            B.guard_bank(2, B.tok("dv"))
        ynf_toks.append(B.tok("dv"))
    return ynf_toks[-1], s_tok


def emit_l1attn_full(B, A, ws, xn_ch, xn_toks, d, tile, s_tok_in):
    qkv_toks_pre = emit_l1attn(B, A, ws, xn_ch, xn_toks, None, d, tile)
    gg_tok = A._gg_tok
    bet_tok = A._bet_tok
    ynf_tok, s_tok = emit_l1attn_gdn(B, A, qkv_toks_pre, gg_tok, bet_tok, s_tok_in)

    y_toks = []
    for o in range(NCH):
        ws.load(d["W1oT"][o])

        @B.block.tensor
        def _(t, o=o, ynf_tok=ynf_tok):
            slab = ws.use(t)
            bk = o % 2
            B.wait_bank(t, bk)
            if o == 0:
                B.wait(t, ynf_tok)
            for ch in range(8):
                mm = t.matmul(B.bank[bk][:, 0:TT], slab[:, ch * 128:(ch + 1) * 128],
                              A.YNF[ch][:], start=(ch == 0), stop=(ch == 7),
                              skip_group_check=True)
            mm.then_inc(*B.inc("pe"))
            ws.done(B.tok("pe"))
        mm_tok = B.tok("pe")

        @B.block.scalar
        def _(s, o=o, mm_tok=mm_tok):
            B.wait(s, mm_tok)
            s.copy(A.PRE[o][:], B.bank[o % 2][:, 0:TT]).then_inc(*B.inc("ac"))
        B.guard_bank(o % 2, B.tok("ac"))
        y_toks.append(B.tok("ac"))
    return [p[:] for p in A.PRE], y_toks, s_tok
from contextlib import ExitStack
from concourse.bass_utils import run_bass_kernel_spmd

CN_ALL = ("c_ones_row", "c_ones_col", "c_eps", "c_ident", "c_ident_bfr",
          "c_tri_incl", "c_pen_m", "c_pen_at",
          "c_onescol_f32", "c_onesrow_f32", "c_onek_bf", "c_zeros_r",
          "c_l0exp")

NTILES = 4


def _bf(a):
    import ml_dtypes
    return np.asarray(a).astype(ml_dtypes.bfloat16)


def _weights_l0a(z):
    return {
        "WxT": _bf(tile_lhsT_by_o(np.ascontiguousarray(z["L0_attn_W_in_x"].T))),
        "WzT": _bf(tile_lhsT_by_o(np.ascontiguousarray(z["L0_attn_W_in_z"].T))),
        "WpT": _bf(tile_lhsT_by_o(np.ascontiguousarray(z["L0_attn_W_param"].T))),
        "WoT": _bf(tile_lhsT_by_o(np.ascontiguousarray(z["L0_attn_W_out"].T))),
        "l0_wdt": np.ascontiguousarray(z["L0_attn_W_dt"].T),
        "l0_cw": np.ascontiguousarray(
            z["L0_attn_conv_w"][:, 0, :].reshape(4, 128, 4).transpose(1, 0, 2)
            .reshape(128, 16)),
        "l0_cb": np.ascontiguousarray(z["L0_attn_conv_b"].reshape(4, 128).T),
        "l0_alog": np.ascontiguousarray(z["L0_attn_A_log"].reshape(512)
                                        .reshape(4, 128).T),
        "l0_dcol": np.ascontiguousarray(z["L0_attn_D"].reshape(4, 128).T),
        "l0_bdt": np.ascontiguousarray(z["L0_attn_b_dt"].reshape(32, 1)),
    }


W_L0A_DT = {"WxT": BF16, "WzT": BF16, "WpT": BF16, "WoT": BF16,
            "l0_wdt": F32R, "l0_cw": F32, "l0_cb": F32, "l0_alog": F32,
            "l0_dcol": F32, "l0_bdt": F32}
W_L0A_SHP = {"WxT": [4, 128, 2048], "WzT": [4, 128, 2048], "WpT": [9, 128, 512],
             "WoT": [16, 128, 512], "l0_wdt": [128, 32], "l0_cw": [128, 16],
             "l0_cb": [128, 4], "l0_alog": [128, 4], "l0_dcol": [128, 4],
             "l0_bdt": [32, 1]}


def _weights_l0f(z):
    return {
        "Wg": _bf(tile_lhsT_by_o(np.ascontiguousarray(z["L0_ffn_sh_gate"].T))),
        "Wu": _bf(tile_lhsT_by_o(np.ascontiguousarray(z["L0_ffn_sh_up"].T))),
        "Wd": _bf(tile_lhsT_by_k(np.ascontiguousarray(z["L0_ffn_sh_down"].T))),
        "Rg": _bf(tile_lhsT_by_o(np.ascontiguousarray(z["L0_ffn_r_gate"].T))),
        "Ru": _bf(tile_lhsT_by_o(np.ascontiguousarray(z["L0_ffn_r_up"].T))),
        "Rd": _bf(tile_lhsT_by_k(np.ascontiguousarray(z["L0_ffn_r_down"].T))),
        "l0f_wrt": _bf(np.ascontiguousarray(
            z["L0_ffn_router_weight"].T.reshape(16, 128, 8)
            .transpose(1, 0, 2).reshape(128, 128))),
    }


W_L0F_SHP = {"Wg": [64, 128, 2048], "Wu": [64, 128, 2048], "Wd": [64, 128, 2048],
             "Rg": [8, 128, 2048], "Ru": [8, 128, 2048], "Rd": [8, 128, 2048],
             "l0f_wrt": [128, 128]}
W_L0F_DT = {n: BF16 for n in W_L0F_SHP}


def _weights_l1a(z):
    return {
        "W1qkvT": _bf(tile_lhsT_by_o(np.ascontiguousarray(z["L1_attn_W_qkv"].T))),
        "W1zT": _bf(tile_lhsT_by_o(np.ascontiguousarray(z["L1_attn_W_z"].T))),
        "W1oT": _bf(tile_lhsT_by_o(np.ascontiguousarray(z["L1_attn_W_out"].T))),
        "a1d_wbt": _bf(np.ascontiguousarray(
            z["L1_attn_W_b"].T.reshape(16, 128, 16).transpose(1, 0, 2).reshape(128, 256))),
        "a1d_wat": _bf(np.ascontiguousarray(
            z["L1_attn_W_a"].T.reshape(16, 128, 16).transpose(1, 0, 2).reshape(128, 256))),
        "a1d_cw1": np.ascontiguousarray(
            z["L1_attn_conv_w"][:, 0, :].reshape(16, 128, 4).transpose(1, 0, 2)
            .reshape(128, 64)),
        "a1d_alog": np.ascontiguousarray(z["L1_attn_A_log"].reshape(16, 1)),
        "a1d_dtb": np.ascontiguousarray(z["L1_attn_dt_bias"].reshape(16, 1)),
        "a1d_nwc": np.ascontiguousarray(
            np.tile(z["L1_attn_norm_weight"], 16).reshape(8, 128).T),
    }


W_L1A_DT = {"W1qkvT": BF16, "W1zT": BF16, "W1oT": BF16, "a1d_wbt": BF16,
            "a1d_wat": BF16, "a1d_cw1": F32, "a1d_alog": F32, "a1d_dtb": F32,
            "a1d_nwc": F32}
W_L1A_SHP = {"W1qkvT": [16, 128, 2048], "W1zT": [8, 128, 2048],
             "W1oT": [16, 128, 1024], "a1d_wbt": [128, 256],
             "a1d_wat": [128, 256], "a1d_cw1": [128, 64], "a1d_alog": [16, 1],
             "a1d_dtb": [16, 1], "a1d_nwc": [128, 8]}


def _weights_l1f(z):
    W = {"WsGU": _bf(tile_lhsT_by_o(np.ascontiguousarray(z["L1_ffn_Ws_gate_up"]))),
         "WsD": _bf(tile_lhsT_by_k(np.ascontiguousarray(z["L1_ffn_Ws_down"])))}
    for e in range(8):
        W[f"WeGU{e}"] = _bf(tile_lhsT_by_o(np.ascontiguousarray(z["L1_ffn_We_gate_up"][e])))
        W[f"WeD{e}"] = _bf(tile_lhsT_by_k(np.ascontiguousarray(z["L1_ffn_We_down"][e])))
    W["l1f_wrt"] = _bf(np.ascontiguousarray(
        z["L1_ffn_W_router"].T.reshape(16, 128, 8).transpose(1, 0, 2).reshape(128, 128)))
    W["l1f_wsgt"] = _bf(np.ascontiguousarray(
        z["L1_ffn_W_sgate"].T.reshape(16, 128).T))
    return W


W_L1F_SHP = {"WsGU": [88, 128, 2048], "WsD": [44, 128, 2048],
             "l1f_wrt": [128, 128], "l1f_wsgt": [128, 16]}
for _e in range(8):
    W_L1F_SHP[f"WeGU{_e}"] = [22, 128, 2048]
    W_L1F_SHP[f"WeD{_e}"] = [11, 128, 2048]
W_L1F_DT = {n: BF16 for n in W_L1F_SHP}


def _std_frame(nc, wshapes, wdt, consts_np, cn):
    x_in = nc.dram_tensor("x_in", [NTILES, 128, NCH * TT], F32R,
                          kind="ExternalInput").ap()
    x_out = nc.dram_tensor("x_out", [NTILES, 128, NCH * TT], F32R,
                           kind="ExternalOutput").ap()
    d = {n: nc.dram_tensor(n, wshapes[n], wdt.get(n, F32R),
                           kind="ExternalInput").ap() for n in wshapes}
    dconsts = {n: nc.dram_tensor(n, list(consts_np[n].shape), CONST_DTYPES[n],
                                 kind="ExternalInput").ap() for n in cn}
    return x_in, x_out, d, dconsts


def _frame_body(stack, nc, cn, dconsts, consts_np):
    bb = B(nc, stack)
    staged = [n for n in cn if n not in ("c_zeros_r", "c_l0exp")]
    for name in staged:
        h = bb.sb("s_" + name, list(consts_np[name].shape), CONST_DTYPES[name])
        bb.consts[name] = h

    @bb.block.gpsimd
    def _(g):
        for name in staged:
            g.dma_start(out=bb.consts[name][:], in_=dconsts[name]) \
                .then_inc(*bb.inc("ld", 16))

    xt = bb.sb("xt", [P, NCH * TT], F32R)
    xnt = bb.sb("xnt", [P, NCH * TT], F32R)
    yt = bb.sb("yt", [P, NCH * TT], F32R)
    xnb = bb.sb("xnb", [P, NCH * TT], BF16)
    x_ch = [xt[:, k * TT:(k + 1) * TT] for k in range(NCH)]
    xn_ch = [xnt[:, k * TT:(k + 1) * TT] for k in range(NCH)]
    y_ch = [yt[:, k * TT:(k + 1) * TT] for k in range(NCH)]
    xnb_ch = [xnb[:, k * TT:(k + 1) * TT] for k in range(NCH)]
    L = LNBufs(bb)
    return bb, xt, x_ch, xn_ch, xnb_ch, y_ch, L


_CN_BASE = ("c_ones_row", "c_ones_col", "c_eps")


def _per_tile_io(bb, xt, tile, x_in, extra_tok=None):
    @bb.block.gpsimd
    def _(g, tile=tile):
        if tile > 0:
            bb.wait(g, bb.tok("st"))
        elif extra_tok is not None:
            bb.wait(g, extra_tok)
        g.dma_start(out=xt[:, 0:NCH * TT], in_=x_in[tile]) \
            .then_inc(*bb.inc("ld", 16))
    return bb.tok("ld")


def _store_tile(bb, xt, tile, x_out, res_toks):
    @bb.block.gpsimd
    def _(g, tile=tile):
        bb.wait(g, res_toks[-1])
        g.dma_start(out=x_out[tile], in_=xt[:, 0:NCH * TT]) \
            .then_inc(*bb.inc("st", 16))


def _finish(bb):
    @bb.block.gpsimd
    def _(g):
        g.wait_ge(bb.sem["st"], bb.cnt["st"])


def build_l0a(consts_np):
    cn = _CN_BASE + ("c_l0exp",)
    nc = bass.Bass("TRN2", target_bir_lowering=False)
    x_in, x_out, d, dconsts = _std_frame(nc, W_L0A_SHP, W_L0A_DT, consts_np, cn)
    with ExitStack() as stack:
        bb, xt, x_ch, xn_ch, xnb_ch, y_ch, L = _frame_body(stack, nc, cn, dconsts, consts_np)
        L0 = L0Bufs(bb, L)
        ws = WStream(bb)
        d = dict(d)
        d["c_l0exp_d"] = dconsts["c_l0exp"]
        ldt, dvt = emit_l0attn_consts(bb, L0, d)
        for engd in (bb.block.vector, bb.block.scalar, bb.block.tensor):
            @engd
            def _(e):
                bb.wait(e, ldt)
                bb.wait(e, dvt)
        for tile in range(NTILES):
            ld_tok = _per_tile_io(bb, xt, tile, x_in, ldt)
            xn_toks = emit_ln_xn(bb, L, x_ch, xnb_ch, scr_ch=xn_ch, pre_toks=[ld_tok] * NCH)
            a_toks = emit_l0attn(bb, L0, ws, xnb_ch, xn_toks, y_ch, d, tile)
            res_toks = emit_ln_residual(bb, L, y_ch, x_ch, pre_toks=a_toks)
            _store_tile(bb, xt, tile, x_out, res_toks)
        _finish(bb)
    return nc


def build_l0f(consts_np):
    cn = _CN_BASE + ("c_ident",)
    nc = bass.Bass("TRN2", target_bir_lowering=False)
    x_in, x_out, d, dconsts = _std_frame(nc, W_L0F_SHP, W_L0F_DT, consts_np, cn)
    with ExitStack() as stack:
        bb, xt, x_ch, xn_ch, xnb_ch, y_ch, L = _frame_body(stack, nc, cn, dconsts, consts_np)
        ws = WStream(bb)
        sgbufs = [bb.mf(6), bb.mf(7)]
        _asl = [0, 1, 2, 3, 8, 10, 11, 12]
        asup = [bb.mf(_asl[i // 2]).bitcast(BF16)[:, (i % 2) * TT:
                (i % 2 + 1) * TT] for i in range(16)]
        R = L0FRBufs(bb, L)
        emit_l0fr_consts(bb, R, d)
        const_tok = bb.tok("ld")
        for tile in range(NTILES):
            ld_tok = _per_tile_io(bb, xt, tile, x_in, const_tok)
            xn_toks = emit_ln_xn(bb, L, x_ch, xnb_ch, scr_ch=xn_ch, pre_toks=[ld_tok] * NCH)
            cb_tok = emit_l0ffn_router(bb, R, xnb_ch, xn_toks)
            emit_swiglu_mlp(bb, ws, xnb_ch, xn_toks, y_ch, d["Wg"], d["Wu"],
                            d["Wd"], 64, sgbufs, asup, accumulate=False)
            f_tok = emit_swiglu_mlp(bb, ws, xnb_ch, xn_toks, y_ch, d["Rg"],
                                    d["Ru"], d["Rd"], 8, sgbufs, asup,
                                    accumulate=True, scale_tile=R.CB0[:],
                                    scale_tok=cb_tok)
            res_toks = emit_ln_residual(bb, L, y_ch, x_ch, pre_toks=[f_tok] * NCH)
            _store_tile(bb, xt, tile, x_out, res_toks)
        _finish(bb)
    return nc


def build_l1a(consts_np):
    cn = _CN_BASE + ("c_ident", "c_ident_bfr", "c_tri_incl", "c_pen_m",
                     "c_pen_at", "c_onescol_f32", "c_onesrow_f32",
                     "c_onek_bf", "c_zeros_r")
    nc = bass.Bass("TRN2", target_bir_lowering=False)
    x_in, x_out, d, dconsts = _std_frame(nc, W_L1A_SHP, W_L1A_DT, consts_np, cn)
    with ExitStack() as stack:
        bb, xt, x_ch, xn_ch, xnb_ch, y_ch, L = _frame_body(stack, nc, cn, dconsts, consts_np)
        A = L1ABufs(bb, y_ch, xn_ch)
        ws = WStream(bb)
        d2 = dict(d)
        d2["c_zeros_r"] = dconsts["c_zeros_r"]
        ldt, dvt = emit_l1attn_consts(bb, A, d2)
        for engd in (bb.block.vector, bb.block.scalar, bb.block.tensor):
            @engd
            def _(e):
                bb.wait(e, ldt)
                bb.wait(e, dvt)
                bb.wait(e, A._sz_tok)
        s_tok = [dvt]
        for tile in range(NTILES):
            ld_tok = _per_tile_io(bb, xt, tile, x_in, ldt)
            xn_toks = emit_ln_xn(bb, L, x_ch, xnb_ch, scr_ch=xn_ch, pre_toks=[ld_tok] * NCH)
            y_list, y_toks, s_tok = emit_l1attn_full(bb, A, ws, xnb_ch, xn_toks,
                                                     d2, tile, s_tok)
            res_toks = emit_ln_residual(bb, L, y_list, x_ch, pre_toks=y_toks)
            _store_tile(bb, xt, tile, x_out, res_toks)
        _finish(bb)
    return nc


def build_l1f(consts_np):
    cn = _CN_BASE + ("c_ident",)
    nc = bass.Bass("TRN2", target_bir_lowering=False)
    x_in, x_out, d, dconsts = _std_frame(nc, W_L1F_SHP, W_L1F_DT, consts_np, cn)
    with ExitStack() as stack:
        bb, xt, x_ch, xn_ch, xnb_ch, y_ch, L = _frame_body(stack, nc, cn, dconsts, consts_np)
        F = L1FBufs(bb, L)
        ws = WStream(bb)
        sgbufs = [bb.mf(6), bb.mf(7)]
        _asl = [0, 1, 2, 3, 8, 10, 11, 12]
        asup = [bb.mf(_asl[i // 2]).bitcast(BF16)[:, (i % 2) * TT:
                (i % 2 + 1) * TT] for i in range(16)]
        ldt = emit_l1ffn_consts(bb, F, d)
        for engd in (bb.block.vector, bb.block.scalar, bb.block.tensor):
            @engd
            def _(e):
                bb.wait(e, ldt)
        for tile in range(NTILES):
            ld_tok = _per_tile_io(bb, xt, tile, x_in, ldt)
            xn_toks = emit_ln_xn(bb, L, x_ch, xnb_ch, scr_ch=xn_ch, pre_toks=[ld_tok] * NCH)
            f_tok = emit_l1ffn(bb, F, ws, xnb_ch, xn_toks, y_ch, d, sgbufs,
                               asup, tile)
            res_toks = emit_ln_residual(bb, L, y_ch, x_ch, pre_toks=[f_tok] * NCH)
            _store_tile(bb, xt, tile, x_out, res_toks)
        _finish(bb)
    return nc


def _to_tiles(xb):
    xf = np.ascontiguousarray(xb.T)
    return np.stack([
        xf[:, t * TT:(t + 1) * TT].reshape(NCH, 128, TT).transpose(1, 0, 2)
        .reshape(128, NCH * TT) for t in range(NTILES)])


def _from_tiles(out):
    return np.concatenate(
        [out[t].reshape(128, NCH, TT).transpose(1, 0, 2).reshape(2048, TT)
         for t in range(NTILES)], axis=1).T


def kernel_unfused(**inputs):
    z = inputs
    consts_np = host_consts()
    x = np.asarray(z["L0_attn_hidden_states"], dtype=np.float32)
    cores = list(range(8))
    xs = [_to_tiles(x[b]) for b in range(8)]

    launches = [
        (build_l0a, _weights_l0a, _CN_BASE + ("c_l0exp",)),
        (build_l0f, _weights_l0f, _CN_BASE),
        (build_l1a, _weights_l1a, _CN_BASE + ("c_ident", "c_ident_bfr",
                                              "c_tri_incl", "c_pen_m",
                                              "c_pen_at",
                                              "c_onescol_f32", "c_onesrow_f32",
                                              "c_onek_bf", "c_zeros_r")),
        (build_l1f, _weights_l1f, _CN_BASE + ("c_ident",)),
    ]
    for build_fn, wfn, cn in launches:
        W = wfn(z)
        nc = build_fn(consts_np)
        base = dict(W)
        for n in cn:
            base[n] = consts_np[n]
        in_maps = [dict(base, x_in=xs[b]) for b in range(8)]
        res = run_bass_kernel_spmd(nc, in_maps, core_ids=cores)
        xs = [np.asarray(res.results[b]["x_out"]) for b in range(8)]

    out = np.stack([_from_tiles(xs[b]) for b in range(8)]).astype(np.float32)
    return out



def build_fused(consts_np):
    cn = ("c_ones_row", "c_ones_col", "c_eps", "c_ident", "c_ident_bfr",
          "c_tri_incl", "c_pen_m", "c_pen_at", "c_onescol_f32",
          "c_onesrow_f32", "c_onek_bf", "c_zeros_r", "c_l0exp")
    nc = bass.Bass("TRN2", target_bir_lowering=False)
    wshapes = {}
    wshapes.update(W_L0A_SHP)
    wshapes.update(W_L0F_SHP)
    wshapes.update(W_L1A_SHP)
    wshapes.update(W_L1F_SHP)
    wdt = {}
    wdt.update(W_L0A_DT)
    wdt.update(W_L1A_DT)
    wdt.update(W_L0F_DT)
    wdt.update(W_L1F_DT)
    x_in, x_out, d, dconsts = _std_frame(nc, wshapes, wdt, consts_np, cn)
    with ExitStack() as stack:
        bb, xt, x_ch, xn_ch, xnb_ch, y_ch, L = _frame_body(stack, nc, cn, dconsts, consts_np)
        L0 = L0Bufs(bb, L)
        A = L1ABufs(bb, y_ch, xn_ch)
        F = L1FBufs(bb, L)
        R = L0FRBufs(bb, L)
        ws = WStream(bb)
        sgbufs = [bb.mf(6), bb.mf(7)]
        _asl = [0, 1, 2, 3, 8, 10, 11, 12]
        asup = [bb.mf(_asl[i // 2]).bitcast(BF16)[:, (i % 2) * TT:
                (i % 2 + 1) * TT] for i in range(16)]
        d2 = dict(d)
        d2["c_zeros_r"] = dconsts["c_zeros_r"]
        d2["c_l0exp_d"] = dconsts["c_l0exp"]
        emit_l0fr_consts(bb, R, d2)
        ldt0, dvt0 = emit_l0attn_consts(bb, L0, d2)
        ldt1, dvt1 = emit_l1attn_consts(bb, A, d2)
        ldt2 = emit_l1ffn_consts(bb, F, d2)
        for engd in (bb.block.vector, bb.block.scalar, bb.block.tensor):
            @engd
            def _(e):
                bb.wait(e, ldt2)
                bb.wait(e, ldt0)
                bb.wait(e, ldt1)
                bb.wait(e, dvt0)
                bb.wait(e, dvt1)
                bb.wait(e, A._sz_tok)
        s_tok = [dvt1]
        for tile in range(NTILES):
            ld_tok = _per_tile_io(bb, xt, tile, x_in, ldt2)
            xn_toks = emit_ln_xn(bb, L, x_ch, xnb_ch, scr_ch=xn_ch, pre_toks=[ld_tok] * NCH)
            a_toks = emit_l0attn(bb, L0, ws, xnb_ch, xn_toks, y_ch, d2, tile)
            res_toks = emit_ln_residual(bb, L, y_ch, x_ch, pre_toks=a_toks)
            xn_toks = emit_ln_xn(bb, L, x_ch, xnb_ch, scr_ch=xn_ch, pre_toks=res_toks)
            cb_tok = emit_l0ffn_router(bb, R, xnb_ch, xn_toks)
            emit_swiglu_mlp(bb, ws, xnb_ch, xn_toks, y_ch, d2["Wg"], d2["Wu"],
                            d2["Wd"], 64, sgbufs, asup, accumulate=False)
            f_tok = emit_swiglu_mlp(bb, ws, xnb_ch, xn_toks, y_ch, d2["Rg"],
                                    d2["Ru"], d2["Rd"], 8, sgbufs, asup,
                                    accumulate=True, scale_tile=R.CB0[:],
                                    scale_tok=cb_tok)
            res_toks = emit_ln_residual(bb, L, y_ch, x_ch, pre_toks=[f_tok] * NCH)
            xn_toks = emit_ln_xn(bb, L, x_ch, xnb_ch, scr_ch=xn_ch, pre_toks=res_toks)
            y_list, y_toks, s_tok = emit_l1attn_full(bb, A, ws, xnb_ch, xn_toks,
                                                     d2, tile, s_tok)
            res_toks = emit_ln_residual(bb, L, y_list, x_ch, pre_toks=y_toks)
            xn_toks = emit_ln_xn(bb, L, x_ch, xnb_ch, scr_ch=xn_ch, pre_toks=res_toks)
            f_tok = emit_l1ffn(bb, F, ws, xnb_ch, xn_toks, y_ch, d2, sgbufs,
                               asup, tile)
            res_toks = emit_ln_residual(bb, L, y_ch, x_ch, pre_toks=[f_tok] * NCH)
            _store_tile(bb, xt, tile, x_out, res_toks)
        _finish(bb)
    return nc


def kernel(**inputs):
    z = inputs
    consts_np = host_consts()
    x = np.asarray(z["L0_attn_hidden_states"], dtype=np.float32)
    xs = [_to_tiles(x[b]) for b in range(8)]
    base = {}
    for wfn in (_weights_l0a, _weights_l0f, _weights_l1a, _weights_l1f):
        base.update(wfn(z))
    base.update({n: consts_np[n] for n in
                 ("c_ones_row", "c_ones_col", "c_eps", "c_ident", "c_ident_bfr",
                  "c_tri_incl", "c_pen_m", "c_pen_at",
                  "c_onescol_f32", "c_onesrow_f32", "c_onek_bf", "c_zeros_r",
                  "c_l0exp")})
    nc = build_fused(consts_np)
    in_maps = [dict(base, x_in=xs[b]) for b in range(8)]
    res = run_bass_kernel_spmd(nc, in_maps, core_ids=list(range(8)))
    out = np.stack([_from_tiles(np.asarray(res.results[b]["x_out"]))
                    for b in range(8)]).astype(np.float32)
    return out
